# version 15
# baseline (speedup 1.0000x reference)
import numpy as np
import ml_dtypes
import concourse.bass as bass
import concourse.mybir as mybir
from concourse.bass import IndirectOffsetOnAxis
from concourse.bass_utils import run_bass_kernel_spmd
from contextlib import ExitStack

F32 = mybir.dt.float32
BF16 = mybir.dt.bfloat16
I32 = mybir.dt.int32
AF = mybir.ActivationFunctionType
ALU = mybir.AluOpType
AX = mybir.AxisListType

NCORES = 8
D = 2048
DC = D // 128
HID = 2560
HC = HID // 128
W = 1024
WH = W + 3
C_RGLRU = 8.0
GELU_C0 = 0.7978845608028654
GELU_C1 = GELU_C0 * 0.044715
STAGES = ("attn_a", "ffn1", "attn_b", "ffn2")
GROUPS4 = [[0, 1, 2, 3], [4, 5, 6, 7]]

SLOTQ = {0: "sync", 1: "sync", 2: "sync", 3: "sync", 4: "scalar",
         5: "gpsimd", 6: "gpsimd", 7: "sync",
         8: "sync", 9: "sync", 10: "sync", 11: "sync", 12: "scalar",
         13: "gpsimd", 14: "gpsimd", 15: "sync"}


class Buf:
    __slots__ = ("name", "ready", "readers")

    def __init__(self, name="buf"):
        self.name = name
        self.ready = None
        self.readers = []


class K:
    ENGINES = ("tensor", "vector", "scalar", "gpsimd", "sync")

    def __init__(self, nc, n_dma_sems=16):
        self.nc = nc
        self.sems = {}
        self.counts = {}
        self.waited = {}
        self.queues = {e: [] for e in self.ENGINES}
        self._sem_ctxs = []
        for e in self.ENGINES:
            self._alloc_sem("s_" + e, 1)
        for i in range(n_dma_sems):
            self._alloc_sem(f"s_dma{i}", 16)
        self._alloc_sem("s_cc", 1)
        self.eng_sem = {e: "s_" + e for e in self.ENGINES}
        self.pending = {}

    def _alloc_sem(self, name, step):
        ctx = self.nc.semaphore(name)
        h = ctx.__enter__()
        self._sem_ctxs.append(ctx)
        self.sems[name] = (h, step)
        self.counts[name] = 0

    def _emit_wait(self, engine, sem_name, count):
        if count <= 0:
            return
        if engine == "tensor" and sem_name == "s_tensor":
            return
        key = (engine, sem_name)
        if self.waited.get(key, 0) >= count:
            return
        self.waited[key] = count
        h, _ = self.sems[sem_name]
        self.queues[engine].append(("wait", h, count))

    def _deps(self, engine, reads, writes):
        for b in reads or ():
            if b.ready:
                self._emit_wait(engine, *b.ready)
        for b in writes or ():
            if b.ready:
                self._emit_wait(engine, *b.ready)
            for r in b.readers:
                self._emit_wait(engine, *r)

    def _record(self, sem_name, count, reads, writes):
        for b in reads or ():
            b.readers.append((sem_name, count))
        for b in writes or ():
            b.ready = (sem_name, count)
            b.readers = []

    def op(self, engine, fn, reads=(), writes=(), sem=None, inc=True):
        sem_name = sem or self.eng_sem[engine]
        self._deps(engine, reads, writes)
        h, step = self.sems[sem_name]
        if inc:
            self.counts[sem_name] += step
            cnt = self.counts[sem_name]
            self.queues[engine].append(("op", fn, h, step))
            pend = self.pending.pop(sem_name, [])
            self._record(sem_name, cnt, list(reads) + pend, writes)
        else:
            self.pending.setdefault(sem_name, []).extend(reads)
            self.queues[engine].append(("op", fn, None, 0))

    def dma(self, slot, fn, reads=(), writes=()):
        engine = SLOTQ[slot]
        sem_name = f"s_dma{slot}"
        self._emit_wait(engine, sem_name, self.counts[sem_name])
        self.op(engine, fn, reads=reads, writes=writes, sem=sem_name)

    def cc(self, fn, reads=(), writes=()):
        self.op("gpsimd", fn, reads=reads, writes=writes, sem="s_cc")

    def finish(self, engine="sync"):
        for name in self.sems:
            c = self.counts[name]
            if c > 0:
                self._emit_wait(engine, name, c)

    def build(self, block):
        for e in self.ENGINES:
            q = self.queues[e]
            if not q:
                continue
            section = getattr(block, e)

            def make(q=q):
                def _(eng):
                    for item in q:
                        if item[0] == "wait":
                            _, h, count = item
                            eng.wait_ge(h, count)
                        else:
                            _, fn, h, step = item
                            inst = fn(eng)
                            if h is not None:
                                inst.then_inc(h, step)
                return _

            section(make())


def build_program():
    nc = bass.Bass("TRN2", target_bir_lowering=False, debug=False,
                   dynamic_dma_scratch_size=20480)

    def dram_in(name, shape, dt=F32):
        return nc.dram_tensor(name, shape, dt, kind="ExternalInput").ap()

    x0p = dram_in("x0", [D, WH])
    wlp = dram_in("wl", [HC, 128, DC * 128])
    wrp = dram_in("wr", [HC, 128, DC * 128])
    wap = dram_in("wa", [HC, 128, HC * 128])
    wip = dram_in("wi", [HC, 128, HC * 128])
    wop = dram_in("wo", [DC, 128, HC * 128])
    bl_i = dram_in("bl", [128, HC]); br_i = dram_in("br", [128, HC])
    ba_i = dram_in("ba", [128, HC]); bi_i = dram_in("bi", [128, HC])
    bo_i = dram_in("bo", [128, DC]); cb_i = dram_in("cb", [128, HC])
    cw_i = dram_in("cw", [128, HC * 4]); lam_i = dram_in("lam", [128, HC])
    hv_i = dram_in("hv", [1, 1]); qsel_i = dram_in("qsel", [1, 4])
    idf_i = dram_in("idf", [128, 128]); idb_i = dram_in("idb", [128, 128], BF16)
    wrt_i = dram_in("wrt", [128, DC]); g1_i = dram_in("g1", [128, DC])
    g2_i = dram_in("g2", [128, DC]); blk_i = dram_in("blkid", [128, 4])
    qsel4_i = dram_in("qsel4", [4, 1])
    fwq = dram_in("fwq", [4, 128, DC * 128]); fwk = dram_in("fwk", [4, 128, DC * 128])
    fwv = dram_in("fwv", [2, 4, 128, 1024]); fwo = dram_in("fwo", [DC, 128, 512])
    fw1 = dram_in("fw1", [16, 128, DC * 128]); fw3 = dram_in("fw3", [16, 128, DC * 128])
    fw2 = dram_in("fw2", [4, 8, 128, 1024])
    fmask = dram_in("fmask", [1024, 1024])
    bwq = dram_in("bwq", [16, 128, DC * 128]); bwk = dram_in("bwk", [4, 128, DC * 128])
    bwv = dram_in("bwv", [8, 128, 1024]); bwo = dram_in("bwo", [DC, 128, DC * 128])
    bmask_i = dram_in("bmask", [1024, 1152], BF16)
    kvidx_i = dram_in("kvidx", [128, 16], I32)
    fcos_i = dram_in("fcos", [64, 1024]); fsin_i = dram_in("fsin", [64, 1024])

    out_t = nc.dram_tensor("out", [W, D], F32, kind="ExternalOutput").ap()

    relay_in = nc.dram_tensor("relay_in", [1, 2 * HID], F32).ap()
    relay_out = nc.dram_tensor("relay_out", [4, 2 * HID], F32).ap()
    lg_in = nc.dram_tensor("lg_in", [1, 1024], F32).ap()
    lg_out = nc.dram_tensor("lg_out", [4, 1024], F32).ap()
    idx_dram = nc.dram_tensor("idx_dram", [1, 1024], I32).ap()
    rw_dram = nc.dram_tensor("rw_dram", [1, 1024], F32).ap()
    cpt_in = nc.dram_tensor("cpt_in", [1024, 2048], BF16).ap()
    cpt_out = nc.dram_tensor("cpt_out", [1024, 2048], BF16).ap()
    at_in = nc.dram_tensor("at_in", [2048, 1024], BF16).ap()
    at_out = nc.dram_tensor("at_out", [2048, 1024], BF16).ap()
    fo_in = nc.dram_tensor("fo_in", [1024, 2048], BF16).ap()
    fo_out = nc.dram_tensor("fo_out", [1024, 2048], BF16).ap()
    xo_dram = nc.dram_tensor("xo_dram", [1024, 2048], BF16).ap()
    xst_dram = nc.dram_tensor("xst_dram", [128, DC * W], F32).ap()
    lcp_dram = nc.dram_tensor("lcp_dram", [HC * 128, W], BF16).ap()
    kvag_in = nc.dram_tensor("kvag_in", [1024, 1024], BF16).ap()
    kvag_out = nc.dram_tensor("kvag_out", [4096, 1024], BF16).ap()

    live = [0]; peak = [0]

    def mk_sb(stack, name, shape, dt=F32):
        t = stack.enter_context(nc.sbuf_tensor(name, shape, dt))
        bpe = {F32: 4, BF16: 2, I32: 4}[dt]
        live[0] += shape[1] * bpe
        peak[0] = max(peak[0], live[0])
        return t

    class Grp:
        def __init__(self, side="left"):
            self.stack = ExitStack()
            self.bytes = 0
            self.side = side

        def sb(self, name, shape, dt=F32):
            before = live[0]
            t = self.stack.enter_context(nc.sbuf_tensor(name, shape, dt, side=self.side))
            bpe = {F32: 4, BF16: 2, I32: 4}[dt]
            live[0] += shape[1] * bpe
            peak[0] = max(peak[0], live[0])
            self.bytes += live[0] - before
            return t

        def free(self):
            self.stack.close()
            live[0] -= self.bytes

    g_persist = Grp()
    sb = g_persist.sb

    identf = sb("identf", [128, 128])
    identb = sb("identb", [128, 128], BF16)
    ones_b = sb("ones_b", [128, 1], BF16)
    ones_row = sb("ones_row", [1, 128])
    zeros_bw = sb("zeros_bw", [128, W], BF16)
    bl_t = sb("bl_t", [128, HC]); br_t = sb("br_t", [128, HC])
    ba_t = sb("ba_t", [128, HC]); bi_t = sb("bi_t", [128, HC])
    bo_t = sb("bo_t", [128, DC]); cb_t = sb("cb_t", [128, HC])
    cw_t = sb("cw_t", [128, HC * 4]); lam_t = sb("lam_t", [128, HC])
    slam_t = sb("slam_t", [128, HC]); slam2_t = sb("slam2_t", [128, HC])
    hv_b = sb("hv_b", [128, 1])
    qsel_b = sb("qsel_b", [128, 4]); omq_b = sb("omq_b", [128, 4])
    qsel_s = sb("qsel_s", [1, 4]); tiny = sb("tiny", [128, 32])
    fsc = sb("fsc", [128, 8])
    cst = sb("cst", [128, 4])
    atot_t = sb("atot_t", [128, HC]); hlast_t = sb("hlast_t", [128, HC])
    rel = sb("rel", [128, 8 * HC]); Hst = sb("Hst", [128, HC])
    e0 = sb("e0", [128, W]); e1 = sb("e1", [128, W])
    g_attA = Grp()
    e2b = g_attA.sb("e2b", [128, W])
    ubuf0 = g_attA.sb("ubuf0", [128, W]); rin0 = g_attA.sb("rin0", [128, WH])
    cacc0 = g_attA.sb("cacc0", [128, W])
    wrawA = g_attA.sb("wrawA", [128, HC * 128])
    wbfA0 = g_attA.sb("wbfA0", [128, HC * 128], BF16)
    wbfA1 = g_attA.sb("wbfA1", [128, HC * 128], BF16)

    PS12 = g_persist.stack.enter_context(nc.psum_tensor("PS12", [128, 2048], F32))
    PS3 = g_persist.stack.enter_context(nc.psum_tensor("PS3", [128, 1536], F32))
    PSB = g_persist.stack.enter_context(nc.psum_tensor("PSB", [128, 1024], BF16))

    k = K(nc)
    B = {}

    def buf(name):
        if name not in B:
            B[name] = Buf(name)
        return B[name]

    P12, P3, PB = buf("PS12"), buf("PS3"), buf("PSB")
    FSC = buf("fsc")
    n_fence = [0]

    def fence(dead_bufs):
        ep = Buf(f"fence{n_fence[0]}")
        col = n_fence[0] % 8
        n_fence[0] += 1
        k.op("vector", lambda e, col=col: e.tensor_copy(
                out=fsc[:, col:col + 1], in_=fsc[:, col:col + 1]),
             reads=[FSC], writes=list(dead_bufs) + [ep, FSC])
        return ep

    for nm, tt_, ap_ in [("bl_t", bl_t, bl_i), ("br_t", br_t, br_i), ("ba_t", ba_t, ba_i),
                         ("bi_t", bi_t, bi_i), ("bo_t", bo_t, bo_i), ("cb_t", cb_t, cb_i),
                         ("cw_t", cw_t, cw_i), ("lam_t", lam_t, lam_i),
                         ("identf", identf, idf_i), ("identb", identb, idb_i),
                         ("qsel_s", qsel_s, qsel_i)]:
        k.dma(7, lambda e, t=tt_, a=ap_: e.dma_start(out=t[:, :], in_=a), writes=[buf(nm)])
    k.dma(7, lambda e: e.dma_start(out=tiny[0:1, 0:1], in_=hv_i), writes=[buf("hv_raw")])
    cONES_B, cONES_R, cZER = buf("cob"), buf("cor"), buf("czw")
    k.op("vector", lambda e: e.memset(ones_b[:, :], 1.0), writes=[cONES_B])
    k.op("vector", lambda e: e.memset(ones_row[:, :], 1.0), writes=[cONES_R])
    k.op("vector", lambda e: e.memset(zeros_bw[:, :], 0.0), writes=[cZER])
    k.op("vector", lambda e: e.memset(fsc[:, :], 0.0), writes=[FSC])
    k.op("vector", lambda e: e.memset(cst[:, 0:1], 1e-5), writes=[buf("cst")])
    k.op("vector", lambda e: e.memset(cst[:, 1:2], 1e-6), writes=[buf("cst")])
    k.op("vector", lambda e: e.memset(cst[:, 2:3], 1.0), writes=[buf("cst")])
    k.op("tensor", lambda e: e.matmul(PS3[:, 0:1], ones_row[:, :], tiny[0:1, 0:1],
                                      start=True, stop=True),
         reads=[cONES_R, buf("hv_raw")], writes=[P3])
    k.op("vector", lambda e: e.tensor_copy(out=hv_b[:, :], in_=PS3[:, 0:1]),
         reads=[P3], writes=[buf("hv_b")])
    k.op("tensor", lambda e: e.matmul(PS3[:, 4:8], ones_row[:, :], qsel_s[:, :],
                                      start=True, stop=True),
         reads=[cONES_R, buf("qsel_s")], writes=[P3])
    k.op("vector", lambda e: e.tensor_copy(out=qsel_b[:, :], in_=PS3[:, 4:8]),
         reads=[P3], writes=[buf("qsel_b")])
    k.op("vector", lambda e: e.tensor_scalar(out=omq_b[:, :], in0=qsel_b[:, :], scalar1=-1.0,
                                             scalar2=1.0, op0=ALU.mult, op1=ALU.add),
         reads=[buf("qsel_b")], writes=[buf("omq_b")])
    k.op("vector", lambda e: e.tensor_scalar_mul(slam_t[:, :], lam_t[:, :], -1.0),
         reads=[buf("lam_t")], writes=[buf("slam")])
    k.op("scalar", lambda e: e.activation(out=tiny[:, 1:1 + HC], in_=slam_t[:, :], func=AF.Abs),
         reads=[buf("slam")], writes=[buf("t_az")])
    k.op("scalar", lambda e: e.activation(out=tiny[:, 1:1 + HC], in_=tiny[:, 1:1 + HC],
                                          func=AF.Exp, scale=-1.0),
         reads=[buf("t_az")], writes=[buf("t_az")])
    k.op("scalar", lambda e: e.activation(out=tiny[:, 1:1 + HC], in_=tiny[:, 1:1 + HC],
                                          func=AF.Ln, bias=cst[:, 2:3]),
         reads=[buf("t_az"), buf("cst")], writes=[buf("t_az")])
    k.op("vector", lambda e: e.tensor_scalar_max(slam_t[:, :], slam_t[:, :], 0.0),
         reads=[buf("slam")], writes=[buf("slam")])
    k.op("vector", lambda e: e.tensor_add(slam_t[:, :], slam_t[:, :], tiny[:, 1:1 + HC]),
         reads=[buf("slam"), buf("t_az")], writes=[buf("slam")])
    k.op("vector", lambda e: e.tensor_scalar_mul(slam_t[:, :], slam_t[:, :], -C_RGLRU),
         reads=[buf("slam")], writes=[buf("slam")])
    k.op("vector", lambda e: e.tensor_scalar_mul(slam2_t[:, :], slam_t[:, :], 2.0),
         reads=[buf("slam")], writes=[buf("slam2")])


    WS_SLOTS = [0, 1, 2, 3, 8, 9, 10, 11, 7, 15]

    def stream_weight(stream, src_ap, m, kc_n, wraw, wbf, cast_eng="scalar", extra=(),
                      soff=0):
        ncols = kc_n * 128
        half = (ncols // 256) * 128
        def gspan(pfx, tid, o0, o1):
            return [buf(f"{pfx}_{tid}_{g}") for g in range(o0 // 128, o1 // 128)]
        allbf = gspan("bf", id(wbf), 0, ncols)
        for h in range(2):
            o0, o1 = h * half, ncols if h else half
            slot = WS_SLOTS[(soff * 4 + h * 2 + (m % 2)) % 10]
            rb = gspan("raw", id(wraw), o0, o1)
            bb = gspan("bf", id(wbf), o0, o1)
            k.dma(slot, lambda e, o0=o0, o1=o1: e.dma_start(
                    out=wraw[:, o0:o1], in_=src_ap[m, :, o0:o1]),
                  reads=list(extra), writes=rb)
            ce = ("scalar", "vector") if cast_eng == "scalar" else ("gpsimd", "vector")
            ce = ce[h]
            if ce == "scalar":
                k.op("scalar", lambda e, o0=o0, o1=o1: e.activation(
                        out=wbf[:, o0:o1], in_=wraw[:, o0:o1], func=AF.Copy),
                     reads=rb, writes=bb)
            else:
                k.op(ce, lambda e, o0=o0, o1=o1: e.tensor_copy(
                        out=wbf[:, o0:o1], in_=wraw[:, o0:o1]),
                     reads=rb, writes=bb)
        return allbf

    def nslices(nw):
        out, o = [], 0
        while o < nw:
            cur = min(512, nw - o)
            out.append((o, cur))
            o += cur
        return out

    def mm_acc(ps, psb, wbf, wbufs, kc_n, rhs_fn, nw, rhs_bufs):
        sl = nslices(nw)
        for kc in range(kc_n):
            for si, (o, cur) in enumerate(sl):
                last = (kc == kc_n - 1) and (si == len(sl) - 1)
                first = kc == 0 and si == 0
                k.op("tensor",
                     lambda e, kc=kc, o=o, cur=cur: e.matmul(
                         ps[:, o:o + cur], wbf[:, kc * 128:(kc + 1) * 128],
                         rhs_fn(kc, o, cur), start=(kc == 0), stop=(kc == kc_n - 1)),
                     reads=list(wbufs) + list(rhs_bufs),
                     writes=[psb] if (last or first) else (), inc=last)

    g_xp = Grp("right")
    xpb = g_xp.sb("xpb", [128, DC * WH], BF16)
    g_a1 = Grp("right")
    x0st0 = g_a1.sb("x0st0", [128, WH])
    x0st1 = g_a1.sb("x0st1", [128, WH])
    sqa = g_a1.sb("sqa", [128, WH], BF16)
    rstd_bc = g_a1.sb("rstd_bc", [128, WH])
    nmrs_bc = g_a1.sb("nmrs_bc", [128, WH])
    stat = g_a1.sb("stat", [1, 3 * WH])

    nsl_wh = [(0, 512), (512, 512), (1024, 3)]
    s0 = stat[0:1, 0:WH]; s1 = stat[0:1, WH:2 * WH]; s2 = stat[0:1, 2 * WH:3 * WH]
    for dc in range(DC):
        x0c_ = [x0st0, x0st1][dc % 2]
        k.dma((dc % 4) * 2, lambda e, dc=dc, x0c_=x0c_: e.dma_start(
                out=x0c_[:, :], in_=x0p[dc * 128:(dc + 1) * 128, :]),
              writes=[buf(f"x0s_{dc % 2}")])
        k.op("vector", lambda e, dc=dc, x0c_=x0c_: e.tensor_copy(
                out=xpb[:, dc * WH:(dc + 1) * WH], in_=x0c_[:, :]),
             reads=[buf(f"x0s_{dc % 2}")], writes=[buf(f"xpb_{dc}")])
        k.op("scalar", lambda e, dc=dc: e.activation(
                out=sqa[:, :], in_=xpb[:, dc * WH:(dc + 1) * WH], func=AF.Square),
             reads=[buf(f"xpb_{dc}")], writes=[buf("sq_0")])
        for si, (o, cur) in enumerate(nsl_wh):
            last = dc == DC - 1 and si == 2
            k.op("tensor", lambda e, dc=dc, o=o, cur=cur: e.matmul(
                    PS12[0:1, o:o + cur], ones_b[:, :],
                    xpb[:, dc * WH + o:dc * WH + o + cur],
                    start=(dc == 0), stop=(dc == DC - 1)),
                 reads=[buf(f"xpb_{dc}"), cONES_B], writes=[P12] if last else ())
            k.op("tensor", lambda e, o=o, cur=cur, dc=dc: e.matmul(
                    PS3[0:1, o:o + cur], ones_b[:, :], sqa[:, o:o + cur],
                    start=(dc == 0), stop=(dc == DC - 1)),
                 reads=[buf("sq_0"), cONES_B], writes=[P3] if last else ())
    k.op("vector", lambda e: e.tensor_scalar_mul(s0, PS12[0:1, 0:WH], 1.0 / D),
         reads=[P12], writes=[buf("st0")])
    k.op("vector", lambda e: e.tensor_scalar_mul(s1, PS3[0:1, 0:WH], 1.0 / D),
         reads=[P3], writes=[buf("st1")])
    k.op("vector", lambda e: e.tensor_mul(s2, s0, s0),
         reads=[buf("st0")], writes=[buf("st2")])
    k.op("vector", lambda e: e.tensor_sub(s1, s1, s2),
         reads=[buf("st1"), buf("st2")], writes=[buf("st1")])
    k.op("scalar", lambda e: e.activation(out=s1, in_=s1, func=AF.Sqrt, bias=cst[0:1, 0:1]),
         reads=[buf("st1"), buf("cst")], writes=[buf("st1")])
    k.op("vector", lambda e: e.reciprocal(s1, s1),
         reads=[buf("st1")], writes=[buf("st1")])
    k.op("vector", lambda e: e.tensor_mul(s0, s0, s1),
         reads=[buf("st0"), buf("st1")], writes=[buf("st0")])
    k.op("vector", lambda e: e.tensor_scalar_mul(s0, s0, -1.0),
         reads=[buf("st0")], writes=[buf("st0")])
    for si, (o, cur) in enumerate(nsl_wh):
        last = si == 2
        k.op("tensor", lambda e, o=o, cur=cur: e.matmul(
                PS12[:, o:o + cur], ones_row[:, :], s1[0:1, o:o + cur],
                start=True, stop=True),
             reads=[cONES_R, buf("st1")], writes=[P12] if last else (), inc=last)
        k.op("tensor", lambda e, o=o, cur=cur: e.matmul(
                PS3[:, o:o + cur], ones_row[:, :], s0[0:1, o:o + cur],
                start=True, stop=True),
             reads=[cONES_R, buf("st0")], writes=[P3] if last else (), inc=last)
    k.op("vector", lambda e: e.tensor_copy(out=rstd_bc[:, :], in_=PS12[:, 0:WH]),
         reads=[P12], writes=[buf("rstd_bc")])
    k.op("vector", lambda e: e.tensor_copy(out=nmrs_bc[:, :], in_=PS3[:, 0:WH]),
         reads=[P3], writes=[buf("nmrs_bc")])
    for dc in range(DC):
        x0c_ = [x0st0, x0st1][dc % 2]
        k.dma((dc % 4) * 2, lambda e, dc=dc, x0c_=x0c_: e.dma_start(
                out=x0c_[:, :], in_=x0p[dc * 128:(dc + 1) * 128, :]),
              writes=[buf(f"x0s_{dc % 2}")])
        eng = "vector" if dc % 2 == 0 else "gpsimd"
        sc = [e0, e1][dc % 2]
        k.op(eng, lambda e, dc=dc, sc=sc, x0c_=x0c_: e.tensor_mul(
                sc[:, :], x0c_[:, 3:WH], rstd_bc[:, 3:WH]),
             reads=[buf(f"x0s_{dc % 2}"), buf("rstd_bc")], writes=[buf(f"sc{dc % 2}")])
        k.op(eng, lambda e, dc=dc, sc=sc: e.tensor_add(
                xpb[:, dc * WH + 3:(dc + 1) * WH], sc[:, :], nmrs_bc[:, 3:WH]),
             reads=[buf(f"sc{dc % 2}"), buf("nmrs_bc")], writes=[buf(f"xpb_{dc}")])
        k.op(eng, lambda e, dc=dc, sc=sc, x0c_=x0c_: e.tensor_mul(
                sc[:, 0:3], x0c_[:, 0:3], rstd_bc[:, 0:3]),
             reads=[buf(f"x0s_{dc % 2}"), buf("rstd_bc"), buf(f"xpb_{dc}")],
             writes=[buf(f"sc{dc % 2}")])
        k.op(eng, lambda e, dc=dc, sc=sc: e.tensor_add(
                xpb[:, dc * WH:dc * WH + 3], sc[:, 0:3], nmrs_bc[:, 0:3]),
             reads=[buf(f"sc{dc % 2}")], writes=[buf(f"xpb_h{dc}")])

    xpb_bufs = [buf(f"xpb_{dc}") for dc in range(DC)] + [buf(f"xpb_h{dc}") for dc in range(DC)]
    fence_a1 = fence([buf("x0s_0"), buf("x0s_1"), buf("sq_0"), buf("rstd_bc"),
                      buf("nmrs_bc"), buf("st0"), buf("st1"), buf("st2"),
                      buf("sc0"), buf("sc1")])
    g_a1.free()

    g_left = Grp()
    left = g_left.sb("left", [128, HC * W], BF16)
    g_conv = Grp()
    convb = g_conv.sb("convb", [128, HC * W], BF16)
    g_a2t = Grp("right")

    if "attn_a" in STAGES:
        pend = {0: stream_weight("wl0", wlp, 0, DC, wrawA, wbfA0)}
        for m in range(HC):
            wbf = [wbfA0, wbfA1][m % 2]
            wbufs = pend.pop(m)
            if m + 1 < HC:
                pend[m + 1] = stream_weight(f"wl{(m + 1) % 2}", wlp, m + 1, DC, wrawA,
                                            [wbfA0, wbfA1][(m + 1) % 2])
            else:
                pend[0] = stream_weight("wr0", wrp, 0, DC, wrawA, wbfA0)
            ps, psb = [(PS3, P3), (PS12, P12)][m % 2]
            mm_acc(ps, psb, wbf, wbufs, DC,
                   lambda kc, o, cur: xpb[:, kc * WH + 3 + o:kc * WH + 3 + o + cur],
                   W, xpb_bufs)
            k.op("scalar", lambda e, ps=ps, m=m: e.activation(
                    out=left[:, m * W:(m + 1) * W], in_=ps[:, 0:W],
                    func=AF.Gelu_apprx_tanh, bias=bl_t[:, m:m + 1]),
                 reads=[psb, buf("bl_t"), fence_a1], writes=[buf(f"left_{m}")])

        for m in range(HC):
            wbf = [wbfA0, wbfA1][m % 2]
            wbufs = pend.pop(m)
            if m + 1 < HC:
                pend[m + 1] = stream_weight(f"wr{(m + 1) % 2}", wrp, m + 1, DC, wrawA,
                                            [wbfA0, wbfA1][(m + 1) % 2])
            ps, psb = [(PS12, P12), (PS3, P3)][m % 2]
            mm_acc(ps, psb, wbf, wbufs, DC,
                   lambda kc, o, cur: xpb[:, kc * WH + o:kc * WH + o + cur],
                   WH, xpb_bufs)
            rin = rin0
            rb = buf("rin0")
            k.op("scalar", lambda e, ps=ps, rin=rin, m=m: e.activation(
                    out=rin[:, :], in_=ps[:, 0:WH], func=AF.Identity, bias=br_t[:, m:m + 1]),
                 reads=[psb, buf("br_t"), fence_a1], writes=[rb])
            k.op("vector", lambda e, rin=rin: e.tensor_scalar(
                    out=rin[:, 0:3], in0=rin[:, 0:3], scalar1=hv_b[:, 0:1], scalar2=None,
                    op0=ALU.mult),
                 reads=[rb, buf("hv_b")], writes=[rb])
            cacc = cacc0
            cb = buf("cacc0")
            k.op("vector", lambda e, rin=rin, cacc=cacc, m=m: e.tensor_scalar(
                    out=cacc[:, :], in0=rin[:, 0:W], scalar1=cw_t[:, 4 * m:4 * m + 1],
                    scalar2=None, op0=ALU.mult),
                 reads=[rb, buf("cw_t")], writes=[cb])
            for j in (1, 2, 3):
                eng = "vector"
                k.op(eng, lambda e, rin=rin, cacc=cacc, m=m, j=j: e.scalar_tensor_tensor(
                        out=cacc[:, :], in0=rin[:, j:j + W],
                        scalar=cw_t[:, 4 * m + j:4 * m + j + 1],
                        in1=cacc[:, :], op0=ALU.mult, op1=ALU.add),
                     reads=[rb, buf("cw_t"), cb], writes=[cb])
            k.op("vector", lambda e, cacc=cacc, m=m: e.tensor_scalar(
                    out=convb[:, m * W:(m + 1) * W], in0=cacc[:, :],
                    scalar1=cb_t[:, m:m + 1], scalar2=None, op0=ALU.add),
                 reads=[cb, buf("cb_t"), fence_a1], writes=[buf(f"convb_{m}")])

    conv_bufs = [buf(f"convb_{m}") for m in range(HC)]
    left_bufs = [buf(f"left_{m}") for m in range(HC)]
    fence_a2 = fence(xpb_bufs + [buf("u0"), buf("rin0"), buf("cacc0"), buf("e2b0")])
    g_a2t.free()
    g_xp.free()

    g_a3 = Grp("right")
    hbuf = g_a3.sb("hbuf", [128, W])
    cpbuf = g_a3.sb("cpbuf", [128, W])
    wrawI = g_a3.sb("wrawI", [128, HC * 128])
    wbfI = g_a3.sb("wbfI", [128, HC * 128], BF16)
    wbfI2 = g_a3.sb("wbfI2", [128, HC * 128], BF16)
    lcst = g_a3.sb("lcst", [128, W], BF16)

    if "attn_a" in STAGES:
        pendA = {0: stream_weight("wa0", wap, 0, HC, wrawA, wbfA0, "scalar")}
        pendI = {0: stream_weight("wi0", wip, 0, HC, wrawI, wbfI, "gpsimd",
                                  extra=[fence_a2], soff=1)}
        for m in range(HC):
            wbA = [wbfA0, wbfA1][m % 2]
            wbI = [wbfI, wbfI2][m % 2]
            wbufsA = pendA.pop(m)
            wbufsI = pendI.pop(m)
            if m + 1 < HC:
                pendA[m + 1] = stream_weight(f"wa{(m + 1) % 2}", wap, m + 1, HC, wrawA,
                                             [wbfA0, wbfA1][(m + 1) % 2], "scalar")
                pendI[m + 1] = stream_weight(f"wi{(m + 1) % 2}", wip, m + 1, HC, wrawI,
                                             [wbfI, wbfI2][(m + 1) % 2], "gpsimd",
                                             soff=1)
            psR, psRb = [(PS3, P3), (PS12, P12)][m % 2]
            psI, psIb = [(PS12, P12), (PS3, P3)][m % 2]
            for kc in range(HC):
                for o in (0, 512):
                    last = kc == HC - 1 and o == 512
                    first = kc == 0 and o == 0
                    k.op("tensor", lambda e, kc=kc, o=o, wbA=wbA, psR=psR: e.matmul(
                            psR[:, o:o + 512], wbA[:, kc * 128:(kc + 1) * 128],
                            convb[:, kc * W + o:kc * W + o + 512],
                            start=(kc == 0), stop=(kc == HC - 1)),
                         reads=wbufsA + conv_bufs,
                         writes=[psRb] if (last or first) else (), inc=last)
                    k.op("tensor", lambda e, kc=kc, o=o, wbI=wbI, psI=psI: e.matmul(
                            psI[:, o:o + 512], wbI[:, kc * 128:(kc + 1) * 128],
                            convb[:, kc * W + o:kc * W + o + 512],
                            start=(kc == 0), stop=(kc == HC - 1)),
                         reads=wbufsI + conv_bufs + [fence_a2],
                         writes=[psIb] if (last or first) else (), inc=last)
            k.op("scalar", lambda e, psR=psR, m=m: e.activation(
                    out=rin0[:, 0:W], in_=psR[:, 0:W], func=AF.Sigmoid,
                    bias=ba_t[:, m:m + 1]),
                 reads=[psRb, buf("ba_t"), fence_a2], writes=[buf("rin0")])
            k.op("scalar", lambda e, psI=psI, m=m: e.activation(
                    out=cacc0[:, :], in_=psI[:, 0:W], func=AF.Sigmoid,
                    bias=bi_t[:, m:m + 1]),
                 reads=[psIb, buf("bi_t"), fence_a2], writes=[buf("cacc0")])
            k.op("scalar", lambda e, m=m: e.activation(
                    out=e1[:, :], in_=rin0[:, 0:W], func=AF.Exp,
                    scale=slam_t[:, m:m + 1]),
                 reads=[buf("rin0"), buf("slam")], writes=[buf("e1")])
            k.op("scalar", lambda e, m=m: e.activation(
                    out=e2b[:, :], in_=rin0[:, 0:W], func=AF.Exp,
                    scale=slam2_t[:, m:m + 1]),
                 reads=[buf("rin0"), buf("slam2")], writes=[buf("e2b")])
            k.op("vector", lambda e: e.tensor_scalar(out=e2b[:, :], in0=e2b[:, :],
                                                     scalar1=-1.0, scalar2=1.0,
                                                     op0=ALU.mult, op1=ALU.add),
                 reads=[buf("e2b")], writes=[buf("e2b")])
            k.op("vector", lambda e: e.tensor_scalar_max(e2b[:, :], e2b[:, :], 1e-12),
                 reads=[buf("e2b")], writes=[buf("e2b")])
            k.op("scalar", lambda e: e.activation(out=e2b[:, :], in_=e2b[:, :], func=AF.Sqrt),
                 reads=[buf("e2b")], writes=[buf("e2b")])
            k.op("gpsimd", lambda e, m=m: e.tensor_mul(
                    ubuf0[:, :], cacc0[:, :], convb[:, m * W:(m + 1) * W]),
                 reads=[buf("cacc0"), conv_bufs[m]], writes=[buf("ubuf0")])
            k.op("gpsimd", lambda e: e.tensor_mul(ubuf0[:, :], ubuf0[:, :], e2b[:, :]),
                 reads=[buf("ubuf0"), buf("e2b")], writes=[buf("ubuf0")])
            k.op("vector", lambda e: e.tensor_tensor_scan(
                    out=hbuf[:, :], data0=e1[:, :], data1=ubuf0[:, :], initial=0.0,
                    op0=ALU.mult, op1=ALU.add),
                 reads=[buf("e1"), buf("ubuf0")], writes=[buf("hbuf")])
            k.op("vector", lambda e: e.tensor_tensor_scan(
                    out=cpbuf[:, :], data0=e1[:, :], data1=zeros_bw[:, :], initial=1.0,
                    op0=ALU.mult, op1=ALU.add),
                 reads=[buf("e1"), cZER], writes=[buf("cpbuf")])
            k.op("vector", lambda e, m=m: e.reduce_sum(
                    out=atot_t[:, m:m + 1], in_=rin0[:, 0:W], axis=AX.X),
                 reads=[buf("rin0")], writes=[buf("atot")])
            k.op("scalar", lambda e, m=m: e.activation(
                    out=atot_t[:, m:m + 1], in_=atot_t[:, m:m + 1], func=AF.Exp,
                    scale=slam_t[:, m:m + 1]),
                 reads=[buf("atot"), buf("slam")], writes=[buf("atot")])
            k.op("vector", lambda e, m=m: e.tensor_copy(
                    out=hlast_t[:, m:m + 1], in_=hbuf[:, W - 1:W]),
                 reads=[buf("hbuf")], writes=[buf("hlast")])
            k.op("gpsimd", lambda e, m=m: e.tensor_mul(
                    lcst[:, :], left[:, m * W:(m + 1) * W], cpbuf[:, :]),
                 reads=[left_bufs[m], buf("cpbuf"), fence_a2, buf("lcst")],
                 writes=[buf("lcst")])
            k.dma(5, lambda e, m=m: e.dma_start(
                    out=lcp_dram[m * 128:(m + 1) * 128, :], in_=lcst[:, :]),
                  reads=[buf("lcst")], writes=[buf("lcpd")])
            k.op("gpsimd", lambda e, m=m: e.tensor_mul(
                    left[:, m * W:(m + 1) * W], left[:, m * W:(m + 1) * W], hbuf[:, :]),
                 reads=[left_bufs[m], buf("hbuf")], writes=[left_bufs[m]])

        k.dma(6, lambda e: e.dma_start(
                out=relay_in[0:1, 0:HID].rearrange("a (p c) -> (a p) c", p=128),
                in_=atot_t[:, :]),
              reads=[buf("atot")], writes=[buf("relay_in")])
        k.dma(6, lambda e: e.dma_start(
                out=relay_in[0:1, HID:2 * HID].rearrange("a (p c) -> (a p) c", p=128),
                in_=hlast_t[:, :]),
              reads=[buf("hlast")], writes=[buf("relay_in")])
        k.cc(lambda e: e.collective_compute(
                "AllGather", ALU.bypass, replica_groups=GROUPS4,
                ins=[relay_in], outs=[relay_out]),
             reads=[buf("relay_in")], writes=[buf("relay_out")])
        for j in range(4):
            k.dma(6, lambda e, j=j: e.dma_start(
                    out=rel[:, j * HC:(j + 1) * HC],
                    in_=relay_out[j:j + 1, 0:HID].rearrange("a (p c) -> (a p) c", p=128)),
                  reads=[buf("relay_out")], writes=[buf("rel")])
            k.dma(6, lambda e, j=j: e.dma_start(
                    out=rel[:, (4 + j) * HC:(5 + j) * HC],
                    in_=relay_out[j:j + 1, HID:2 * HID].rearrange("a (p c) -> (a p) c", p=128)),
                  reads=[buf("relay_out")], writes=[buf("rel")])
        k.op("vector", lambda e: e.memset(Hst[:, :], 0.0), writes=[buf("Hst")])
        for j in range(4):
            k.op("vector", lambda e, j=j: e.tensor_scalar(
                    out=e0[:, 0:HC], in0=rel[:, j * HC:(j + 1) * HC],
                    scalar1=qsel_b[:, j:j + 1], scalar2=omq_b[:, j:j + 1],
                    op0=ALU.mult, op1=ALU.add),
                 reads=[buf("rel"), buf("qsel_b"), buf("omq_b")], writes=[buf("e0")])
            k.op("vector", lambda e, j=j: e.tensor_scalar(
                    out=e1[:, 0:HC], in0=rel[:, (4 + j) * HC:(5 + j) * HC],
                    scalar1=qsel_b[:, j:j + 1], scalar2=None, op0=ALU.mult),
                 reads=[buf("rel"), buf("qsel_b")], writes=[buf("e1")])
            k.op("vector", lambda e: e.tensor_mul(Hst[:, :], Hst[:, :], e0[:, 0:HC]),
                 reads=[buf("Hst"), buf("e0")], writes=[buf("Hst")])
            k.op("vector", lambda e: e.tensor_add(Hst[:, :], Hst[:, :], e1[:, 0:HC]),
                 reads=[buf("Hst"), buf("e1")], writes=[buf("Hst")])
        for m in range(HC):
            k.dma(5, lambda e, m=m: e.dma_start(
                    out=lcst[:, :], in_=lcp_dram[m * 128:(m + 1) * 128, :]),
                  reads=[buf("lcpd"), buf("lcst")], writes=[buf("lcst")])
            k.op("vector", lambda e, m=m: e.scalar_tensor_tensor(
                    out=left[:, m * W:(m + 1) * W], in0=lcst[:, :],
                    scalar=Hst[:, m:m + 1], in1=left[:, m * W:(m + 1) * W],
                    op0=ALU.mult, op1=ALU.add),
                 reads=[buf("lcst"), buf("Hst"), left_bufs[m]],
                 writes=[left_bufs[m]])

    fence_a3 = fence(conv_bufs + [buf("lcst"), buf("hbuf"), buf("cpbuf")] +
                     [buf(n) for n in list(B) if n.startswith(("raw_", "bf_"))])
    g_a3.free()
    g_conv.free()

    g_a5 = Grp("right")
    x = g_a5.sb("x", [128, DC * W])
    g_a5r = Grp("right")
    x0c0 = g_a5r.sb("x0c0", [128, W])
    x0c1 = x0c0

    if "attn_a" in STAGES:
        pend = {0: stream_weight("wo0", wop, 0, HC, wrawA, wbfA0)}
        for mo in range(DC):
            wbf = [wbfA0, wbfA1][mo % 2]
            wbufs = pend.pop(mo)
            if mo + 1 < DC:
                pend[mo + 1] = stream_weight(f"wo{(mo + 1) % 2}", wop, mo + 1, HC, wrawA,
                                             [wbfA0, wbfA1][(mo + 1) % 2])
            ps, psb = [(PS3, P3), (PS12, P12)][mo % 2]
            mm_acc(ps, psb, wbf, wbufs, HC,
                   lambda kc, o, cur: left[:, kc * W + o:kc * W + o + cur],
                   W, left_bufs)
            x0c = x0c0
            k.dma(4, lambda e, mo=mo, x0c=x0c: e.dma_start(
                    out=x0c[:, :], in_=x0p[mo * 128:(mo + 1) * 128, 3:WH]),
                  reads=[fence_a3, buf("x0c0")], writes=[buf("x0c0")])
            k.op("vector", lambda e, ps=ps, mo=mo, x0c=x0c: e.scalar_tensor_tensor(
                    out=x[:, mo * W:(mo + 1) * W], in0=ps[:, 0:W],
                    scalar=bo_t[:, mo:mo + 1], in1=x0c[:, :], op0=ALU.add, op1=ALU.add),
                 reads=[psb, buf("bo_t"), buf("x0c0"), fence_a3],
                 writes=[buf(f"x_{mo}")])
    else:
        for mo in range(DC):
            k.dma(4, lambda e, mo=mo: e.dma_start(
                    out=x[:, mo * W:(mo + 1) * W], in_=x0p[mo * 128:(mo + 1) * 128, 3:WH]),
                  reads=[fence_a3], writes=[buf(f"x_{mo}")])

    x_bufs = [buf(f"x_{mo}") for mo in range(DC)]
    fence_a5 = fence(left_bufs +
                     [buf(n) for n in list(B) if n.startswith(("raw_", "bf_"))] +
                     [buf("x0c0")])
    g_a5r.free()
    g_left.free()
    g_attA.free()

    gFper = Grp()
    wrt_t = gFper.sb("wrt_t", [128, DC]); g1_t = gFper.sb("g1_t", [128, DC])
    g2_t = gFper.sb("g2_t", [128, DC])
    blk_t = gFper.sb("blk_t", [128, 4]); qsel4_t = gFper.sb("qsel4_t", [4, 1])
    ones_f = gFper.sb("ones_f", [128, 1])
    cfm = gFper.sb("cfm", [128, 1024], BF16); sfm = gFper.sb("sfm", [128, 1024], BF16)
    fones = gFper.sb("fones", [1, 1024])
    crawf = gFper.sb("crawf", [128, 1024])
    cONESF = buf("cONESF")
    k.op("vector", lambda e: e.memset(ones_f[:, :], 1.0), reads=[fence_a5], writes=[cONESF])
    k.op("vector", lambda e: e.memset(fones[:, :], 1.0), reads=[fence_a5], writes=[buf("fones")])
    for nm, tt_, ap_ in [("wrt_t", wrt_t, wrt_i), ("g1_t", g1_t, g1_i), ("g2_t", g2_t, g2_i),
                         ("blk_t", blk_t, blk_i), ("qsel4_t", qsel4_t, qsel4_i)]:
        k.dma(7, lambda e, t=tt_, a=ap_: e.dma_start(out=t[:, :], in_=a),
              reads=[fence_a5], writes=[buf(nm)])
    k.dma(7, lambda e: e.dma_start(out=crawf[0:64, :], in_=fcos_i),
          reads=[fence_a5], writes=[buf("crawf")])
    k.dma(7, lambda e: e.dma_start(out=crawf[64:128, :], in_=fcos_i),
          reads=[fence_a5], writes=[buf("crawf")])
    k.op("vector", lambda e: e.tensor_copy(out=cfm[:, :], in_=crawf[:, :]),
         reads=[buf("crawf"), fence_a5], writes=[buf("cfm")])
    k.dma(7, lambda e: e.dma_start(out=crawf[0:64, :], in_=fsin_i),
          reads=[buf("cfm")], writes=[buf("crawf")])
    k.dma(7, lambda e: e.dma_start(out=crawf[64:128, :], in_=fsin_i),
          reads=[buf("cfm")], writes=[buf("crawf")])
    k.op("vector", lambda e: e.tensor_copy(out=sfm[:, :], in_=crawf[:, :]),
         reads=[buf("crawf")], writes=[buf("sfm")])

    def fraw_bufs(fr, o0, o1):
        return [buf(f"raw_{id(fr)}_{g}") for g in range(o0 // 128, o1 // 128)]

    def emit_ffn(tag, xt, x_bufs, fence_in, free_x):
        T = tag
        gF1 = Grp()
        xpbf = gF1.sb(T + "xpbf", [128, DC * W], BF16)
        rstdb = gF1.sb(T + "rstdb", [128, W])
        nmrsb = gF1.sb(T + "nmrsb", [128, W])
        statf = gF1.sb(T + "statf", [1, 3 * W])
        sqf = gF1.sb(T + "sqf", [128, W], BF16)
        xc = gF1.sb(T + "xc", [128, W])
        lgt = gF1.sb(T + "lgt", [128, 36])
        cscr = gF1.sb(T + "cscr", [128, 36])
        lohi = gF1.sb(T + "lohi", [1, 12])
        llocal = gF1.sb(T + "llocal", [1, 1024])
        lrow2 = gF1.sb(T + "lrow2", [1, 2 * 1024])
        idxi_row = gF1.sb(T + "idxi_row", [1, 1024], I32)
        trs = gF1.sb(T + "trs", [128, 128])
        scst = gF1.sb(T + "scst", [128, 2048], BF16)
        idxt = gF1.sb(T + "idxt", [128, 8], I32)
        TRS = buf(T + "trs")

        s0 = statf[0:1, 0:W]; s1 = statf[0:1, W:2 * W]; s2 = statf[0:1, 2 * W:3 * W]
        for dc in range(DC):
            k.op("vector", lambda e, dc=dc: e.tensor_copy(
                    out=xpbf[:, dc * W:(dc + 1) * W], in_=xt[:, dc * W:(dc + 1) * W]),
                 reads=[x_bufs[dc], fence_in], writes=[buf(T + f"xb_{dc}")])
            k.op("scalar", lambda e, dc=dc: e.activation(
                    out=sqf[:, :], in_=xpbf[:, dc * W:(dc + 1) * W], func=AF.Square),
                 reads=[buf(T + f"xb_{dc}")], writes=[buf(T + "sq")])
            for o in (0, 512):
                k.op("tensor", lambda e, dc=dc, o=o: e.matmul(
                        PS12[0:1, o:o + 512], ones_b[:, :],
                        xpbf[:, dc * W + o:dc * W + o + 512],
                        start=(dc == 0), stop=(dc == DC - 1)),
                     reads=[buf(T + f"xb_{dc}"), cONES_B], writes=[P12])
                k.op("tensor", lambda e, o=o, dc=dc: e.matmul(
                        PS3[0:1, o:o + 512], ones_b[:, :], sqf[:, o:o + 512],
                        start=(dc == 0), stop=(dc == DC - 1)),
                     reads=[buf(T + "sq"), cONES_B], writes=[P3])
        k.op("vector", lambda e: e.tensor_scalar_mul(s0, PS12[0:1, 0:W], 1.0 / D),
             reads=[P12], writes=[buf(T + "st0")])
        k.op("vector", lambda e: e.tensor_scalar_mul(s1, PS3[0:1, 0:W], 1.0 / D),
             reads=[P3], writes=[buf(T + "st1")])
        k.op("vector", lambda e: e.tensor_mul(s2, s0, s0),
             reads=[buf(T + "st0")], writes=[buf(T + "st2")])
        k.op("vector", lambda e: e.tensor_sub(s1, s1, s2),
             reads=[buf(T + "st1"), buf(T + "st2")], writes=[buf(T + "st1")])
        k.op("scalar", lambda e: e.activation(out=s1, in_=s1, func=AF.Sqrt,
                                              bias=cst[0:1, 0:1]),
             reads=[buf(T + "st1"), buf("cst")], writes=[buf(T + "st1")])
        k.op("vector", lambda e: e.reciprocal(s1, s1),
             reads=[buf(T + "st1")], writes=[buf(T + "st1")])
        k.op("vector", lambda e: e.tensor_mul(s0, s0, s1),
             reads=[buf(T + "st0"), buf(T + "st1")], writes=[buf(T + "st0")])
        k.op("vector", lambda e: e.tensor_scalar_mul(s0, s0, -1.0),
             reads=[buf(T + "st0")], writes=[buf(T + "st0")])
        for o in (0, 512):
            k.op("tensor", lambda e, o=o: e.matmul(
                    PS12[:, o:o + 512], ones_row[:, :], s1[0:1, o:o + 512],
                    start=True, stop=True),
                 reads=[cONES_R, buf(T + "st1")], writes=[P12])
            k.op("tensor", lambda e, o=o: e.matmul(
                    PS3[:, o:o + 512], ones_row[:, :], s0[0:1, o:o + 512],
                    start=True, stop=True),
                 reads=[cONES_R, buf(T + "st0")], writes=[P3])
        k.op("vector", lambda e: e.tensor_copy(out=rstdb[:, :], in_=PS12[:, 0:W]),
             reads=[P12], writes=[buf(T + "rstdb")])
        k.op("vector", lambda e: e.tensor_copy(out=nmrsb[:, :], in_=PS3[:, 0:W]),
             reads=[P3], writes=[buf(T + "nmrsb")])
        for dc in range(DC):
            k.op("vector", lambda e, dc=dc: e.tensor_mul(
                    xc[:, :], xt[:, dc * W:(dc + 1) * W], rstdb[:, :]),
                 reads=[x_bufs[dc], buf(T + "rstdb"), buf(T + "xcb")],
                 writes=[buf(T + "xcb")])
            k.op("vector", lambda e, dc=dc: e.tensor_add(xc[:, :], xc[:, :], nmrsb[:, :]),
                 reads=[buf(T + "xcb"), buf(T + "nmrsb")], writes=[buf(T + "xcb")])
            for o in (0, 512):
                k.op("tensor", lambda e, dc=dc, o=o: e.matmul(
                        PS12[0:1, o:o + 512], wrt_t[:, dc:dc + 1], xc[:, o:o + 512],
                        start=(dc == 0), stop=(dc == DC - 1)),
                     reads=[buf("wrt_t"), buf(T + "xcb")], writes=[P12])
            k.op("gpsimd", lambda e, dc=dc: e.tensor_copy(
                    out=xpbf[:, dc * W:(dc + 1) * W], in_=xc[:, :]),
                 reads=[buf(T + "xcb")], writes=[buf(T + f"xb_{dc}")])
            k.op("gpsimd", lambda e, dc=dc: e.tensor_add(
                    xt[:, dc * W:(dc + 1) * W], xt[:, dc * W:(dc + 1) * W], xc[:, :]),
                 reads=[buf(T + "xcb"), x_bufs[dc]], writes=[x_bufs[dc]])
            k.dma((dc % 4) * 2, lambda e, dc=dc: e.dma_start(
                    out=xst_dram[:, dc * W:(dc + 1) * W], in_=xt[:, dc * W:(dc + 1) * W]),
                  reads=[x_bufs[dc]], writes=[buf("xst")])
        xp_bufs = [buf(T + f"xb_{dc}") for dc in range(DC)]
        k.op("vector", lambda e: e.tensor_copy(out=llocal[:, :], in_=PS12[0:1, 0:1024]),
             reads=[P12], writes=[buf(T + "llocal")])
        k.dma(6, lambda e: e.dma_start(out=lg_in, in_=llocal[:, :]),
              reads=[buf(T + "llocal")], writes=[buf("lg_in")])
        k.cc(lambda e: e.collective_compute(
                "AllGather", ALU.bypass, replica_groups=GROUPS4,
                ins=[lg_in], outs=[lg_out]),
             reads=[buf("lg_in")], writes=[buf("lg_out")])
        k.dma(6, lambda e: e.dma_start(
                out=lgt[:, 0:32], in_=lg_out.rearrange("a (p c) -> (a p) c", p=128)),
              reads=[buf("lg_out")], writes=[buf(T + "lgt")])
        for ch in range(8):
            for hh in range(2):
                k.dma(2, lambda e, ch=ch, hh=hh: e.dma_start(
                        out=cpt_in[ch * 128:(ch + 1) * 128, hh * 1024:(hh + 1) * 1024],
                        in_=zeros_bw[:, :]),
                      reads=[cZER], writes=[buf("cpt_in")])
        k.op("vector", lambda e: e.memset(trs[:, :], 0.0), writes=[TRS])
        k.op("vector", lambda e: e.tensor_reduce(
                out=trs[:, 0:1], in_=lgt[:, 0:32], axis=AX.X, op=ALU.min),
             reads=[buf(T + "lgt")], writes=[TRS])
        k.op("vector", lambda e: e.reduce_max(out=trs[:, 32:33], in_=lgt[:, 0:32], axis=AX.X),
             reads=[buf(T + "lgt")], writes=[TRS])
        k.op("tensor", lambda e: e.transpose(PS3[:, 0:128], trs[:, :], identf[:, :]),
             reads=[TRS, buf("identf")], writes=[P3])
        k.op("vector", lambda e: e.tensor_reduce(
                out=lohi[0:1, 0:1], in_=PS3[0:1, 0:128], axis=AX.X, op=ALU.min),
             reads=[P3], writes=[buf(T + "lo")])
        k.op("vector", lambda e: e.reduce_max(
                out=lohi[0:1, 1:2], in_=PS3[32:33, 0:128], axis=AX.X),
             reads=[P3], writes=[buf(T + "hi")])
        k.op("vector", lambda e: e.tensor_scalar_add(lohi[0:1, 0:1], lohi[0:1, 0:1], -1.0),
             reads=[buf(T + "lo")], writes=[buf(T + "lo")])
        for it in range(48):
            k.op("vector", lambda e: e.tensor_add(lohi[0:1, 2:3], lohi[0:1, 0:1],
                                                  lohi[0:1, 1:2]),
                 reads=[buf(T + "lo"), buf(T + "hi")], writes=[buf(T + "mid")])
            k.op("vector", lambda e: e.tensor_scalar_mul(lohi[0:1, 2:3], lohi[0:1, 2:3], 0.5),
                 reads=[buf(T + "mid")], writes=[buf(T + "mid")])
            k.op("tensor", lambda e: e.matmul(PS3[:, 140:141], ones_row[:, :],
                                              lohi[0:1, 2:3], start=True, stop=True),
                 reads=[cONES_R, buf(T + "mid")], writes=[P3])
            k.op("vector", lambda e: e.tensor_scalar(
                    out=cscr[:, 0:32], in0=lgt[:, 0:32], scalar1=PS3[:, 140:141],
                    scalar2=0.0, op0=ALU.is_gt, op1=ALU.add, accum_out=cscr[:, 34:35]),
                 reads=[buf(T + "lgt"), P3], writes=[buf(T + "csum")])
            k.op("tensor", lambda e: e.matmul(PS3[0:1, 141:142], cscr[:, 34:35],
                                              ones_f[:, :], start=True, stop=True),
                 reads=[buf(T + "csum"), cONESF], writes=[P3])
            k.op("vector", lambda e: e.tensor_single_scalar(
                    out=lohi[0:1, 3:4], in_=PS3[0:1, 141:142], scalar=1023.5,
                    op=ALU.is_gt),
                 reads=[P3], writes=[buf(T + "ge")])
            k.op("vector", lambda e: e.tensor_sub(lohi[0:1, 4:5], lohi[0:1, 2:3],
                                                  lohi[0:1, 0:1]),
                 reads=[buf(T + "mid"), buf(T + "lo")], writes=[buf(T + "dm")])
            k.op("vector", lambda e: e.scalar_tensor_tensor(
                    out=lohi[0:1, 0:1], in0=lohi[0:1, 4:5], scalar=lohi[0:1, 3:4],
                    in1=lohi[0:1, 0:1], op0=ALU.mult, op1=ALU.add),
                 reads=[buf(T + "dm"), buf(T + "ge"), buf(T + "lo")],
                 writes=[buf(T + "lo")])
            k.op("vector", lambda e: e.tensor_scalar(
                    out=lohi[0:1, 5:6], in0=lohi[0:1, 3:4], scalar1=-1.0, scalar2=1.0,
                    op0=ALU.mult, op1=ALU.add),
                 reads=[buf(T + "ge")], writes=[buf(T + "omge")])
            k.op("vector", lambda e: e.tensor_sub(lohi[0:1, 6:7], lohi[0:1, 2:3],
                                                  lohi[0:1, 1:2]),
                 reads=[buf(T + "mid"), buf(T + "hi")], writes=[buf(T + "dmh")])
            k.op("vector", lambda e: e.scalar_tensor_tensor(
                    out=lohi[0:1, 1:2], in0=lohi[0:1, 6:7], scalar=lohi[0:1, 5:6],
                    in1=lohi[0:1, 1:2], op0=ALU.mult, op1=ALU.add),
                 reads=[buf(T + "dmh"), buf(T + "omge"), buf(T + "hi")],
                 writes=[buf(T + "hi")])
        k.op("tensor", lambda e: e.matmul(PS3[:, 140:141], ones_row[:, :],
                                          lohi[0:1, 0:1], start=True, stop=True),
             reads=[cONES_R, buf(T + "lo")], writes=[P3])
        k.op("vector", lambda e: e.tensor_copy(out=cscr[:, 35:36], in_=PS3[:, 140:141]),
             reads=[P3], writes=[buf(T + "lobc")])
        k.op("vector", lambda e: e.tensor_scalar(
                out=cscr[:, 0:32], in0=lgt[:, 0:32], scalar1=cscr[:, 35:36], scalar2=None,
                op0=ALU.is_gt),
             reads=[buf(T + "lgt"), buf(T + "lobc")], writes=[buf(T + "mglob")])
        k.op("tensor", lambda e: e.matmul(PS3[0:4, 256:288], blk_t[:, :], cscr[:, 0:32],
                                          start=True, stop=True),
             reads=[buf("blk_t"), buf(T + "mglob")], writes=[P3])
        k.op("vector", lambda e: e.reduce_sum(out=trs[0:4, 1:2], in_=PS3[0:4, 256:288],
                                              axis=AX.X),
             reads=[P3], writes=[TRS])
        k.op("tensor", lambda e: e.matmul(PS3[0:1, 142:143], trs[0:4, 1:2],
                                          qsel4_t[:, :], start=True, stop=True),
             reads=[TRS, buf("qsel4_t")], writes=[P3])
        k.op("vector", lambda e: e.tensor_scalar_add(lohi[0:1, 7:8], PS3[0:1, 142:143],
                                                     -1.0),
             reads=[P3], writes=[buf(T + "cb1")])
        k.op("vector", lambda e: e.tensor_scalar(
                out=cscr[:, 0:32], in0=cscr[:, 0:32], scalar1=1.0e9, scalar2=-1.0e9,
                op0=ALU.mult, op1=ALU.add),
             reads=[buf(T + "mglob")], writes=[buf(T + "mglob")])
        k.op("vector", lambda e: e.tensor_add(cscr[:, 0:32], cscr[:, 0:32], lgt[:, 0:32]),
             reads=[buf(T + "mglob"), buf(T + "lgt")], writes=[buf(T + "mlg")])
        k.op("vector", lambda e: e.reduce_max(out=trs[:, 64:65], in_=cscr[:, 0:32],
                                              axis=AX.X),
             reads=[buf(T + "mlg")], writes=[TRS])
        k.op("tensor", lambda e: e.transpose(PS3[:, 0:128], trs[:, :], identf[:, :]),
             reads=[TRS, buf("identf")], writes=[P3])
        k.op("vector", lambda e: e.reduce_max(out=lohi[0:1, 8:9], in_=PS3[64:65, 0:128],
                                              axis=AX.X),
             reads=[P3], writes=[buf(T + "Mv")])
        k.op("vector", lambda e: e.tensor_scalar_mul(lohi[0:1, 9:10], lohi[0:1, 8:9], -1.0),
             reads=[buf(T + "Mv")], writes=[buf(T + "Mneg")])
        k.op("tensor", lambda e: e.matmul(PS3[:, 140:141], ones_row[:, :], lohi[0:1, 9:10],
                                          start=True, stop=True),
             reads=[cONES_R, buf(T + "Mneg")], writes=[P3])
        k.op("vector", lambda e: e.tensor_copy(out=cscr[:, 35:36], in_=PS3[:, 140:141]),
             reads=[P3, buf(T + "lobc")], writes=[buf(T + "Mnb")])
        k.op("scalar", lambda e: e.activation(
                out=cscr[:, 0:32], in_=cscr[:, 0:32], func=AF.Exp, bias=cscr[:, 35:36],
                accum_out=cscr[:, 34:35]),
             reads=[buf(T + "mlg"), buf(T + "Mnb"), buf(T + "csum")],
             writes=[buf(T + "esum")]),
        k.op("tensor", lambda e: e.matmul(PS3[0:1, 141:142], cscr[:, 34:35], ones_f[:, :],
                                          start=True, stop=True),
             reads=[buf(T + "esum"), cONESF], writes=[P3])
        k.op("vector", lambda e: e.reciprocal(lohi[0:1, 10:11], PS3[0:1, 141:142]),
             reads=[P3], writes=[buf(T + "Zi")])
        k.op("vector", lambda e: e.tensor_single_scalar(
                out=lrow2[0:1, 0:1024], in_=llocal[:, :], scalar=lohi[0:1, 0:1],
                op=ALU.is_gt),
             reads=[buf(T + "llocal"), buf(T + "lo")], writes=[buf(T + "mmy")])
        k.op("vector", lambda e: e.tensor_tensor_scan(
                out=lrow2[0:1, 1024:2048], data0=fones[:, :], data1=lrow2[0:1, 0:1024],
                initial=0.0, op0=ALU.mult, op1=ALU.add),
             reads=[buf(T + "mmy"), buf("fones")], writes=[buf(T + "cmy")])
        k.op("vector", lambda e: e.tensor_single_scalar(
                out=lrow2[0:1, 1024:2048], in_=lrow2[0:1, 1024:2048], scalar=lohi[0:1, 7:8],
                op=ALU.add),
             reads=[buf(T + "cmy"), buf(T + "cb1")], writes=[buf(T + "cmy")])
        k.op("vector", lambda e: e.tensor_scalar(
                out=lrow2[0:1, 0:1024], in0=lrow2[0:1, 0:1024], scalar1=-4096.0,
                scalar2=4096.0, op0=ALU.mult, op1=ALU.add),
             reads=[buf(T + "mmy")], writes=[buf(T + "mmy")])
        k.op("vector", lambda e: e.tensor_add(
                lrow2[0:1, 1024:2048], lrow2[0:1, 1024:2048], lrow2[0:1, 0:1024]),
             reads=[buf(T + "cmy"), buf(T + "mmy")], writes=[buf(T + "cmy")])
        k.op("vector", lambda e: e.tensor_copy(out=idxi_row[0:1, :],
                                               in_=lrow2[0:1, 1024:2048]),
             reads=[buf(T + "cmy")], writes=[buf(T + "idxi")])
        k.dma(6, lambda e: e.dma_start(out=idx_dram, in_=idxi_row[0:1, :]),
              reads=[buf(T + "idxi")], writes=[buf("idx_dram")])
        k.op("scalar", lambda e: e.activation(
                out=lrow2[0:1, 0:1024], in_=llocal[:, :], func=AF.Exp,
                bias=lohi[0:1, 9:10]),
             reads=[buf(T + "llocal"), buf(T + "Mneg")],
             writes=[buf(T + "rw"), buf(T + "mmy")])
        k.op("vector", lambda e: e.tensor_single_scalar(
                out=lrow2[0:1, 0:1024], in_=lrow2[0:1, 0:1024], scalar=lohi[0:1, 10:11],
                op=ALU.mult),
             reads=[buf(T + "rw"), buf(T + "Zi")], writes=[buf(T + "rw")])
        k.dma(6, lambda e: e.dma_start(out=rw_dram, in_=lrow2[0:1, 0:1024]),
              reads=[buf(T + "rw")], writes=[buf("rw_dram")])
        for ch in range(8):
            k.dma(6, lambda e, ch=ch: e.dma_start(
                    out=idxt[:, ch:ch + 1],
                    in_=idx_dram[0:1, ch * 128:(ch + 1) * 128].rearrange(
                        "a (p c) -> (a p) c", p=128)),
                  reads=[buf("idx_dram")], writes=[buf(T + "idxt")])
            for dc in range(DC):
                col = (dc % 8) * 128
                k.op("tensor", lambda e, ch=ch, dc=dc, col=col: e.transpose(
                        PSB[:, col:col + 128],
                        xpbf[:, dc * W + ch * 128:dc * W + (ch + 1) * 128],
                        identb[:, :]),
                     reads=xp_bufs + [buf("identb")],
                     writes=[PB] if dc in (0, 7, 8, 15) else (), inc=dc in (7, 15))
                if dc in (7, 15):
                    for d2 in range(dc - 7, dc + 1):
                        k.op("vector", lambda e, d2=d2: e.tensor_copy(
                                out=scst[:, d2 * 128:(d2 + 1) * 128],
                                in_=PSB[:, (d2 % 8) * 128:(d2 % 8 + 1) * 128]),
                             reads=[PB], writes=[buf(T + "scst")])
            k.dma(5, lambda e, ch=ch: e.indirect_dma_start(
                    out=cpt_in, out_offset=IndirectOffsetOnAxis(idxt[:, ch:ch + 1], 0),
                    in_=scst[:, :], in_offset=None,
                    bounds_check=1023, oob_is_err=False),
                  reads=[buf(T + "scst"), buf(T + "idxt"), buf("cpt_in")],
                  writes=[buf("cpt_in")])
        k.cc(lambda e: e.collective_compute(
                "AllReduce", ALU.add, replica_groups=GROUPS4,
                ins=[cpt_in], outs=[cpt_out]),
             reads=[buf("cpt_in")], writes=[buf("cpt_out")])

        fence_f1 = fence(xp_bufs + list(x_bufs) + [buf(T + n) for n in
                         ("rstdb", "nmrsb", "st0", "st1", "st2", "sq", "xcb",
                          "lgt", "llocal", "trs", "scst", "mglob",
                          "mlg", "esum", "csum", "idxt", "idxi", "mmy", "cmy", "rw",
                          "lo", "hi", "mid", "ge", "dm", "omge", "dmh", "cb1",
                          "Mv", "Mneg", "Zi", "lobc", "Mnb")])
        gF1.free()
        free_x()

        gFil = Grp()
        filt = gFil.sb(T + "filt", [128, DC * W], BF16)
        gFA = Grp()
        hn = gFA.sb(T + "hn", [128, DC * W], BF16)
        qr = gFA.sb(T + "qr", [128, 4 * W], BF16)
        kr = gFA.sb(T + "kr", [128, 4 * W], BF16)
        vtok = gFA.sb(T + "vtok", [128, 8 * 512], BF16)
        obuf = gFA.sb(T + "obuf", [128, 4 * W], BF16)
        PT = gFA.sb(T + "PT", [128, W], BF16)
        Pb = gFA.sb(T + "Pb", [128, W], BF16)
        wobf = gFA.sb(T + "wobf", [128, 512], BF16)
        ftst = gFA.sb(T + "ftst", [128, 2048], BF16)
        fraw = gFA.sb(T + "fraw", [128, DC * 128])
        fbf0 = gFA.sb(T + "fbf0", [128, DC * 128], BF16)
        fbf1 = gFA.sb(T + "fbf1", [128, DC * 128], BF16)
        atst = gFA.sb(T + "atst", [128, W], BF16)
        rsc = gFA.sb(T + "rsc", [128, W])
        lrow_f = gFA.sb(T + "lrow_f", [1, W])
        rstb2 = gFA.sb(T + "rstb2", [128, W])
        smx = gFA.sb(T + "smx", [128, 2])
        gWv = Grp()
        wvres = gWv.sb(T + "wvres", [128, 16 * 256], BF16)
        qraw = e1
        rwork = e0[0:64, :]
        rwork2 = e0[64:128, :]

        for ch in range(8):
            k.dma(2, lambda e, ch=ch: e.dma_start(
                    out=ftst[:, :], in_=cpt_out[ch * 128:(ch + 1) * 128, :]),
                  reads=[buf("cpt_out"), fence_f1, buf(T + "ftstB")],
                  writes=[buf(T + "ftstB")])
            for dc in range(DC):
                col = (dc % 8) * 128
                k.op("tensor", lambda e, dc=dc, col=col: e.transpose(
                        PSB[:, col:col + 128], ftst[:, dc * 128:(dc + 1) * 128],
                        identb[:, :]),
                     reads=[buf(T + "ftstB"), buf("identb")],
                     writes=[PB] if dc in (0, 7, 8, 15) else (), inc=dc in (7, 15))
                if dc in (7, 15):
                    for d2 in range(dc - 7, dc + 1):
                        k.op("vector", lambda e, d2=d2, ch=ch: e.tensor_copy(
                                out=filt[:, d2 * W + ch * 128:d2 * W + (ch + 1) * 128],
                                in_=PSB[:, (d2 % 8) * 128:(d2 % 8 + 1) * 128]),
                             reads=[PB, fence_f1], writes=[buf(T + f"filt_{d2}")])
        filt_bufs = [buf(T + f"filt_{dc}") for dc in range(DC)]

        for dc in range(DC):
            k.op("scalar", lambda e, dc=dc: e.activation(
                    out=Pb[:, :], in_=filt[:, dc * W:(dc + 1) * W], func=AF.Square),
                 reads=[filt_bufs[dc]], writes=[buf(T + "Pb")])
            for o in (0, 512):
                k.op("tensor", lambda e, o=o, dc=dc: e.matmul(
                        PS3[0:1, o:o + 512], ones_b[:, :], Pb[:, o:o + 512],
                        start=(dc == 0), stop=(dc == DC - 1)),
                     reads=[buf(T + "Pb"), cONES_B], writes=[P3])
        k.op("vector", lambda e: e.tensor_scalar_mul(lrow_f[0:1, :], PS3[0:1, 0:W], 1.0 / D),
             reads=[P3], writes=[buf(T + "ms1")])
        k.op("scalar", lambda e: e.activation(out=lrow_f[0:1, :], in_=lrow_f[0:1, :],
                                              func=AF.Sqrt, bias=cst[0:1, 1:2]),
             reads=[buf(T + "ms1"), buf("cst")], writes=[buf(T + "ms1")])
        k.op("vector", lambda e: e.reciprocal(lrow_f[0:1, :], lrow_f[0:1, :]),
             reads=[buf(T + "ms1")], writes=[buf(T + "ms1")])
        for o in (0, 512):
            k.op("tensor", lambda e, o=o: e.matmul(
                    PS12[:, o:o + 512], ones_row[:, :], lrow_f[0:1, o:o + 512],
                    start=True, stop=True),
                 reads=[cONES_R, buf(T + "ms1")], writes=[P12])
        k.op("vector", lambda e: e.tensor_copy(out=rstb2[:, :], in_=PS12[:, 0:W]),
             reads=[P12], writes=[buf(T + "rstb2")])
        for dc in range(DC):
            k.op("vector", lambda e, dc=dc: e.scalar_tensor_tensor(
                    out=hn[:, dc * W:(dc + 1) * W], in0=filt[:, dc * W:(dc + 1) * W],
                    scalar=g1_t[:, dc:dc + 1], in1=rstb2[:, :],
                    op0=ALU.mult, op1=ALU.mult),
                 reads=[filt_bufs[dc], buf("g1_t"), buf(T + "rstb2")],
                 writes=[buf(T + f"hn_{dc}")])
        hn_bufs = [buf(T + f"hn_{dc}") for dc in range(DC)]

        def qk_head(wsrc, dst, dname, pendq, nxt):
            for mh in range(4):
                wbf = [fbf0, fbf1][mh % 2]
                wbufs = pendq.pop(mh)
                if mh + 1 < 4:
                    pendq[mh + 1] = stream_weight(T + f"{dname}{(mh + 1) % 2}", wsrc,
                                                  mh + 1, DC, fraw,
                                                  [fbf0, fbf1][(mh + 1) % 2])
                elif nxt is not None:
                    pendq[0] = stream_weight(T + "k0", nxt, 0, DC, fraw, fbf0)
                ps2, pb2 = [(PS12, P12), (PS3, P3)][mh % 2]
                mm_acc(ps2, pb2, wbf, wbufs, DC,
                       lambda kc, o, cur: hn[:, kc * W + o:kc * W + o + cur], W, hn_bufs)
                k.op("scalar", lambda e, ps2=ps2: e.activation(
                        out=qraw[:, :], in_=ps2[:, 0:W], func=AF.Copy),
                     reads=[pb2, buf("e0"), buf(T + "rsc")], writes=[buf("e1")])
                k.op("vector", lambda e: e.tensor_mul(e0[0:64, :], qraw[0:64, :],
                                                      cfm[0:64, :]),
                     reads=[buf("e1"), buf("cfm"), buf("e0")], writes=[buf("e0")])
                k.op("gpsimd", lambda e: e.tensor_mul(rsc[0:64, :], qraw[64:128, :],
                                                      sfm[64:128, :]),
                     reads=[buf("e1"), buf("sfm"), buf(T + "rsc")], writes=[buf(T + "rsc")])
                k.op("vector", lambda e, mh=mh, dst=dst: e.tensor_sub(
                        dst[0:64, mh * W:(mh + 1) * W], e0[0:64, :], rsc[0:64, :]),
                     reads=[buf("e0"), buf(T + "rsc")],
                     writes=[buf(T + f"{dname}_{mh}")])
                k.op("gpsimd", lambda e: e.tensor_mul(e0[0:64, :], qraw[0:64, :],
                                                      sfm[0:64, :]),
                     reads=[buf("e1"), buf("sfm"), buf("e0")],
                     writes=[buf("e0")])
                k.op("vector", lambda e: e.tensor_mul(rsc[0:64, :], qraw[64:128, :],
                                                      cfm[64:128, :]),
                     reads=[buf("e1"), buf("cfm"), buf(T + "rsc")],
                     writes=[buf(T + "rsc")])
                k.op("vector", lambda e, mh=mh, dst=dst: e.tensor_add(
                        dst[64:128, mh * W:(mh + 1) * W], e0[0:64, :], rsc[0:64, :]),
                     reads=[buf("e0"), buf(T + "rsc")],
                     writes=[buf(T + f"{dname}_{mh}")])

        pendq = {0: stream_weight(T + "qr0", fwq, 0, DC, fraw, fbf0)}
        qk_head(fwq, qr, "qr", pendq, fwk)
        qk_head(fwk, kr, "kr", pendq, None)
        qr_bufs = [buf(T + f"qr_{mh}") for mh in range(4)]
        kr_bufs = [buf(T + f"kr_{mh}") for mh in range(4)]

        for hf in range(2):
            for j in range(4):
                ho = (j % 2) * 1024
                k.dma((2 * hf + j) % 4, lambda e, j=j, hf=hf, ho=ho: e.dma_start(
                        out=fraw[:, ho:ho + 1024], in_=fwv[hf, j]),
                      writes=fraw_bufs(fraw, ho, ho + 1024))
                k.op("scalar", lambda e, j=j, ho=ho: e.activation(
                        out=wvres[:, j * 1024:(j + 1) * 1024], in_=fraw[:, ho:ho + 1024],
                        func=AF.Copy),
                     reads=fraw_bufs(fraw, ho, ho + 1024), writes=[buf(T + "wvr")])
            for tc in range(8):
                ps2, pb2 = [(PS12, P12), (PS3, P3)][tc % 2]
                for kc in range(DC):
                    k.op("tensor", lambda e, tc=tc, kc=kc, ps2=ps2: e.matmul(
                            ps2[:, 1024:1280],
                            hn[:, kc * W + tc * 128:kc * W + (tc + 1) * 128],
                            wvres[:, kc * 256:(kc + 1) * 256],
                            start=(kc == 0), stop=(kc == DC - 1)),
                         reads=hn_bufs + [buf(T + "wvr")],
                         writes=[pb2] if kc in (0, DC - 1) else (), inc=kc == DC - 1)
                k.op("vector", lambda e, tc=tc, hf=hf, ps2=ps2: e.tensor_copy(
                        out=vtok[:, tc * 512 + hf * 256:tc * 512 + (hf + 1) * 256],
                        in_=ps2[:, 1024:1280]),
                     reads=[pb2], writes=[buf(T + f"vt_{tc}")])
        vt_bufs = [buf(T + f"vt_{tc}") for tc in range(8)]
        fence_wv = fence([buf(T + "wvr")])
        gWv.free()

        for qt in range(8):
            k.dma(2, lambda e, qt=qt: e.dma_start(
                    out=e0[:, :], in_=fmask[qt * 128:(qt + 1) * 128, :]),
                  reads=[buf("e0")], writes=[buf("e0")])
            for h in range(4):
                ps2, pb2 = [(PS12, P12), (PS3, P3)][h % 2]
                for o in (0, 512):
                    k.op("tensor", lambda e, qt=qt, h=h, o=o, ps2=ps2: e.matmul(
                            ps2[:, o:o + 512],
                            qr[:, h * W + qt * 128:h * W + (qt + 1) * 128],
                            kr[:, h * W + o:h * W + o + 512],
                            start=True, stop=True),
                         reads=[qr_bufs[h], kr_bufs[h]],
                         writes=[pb2], inc=o == 512)
                k.op("vector", lambda e, ps2=ps2: e.scalar_tensor_tensor(
                        out=e1[:, :], in0=ps2[:, 0:W], scalar=0.08838834764831845,
                        in1=e0[:, :], op0=ALU.mult, op1=ALU.add),
                     reads=[pb2, buf("e0")], writes=[buf("e1")])
                k.op("vector", lambda e: e.reduce_max(out=smx[:, 0:1], in_=e1[:, :],
                                                      axis=AX.X, negate=True),
                     reads=[buf("e1")], writes=[buf(T + "nmx")])
                k.op("scalar", lambda e: e.activation(
                        out=Pb[:, :], in_=e1[:, :], func=AF.Exp, bias=smx[:, 0:1],
                        accum_out=smx[:, 1:2]),
                     reads=[buf("e1"), buf(T + "nmx")], writes=[buf(T + "Pb")])
                k.op("vector", lambda e: e.reciprocal(smx[:, 1:2], smx[:, 1:2]),
                     reads=[buf(T + "Pb")], writes=[buf(T + "rsum")])
                k.op("scalar", lambda e: e.activation(
                        out=Pb[:, :], in_=Pb[:, :], func=AF.Copy, scale=smx[:, 1:2]),
                     reads=[buf(T + "Pb"), buf(T + "rsum")], writes=[buf(T + "Pb")])
                for c in range(8):
                    k.op("tensor", lambda e, c=c: e.transpose(
                            PSB[:, c * 128:(c + 1) * 128], Pb[:, c * 128:(c + 1) * 128],
                            identb[:, :]),
                         reads=[buf(T + "Pb"), buf("identb")],
                         writes=[PB] if c in (0, 7) else (), inc=c == 7)
                k.op("vector", lambda e: e.tensor_copy(out=PT[:, :], in_=PSB[:, 0:W]),
                     reads=[PB], writes=[buf(T + "PT")])
                for c in range(8):
                    k.op("tensor", lambda e, c=c, h=h, ps2=ps2: e.matmul(
                            ps2[:, 1024:1152],
                            vtok[:, c * 512 + h * 128:c * 512 + (h + 1) * 128],
                            PT[:, c * 128:(c + 1) * 128],
                            start=(c == 0), stop=(c == 7)),
                         reads=[vt_bufs[c], buf(T + "PT")],
                         writes=[pb2] if c in (0, 7) else (), inc=c == 7)
                k.op("vector", lambda e, h=h, qt=qt, ps2=ps2: e.tensor_copy(
                        out=obuf[:, h * W + qt * 128:h * W + (qt + 1) * 128],
                        in_=ps2[:, 1024:1152]),
                     reads=[pb2], writes=[buf(T + f"ob_{h}")])
        ob_bufs = [buf(T + f"ob_{h}") for h in range(4)]

        for mc in range(DC):
            k.dma(mc % 4, lambda e, mc=mc: e.dma_start(out=fraw[:, 0:512], in_=fwo[mc]),
                  reads=[buf(T + "wor")], writes=fraw_bufs(fraw, 0, 512))
            k.op("scalar", lambda e: e.activation(out=wobf[:, :], in_=fraw[:, 0:512],
                                                  func=AF.Copy),
                 reads=fraw_bufs(fraw, 0, 512), writes=[buf(T + "wor")])
            ps2, pb2 = [(PS12, P12), (PS3, P3)][mc % 2]
            for kc in range(4):
                for o in (0, 512):
                    k.op("tensor", lambda e, kc=kc, o=o, ps2=ps2: e.matmul(
                            ps2[:, o:o + 512], wobf[:, kc * 128:(kc + 1) * 128],
                            obuf[:, kc * W + o:kc * W + o + 512],
                            start=(kc == 0), stop=(kc == 3)),
                         reads=[buf(T + "wor")] + ob_bufs,
                         writes=[pb2] if (kc == 0 and o == 0) or (kc == 3 and o == 512)
                         else (), inc=(kc == 3 and o == 512))
            k.op("vector", lambda e, ps2=ps2: e.tensor_copy(out=atst[:, :], in_=ps2[:, 0:W]),
                 reads=[pb2], writes=[buf(T + "atst")])
            k.dma(6, lambda e, mc=mc: e.dma_start(
                    out=at_in[mc * 128:(mc + 1) * 128, :], in_=atst[:, :]),
                  reads=[buf(T + "atst")], writes=[buf("at_in")])
        k.cc(lambda e: e.collective_compute(
                "AllReduce", ALU.add, replica_groups=GROUPS4,
                ins=[at_in], outs=[at_out]),
             reads=[buf("at_in")], writes=[buf("at_out")])
        for dc in range(DC):
            k.dma(2, lambda e, dc=dc: e.dma_start(
                    out=atst[:, :], in_=at_out[dc * 128:(dc + 1) * 128, :]),
                  reads=[buf("at_out"), buf(T + "atst")], writes=[buf(T + "atst")])
            k.op("vector", lambda e, dc=dc: e.tensor_add(
                    filt[:, dc * W:(dc + 1) * W], filt[:, dc * W:(dc + 1) * W],
                    atst[:, :]),
                 reads=[filt_bufs[dc], buf(T + "atst")], writes=[filt_bufs[dc]])

        fence_fa = fence(hn_bufs + qr_bufs + kr_bufs + vt_bufs + ob_bufs +
                         [buf("e0"), buf("e1"), fence_wv] +
                         [buf(T + n) for n in ("PT", "Pb", "wor",
                                               "ftstB", "atst", "nmx",
                                               "rsum", "ms1", "rstb2")])
        gFA.free()

        gFB = Grp()
        h2 = gFB.sb(T + "h2", [128, DC * W], BF16)
        gbuf = gFB.sb(T + "g", [128, 16 * W], BF16)
        w2h = gFB.sb(T + "w2h", [128, 16 * 512], BF16)
        fraw2 = gFB.sb(T + "fraw2", [128, DC * 128])
        fbf20 = gFB.sb(T + "fbf20", [128, DC * 128], BF16)
        fbf21 = gFB.sb(T + "fbf21", [128, DC * 128], BF16)
        fbf22 = gFB.sb(T + "fbf22", [128, DC * 128], BF16)
        sq2 = gFB.sb(T + "sq2", [128, W], BF16)
        rstb3 = gFB.sb(T + "rstb3", [128, W])
        lrowb = gFB.sb(T + "lrowb", [1, W])
        tst2 = gFB.sb(T + "tst2", [128, 2048], BF16)

        for dc in range(DC):
            k.op("scalar", lambda e, dc=dc: e.activation(
                    out=sq2[:, :], in_=filt[:, dc * W:(dc + 1) * W], func=AF.Square),
                 reads=[filt_bufs[dc], fence_fa], writes=[buf(T + "sq2")])
            for o in (0, 512):
                k.op("tensor", lambda e, o=o, dc=dc: e.matmul(
                        PS3[0:1, o:o + 512], ones_b[:, :], sq2[:, o:o + 512],
                        start=(dc == 0), stop=(dc == DC - 1)),
                     reads=[buf(T + "sq2"), cONES_B], writes=[P3])
        k.op("vector", lambda e: e.tensor_scalar_mul(lrowb[:, :], PS3[0:1, 0:W], 1.0 / D),
             reads=[P3], writes=[buf(T + "ms2")])
        k.op("scalar", lambda e: e.activation(out=lrowb[:, :], in_=lrowb[:, :],
                                              func=AF.Sqrt, bias=cst[0:1, 1:2]),
             reads=[buf(T + "ms2"), buf("cst")], writes=[buf(T + "ms2")])
        k.op("vector", lambda e: e.reciprocal(lrowb[:, :], lrowb[:, :]),
             reads=[buf(T + "ms2")], writes=[buf(T + "ms2")])
        for o in (0, 512):
            k.op("tensor", lambda e, o=o: e.matmul(
                    PS12[:, o:o + 512], ones_row[:, :], lrowb[0:1, o:o + 512],
                    start=True, stop=True),
                 reads=[cONES_R, buf(T + "ms2")], writes=[P12])
        k.op("vector", lambda e: e.tensor_copy(out=rstb3[:, :], in_=PS12[:, 0:W]),
             reads=[P12], writes=[buf(T + "rstb3")])
        for dc in range(DC):
            k.op("vector", lambda e, dc=dc: e.scalar_tensor_tensor(
                    out=h2[:, dc * W:(dc + 1) * W], in0=filt[:, dc * W:(dc + 1) * W],
                    scalar=g2_t[:, dc:dc + 1], in1=rstb3[:, :],
                    op0=ALU.mult, op1=ALU.mult),
                 reads=[filt_bufs[dc], buf("g2_t"), buf(T + "rstb3"), fence_fa],
                 writes=[buf(T + f"h2_{dc}")])
        h2_bufs = [buf(T + f"h2_{dc}") for dc in range(DC)]

        fbrot = [fbf20, fbf21, fbf22]
        pend1 = {0: stream_weight(T + "w10", fw1, 0, DC, fraw2, fbrot[0], "scalar",
                                  extra=[fence_fa])}
        pend3 = {0: stream_weight(T + "w30", fw3, 0, DC, fraw2, fbrot[1], "scalar",
                                  extra=[fence_fa], soff=1)}
        for m in range(16):
            wb1 = fbrot[(2 * m) % 3]
            wb3 = fbrot[(2 * m + 1) % 3]
            wbufs1 = pend1.pop(m)
            wbufs3 = pend3.pop(m)
            mm_acc(PS12, P12, wb1, wbufs1, DC,
                   lambda kc, o, cur: h2[:, kc * W + o:kc * W + o + cur], W, h2_bufs)
            if m + 1 < 16:
                pend1[m + 1] = stream_weight(T + f"w1{(m + 1) % 2}", fw1, m + 1, DC,
                                             fraw2, fbrot[(2 * m + 2) % 3], "scalar")
                pend3[m + 1] = stream_weight(T + f"w3{(m + 1) % 2}", fw3, m + 1, DC,
                                             fraw2, fbrot[(2 * m + 3) % 3], "scalar",
                                             soff=1)
            mm_acc(PS3, P3, wb3, wbufs3, DC,
                   lambda kc, o, cur: h2[:, kc * W + o:kc * W + o + cur], W, h2_bufs)
            k.op("scalar", lambda e: e.activation(out=e0[:, :], in_=PS12[:, 0:W],
                                                  func=AF.Silu),
                 reads=[P12], writes=[buf("e0")])
            k.op("vector", lambda e, m=m: e.tensor_mul(
                    gbuf[:, m * W:(m + 1) * W], e0[:, :], PS3[:, 0:W]),
                 reads=[buf("e0"), P3], writes=[buf(T + f"g_{m}")])
        g_bufs = [buf(T + f"g_{m}") for m in range(16)]

        for quar in range(4):
            for j in range(8):
                ho = (j % 2) * 1024
                k.dma(j % 4, lambda e, j=j, quar=quar, ho=ho: e.dma_start(
                        out=fraw2[:, ho:ho + 1024], in_=fw2[quar, j]),
                      writes=fraw_bufs(fraw2, ho, ho + 1024))
                k.op("scalar", lambda e, j=j, ho=ho: e.activation(
                        out=w2h[:, j * 1024:(j + 1) * 1024], in_=fraw2[:, ho:ho + 1024],
                        func=AF.Copy),
                     reads=fraw_bufs(fraw2, ho, ho + 1024), writes=[buf(T + "w2c")])
            for tc in range(8):
                ps2, pb2 = [(PS12, P12), (PS3, P3)][tc % 2]
                for kc in range(16):
                    k.op("tensor", lambda e, tc=tc, kc=kc, ps2=ps2: e.matmul(
                            ps2[:, 0:512],
                            gbuf[:, kc * W + tc * 128:kc * W + (tc + 1) * 128],
                            w2h[:, kc * 512:(kc + 1) * 512],
                            start=(kc == 0), stop=(kc == 15)),
                         reads=g_bufs + [buf(T + "w2c")],
                         writes=[pb2] if kc in (0, 15) else (),
                         inc=kc == 15)
                k.op("vector", lambda e, ps2=ps2: e.tensor_copy(
                        out=tst2[:, 0:512], in_=ps2[:, 0:512]),
                     reads=[pb2, buf(T + "tst2")], writes=[buf(T + "tst2")])
                k.dma(6, lambda e, tc=tc, quar=quar: e.dma_start(
                        out=fo_in[tc * 128:(tc + 1) * 128, quar * 512:(quar + 1) * 512],
                        in_=tst2[:, 0:512]),
                      reads=[buf(T + "tst2")], writes=[buf("fo_in")])
        k.cc(lambda e: e.collective_compute(
                "AllReduce", ALU.add, replica_groups=GROUPS4,
                ins=[fo_in], outs=[fo_out]),
             reads=[buf("fo_in")], writes=[buf("fo_out")])

        for ch in range(8):
            k.dma(2, lambda e, ch=ch: e.dma_start(
                    out=tst2[:, :], in_=fo_out[ch * 128:(ch + 1) * 128, :]),
                  reads=[buf("fo_out"), buf(T + "tst2")], writes=[buf(T + "tst2")])
            for dc in range(DC):
                col = (dc % 8) * 128
                k.op("tensor", lambda e, dc=dc, ch=ch, col=col: e.transpose(
                        PSB[:, col:col + 128],
                        filt[:, dc * W + ch * 128:dc * W + (ch + 1) * 128], identb[:, :]),
                     reads=filt_bufs + [buf("identb")],
                     writes=[PB] if dc in (0, 7, 8, 15) else (), inc=dc in (7, 15))
                if dc in (7, 15):
                    for d2 in range(dc - 7, dc + 1):
                        k.op("vector", lambda e, d2=d2: e.tensor_add(
                                tst2[:, d2 * 128:(d2 + 1) * 128],
                                tst2[:, d2 * 128:(d2 + 1) * 128],
                                PSB[:, (d2 % 8) * 128:(d2 % 8 + 1) * 128]),
                             reads=[PB, buf(T + "tst2")], writes=[buf(T + "tst2")])
            k.dma(6, lambda e, ch=ch: e.dma_start(
                    out=xo_dram[ch * 128:(ch + 1) * 128, :], in_=tst2[:, :]),
                  reads=[buf(T + "tst2")], writes=[buf("xo_dram")])

        fence_fb = fence(h2_bufs + g_bufs + filt_bufs +
                         [buf(T + n) for n in ("tst2", "w2c", "sq2", "rstb3", "ms2")])
        gFB.free()
        gFil.free()

        gXn = Grp("right")
        xn = gXn.sb(T + "xn", [128, DC * W])
        gFe = Grp()
        gst = gFe.sb(T + "gst", [128, 2048], BF16)
        gsb2 = gFe.sb(T + "gsb2", [128, 2048], BF16)
        rwt = gFe.sb(T + "rwt", [128, 8])
        idxt2 = gFe.sb(T + "idxt2", [128, 8], I32)

        for dc in range(DC):
            k.dma(dc % 4 + 8 * (dc // 4 % 2), lambda e, dc=dc: e.dma_start(
                    out=xn[:, dc * W:(dc + 1) * W], in_=xst_dram[:, dc * W:(dc + 1) * W]),
                  reads=[buf("xst"), fence_fb], writes=[buf(T + f"xn_{dc}")])
        xn_bufs = [buf(T + f"xn_{dc}") for dc in range(DC)]
        for ch in range(8):
            k.dma(6, lambda e, ch=ch: e.dma_start(
                    out=idxt2[:, ch:ch + 1],
                    in_=idx_dram[0:1, ch * 128:(ch + 1) * 128].rearrange(
                        "a (p c) -> (a p) c", p=128)),
                  reads=[buf("idx_dram"), fence_fb], writes=[buf(T + "idxt2")])
            k.dma(6, lambda e, ch=ch: e.dma_start(
                    out=rwt[:, ch:ch + 1],
                    in_=rw_dram[0:1, ch * 128:(ch + 1) * 128].rearrange(
                        "a (p c) -> (a p) c", p=128)),
                  reads=[buf("rw_dram"), fence_fb], writes=[buf(T + "rwt")])
        for ch in range(8):
            k.op("vector", lambda e: e.memset(gst[:, :], 0.0),
                 reads=[fence_fb, buf(T + "gstB")], writes=[buf(T + "gstB")])
            k.dma(5, lambda e, ch=ch: e.indirect_dma_start(
                    out=gst[:, :], out_offset=None,
                    in_=xo_dram, in_offset=IndirectOffsetOnAxis(idxt2[:, ch:ch + 1], 0),
                    bounds_check=1023, oob_is_err=False),
                  reads=[buf("xo_dram"), buf(T + "idxt2"), buf(T + "gstB")],
                  writes=[buf(T + "gstB")])
            k.op("vector", lambda e, ch=ch: e.tensor_scalar(
                    out=gsb2[:, :], in0=gst[:, :], scalar1=rwt[:, ch:ch + 1],
                    scalar2=None, op0=ALU.mult),
                 reads=[buf(T + "gstB"), buf(T + "rwt")], writes=[buf(T + "gsb2")])
            for dc in range(DC):
                col = (dc % 8) * 128
                k.op("tensor", lambda e, dc=dc, col=col: e.transpose(
                        PSB[:, col:col + 128], gsb2[:, dc * 128:(dc + 1) * 128],
                        identb[:, :]),
                     reads=[buf(T + "gsb2"), buf("identb")],
                     writes=[PB] if dc in (0, 7, 8, 15) else (), inc=dc in (7, 15))
                if dc in (7, 15):
                    for d2 in range(dc - 7, dc + 1):
                        k.op("vector", lambda e, d2=d2, ch=ch: e.tensor_add(
                                xn[:, d2 * W + ch * 128:d2 * W + (ch + 1) * 128],
                                xn[:, d2 * W + ch * 128:d2 * W + (ch + 1) * 128],
                                PSB[:, (d2 % 8) * 128:(d2 % 8 + 1) * 128]),
                             reads=[PB, buf(T + f"xn_{d2}")], writes=[buf(T + f"xn_{d2}")])
        fence_fe = fence([buf(T + n) for n in ("gstB", "gsb2", "rwt", "idxt2")])
        gFe.free()
        return xn, xn_bufs, fence_fe, gXn


    def emit_attn_b(tag, xt, x_bufs, fence_in, free_x):
        T = tag
        ISQ = 0.08838834764831845
        for dc in range(DC):
            k.dma(dc % 2, lambda e, dc=dc: e.dma_start(
                    out=xst_dram[:, dc * W:(dc + 1) * W], in_=xt[:, dc * W:(dc + 1) * W]),
                  reads=[x_bufs[dc], fence_in], writes=[buf("xst")])
        fence_x = fence(list(x_bufs))
        free_x()

        gXp = Grp("right")
        xpb2 = gXp.sb(T + "xpb2", [128, DC * W], BF16)
        gLN = Grp()
        xs0 = gLN.sb(T + "xs0", [128, W])
        xs1 = gLN.sb(T + "xs1", [128, W])
        sqb = gLN.sb(T + "sqb", [128, W], BF16)
        statb = gLN.sb(T + "statb", [1, 3 * W])

        s0 = statb[0:1, 0:W]; s1 = statb[0:1, W:2 * W]; s2 = statb[0:1, 2 * W:3 * W]
        for dc in range(DC):
            xs = [xs0, xs1][dc % 2]
            k.dma((dc % 4) * 2, lambda e, dc=dc, xs=xs: e.dma_start(
                    out=xs[:, :], in_=xst_dram[:, dc * W:(dc + 1) * W]),
                  reads=[buf("xst"), fence_x], writes=[buf(T + f"xs_{dc % 2}")])
            k.op("vector", lambda e, dc=dc, xs=xs: e.tensor_copy(
                    out=xpb2[:, dc * W:(dc + 1) * W], in_=xs[:, :]),
                 reads=[buf(T + f"xs_{dc % 2}"), fence_x], writes=[buf(T + f"xb_{dc}")])
            k.op("scalar", lambda e, dc=dc: e.activation(
                    out=sqb[:, :], in_=xpb2[:, dc * W:(dc + 1) * W], func=AF.Square),
                 reads=[buf(T + f"xb_{dc}")], writes=[buf(T + "sq")])
            for o in (0, 512):
                k.op("tensor", lambda e, dc=dc, o=o: e.matmul(
                        PS12[0:1, o:o + 512], ones_b[:, :],
                        xpb2[:, dc * W + o:dc * W + o + 512],
                        start=(dc == 0), stop=(dc == DC - 1)),
                     reads=[buf(T + f"xb_{dc}"), cONES_B], writes=[P12])
                k.op("tensor", lambda e, o=o, dc=dc: e.matmul(
                        PS3[0:1, o:o + 512], ones_b[:, :], sqb[:, o:o + 512],
                        start=(dc == 0), stop=(dc == DC - 1)),
                     reads=[buf(T + "sq"), cONES_B], writes=[P3])
        k.op("vector", lambda e: e.tensor_scalar_mul(s0, PS12[0:1, 0:W], 1.0 / D),
             reads=[P12], writes=[buf(T + "st0")])
        k.op("vector", lambda e: e.tensor_scalar_mul(s1, PS3[0:1, 0:W], 1.0 / D),
             reads=[P3], writes=[buf(T + "st1")])
        k.op("vector", lambda e: e.tensor_mul(s2, s0, s0),
             reads=[buf(T + "st0")], writes=[buf(T + "st2")])
        k.op("vector", lambda e: e.tensor_sub(s1, s1, s2),
             reads=[buf(T + "st1"), buf(T + "st2")], writes=[buf(T + "st1")])
        k.op("scalar", lambda e: e.activation(out=s1, in_=s1, func=AF.Sqrt,
                                              bias=cst[0:1, 0:1]),
             reads=[buf(T + "st1"), buf("cst")], writes=[buf(T + "st1")])
        k.op("vector", lambda e: e.reciprocal(s1, s1),
             reads=[buf(T + "st1")], writes=[buf(T + "st1")])
        k.op("vector", lambda e: e.tensor_mul(s0, s0, s1),
             reads=[buf(T + "st0"), buf(T + "st1")], writes=[buf(T + "st0")])
        k.op("vector", lambda e: e.tensor_scalar_mul(s0, s0, -1.0),
             reads=[buf(T + "st0")], writes=[buf(T + "st0")])
        for o in (0, 512):
            k.op("tensor", lambda e, o=o: e.matmul(
                    PS12[:, o:o + 512], ones_row[:, :], s1[0:1, o:o + 512],
                    start=True, stop=True),
                 reads=[cONES_R, buf(T + "st1")], writes=[P12])
            k.op("tensor", lambda e, o=o: e.matmul(
                    PS3[:, o:o + 512], ones_row[:, :], s0[0:1, o:o + 512],
                    start=True, stop=True),
                 reads=[cONES_R, buf(T + "st0")], writes=[P3])
        k.op("vector", lambda e: e.tensor_copy(out=e0[:, :], in_=PS12[:, 0:W]),
             reads=[P12, buf("e0")], writes=[buf("e0")])
        k.op("vector", lambda e: e.tensor_copy(out=e1[:, :], in_=PS3[:, 0:W]),
             reads=[P3, buf("e1")], writes=[buf("e1")])
        for dc in range(DC):
            xs = [xs0, xs1][dc % 2]
            k.dma((dc % 4) * 2, lambda e, dc=dc, xs=xs: e.dma_start(
                    out=xs[:, :], in_=xst_dram[:, dc * W:(dc + 1) * W]),
                  reads=[buf("xst"), buf(T + f"xs_{dc % 2}")],
                  writes=[buf(T + f"xs_{dc % 2}")])
            eng = "vector" if dc % 2 == 0 else "gpsimd"
            k.op(eng, lambda e, dc=dc, xs=xs: e.tensor_mul(
                    xpb2[:, dc * W:(dc + 1) * W], xs[:, :], e0[:, :]),
                 reads=[buf(T + f"xs_{dc % 2}"), buf("e0"), buf(T + f"xb_{dc}")],
                 writes=[buf(T + f"xb_{dc}")])
            k.op(eng, lambda e, dc=dc: e.tensor_add(
                    xpb2[:, dc * W:(dc + 1) * W], xpb2[:, dc * W:(dc + 1) * W], e1[:, :]),
                 reads=[buf(T + f"xb_{dc}"), buf("e1")], writes=[buf(T + f"xb_{dc}")])
        xb_bufs = [buf(T + f"xb_{dc}") for dc in range(DC)]
        fence_ln = fence([buf(T + "xs_0"), buf(T + "xs_1"), buf(T + "sq"),
                          buf(T + "st0"), buf(T + "st1"), buf(T + "st2"),
                          buf("e0"), buf("e1")])
        gLN.free()

        gBo = Grp()
        o2 = gBo.sb(T + "o2", [128, DC * W], BF16)
        braw = gBo.sb(T + "braw", [128, DC * 128])
        bbf0 = gBo.sb(T + "bbf0", [128, DC * 128], BF16)
        bbf1 = gBo.sb(T + "bbf1", [128, DC * 128], BF16)
        tstb = gBo.sb(T + "tstb", [128, W], BF16)
        gBq = Grp()
        qfm = gBq.sb(T + "qfm", [128, DC * W], BF16)
        gB1 = Grp()
        kfm = gB1.sb(T + "kfm", [128, 4 * W], BF16)
        wvres2 = gB1.sb(T + "wvres2", [128, 16 * 512], BF16)

        pendb = {0: stream_weight(T + "k0", bwk, 0, DC, braw, bbf0,
                                  extra=[fence_ln])}
        for mc in range(4):
            wbf = [bbf0, bbf1][mc % 2]
            wbufs = pendb.pop(mc)
            if mc + 1 < 4:
                pendb[mc + 1] = stream_weight(T + f"k{(mc + 1) % 2}", bwk, mc + 1, DC,
                                              braw, [bbf0, bbf1][(mc + 1) % 2])
            else:
                pendb[0] = stream_weight(T + "q0", bwq, 0, DC, braw, bbf0)
            ps2, pb2 = [(PS12, P12), (PS3, P3)][mc % 2]
            mm_acc(ps2, pb2, wbf, wbufs, DC,
                   lambda kc, o, cur: xpb2[:, kc * W + o:kc * W + o + cur], W, xb_bufs)
            k.op("vector", lambda e, mc=mc, ps2=ps2: e.tensor_copy(
                    out=kfm[:, mc * W:(mc + 1) * W], in_=ps2[:, 0:W]),
                 reads=[pb2], writes=[buf(T + "kf")])
        for tc in range(8):
            for c in range(4):
                k.op("tensor", lambda e, tc=tc, c=c: e.transpose(
                        PSB[:, c * 128:(c + 1) * 128],
                        kfm[:, c * W + tc * 128:c * W + (tc + 1) * 128], identb[:, :]),
                     reads=[buf(T + "kf"), buf("identb")],
                     writes=[PB] if c in (0, 3) else (), inc=c == 3)
            k.op("vector", lambda e, tc=tc: e.tensor_copy(out=tstb[:, 0:512],
                                                          in_=PSB[:, 0:512]),
                 reads=[PB, buf(T + "tstbB")], writes=[buf(T + "tstbB")])
            k.dma(6, lambda e, tc=tc: e.dma_start(
                    out=kvag_in[tc * 128:(tc + 1) * 128, 0:512], in_=tstb[:, 0:512]),
                  reads=[buf(T + "tstbB")], writes=[buf("kvag_in")])
        for j in range(8):
            ho = (j % 2) * 1024
            k.dma(j % 4, lambda e, j=j, ho=ho: e.dma_start(
                    out=braw[:, ho:ho + 1024], in_=bwv[j]),
                  writes=fraw_bufs(braw, ho, ho + 1024))
            k.op("scalar", lambda e, j=j, ho=ho: e.activation(
                    out=wvres2[:, j * 1024:(j + 1) * 1024], in_=braw[:, ho:ho + 1024],
                    func=AF.Copy),
                 reads=fraw_bufs(braw, ho, ho + 1024), writes=[buf(T + "wv2")])
        for tc in range(8):
            ps2, pb2 = [(PS12, P12), (PS3, P3)][tc % 2]
            for kc in range(DC):
                k.op("tensor", lambda e, tc=tc, kc=kc, ps2=ps2: e.matmul(
                        ps2[:, 1024:1536],
                        xpb2[:, kc * W + tc * 128:kc * W + (tc + 1) * 128],
                        wvres2[:, kc * 512:(kc + 1) * 512],
                        start=(kc == 0), stop=(kc == DC - 1)),
                     reads=xb_bufs + [buf(T + "wv2")],
                     writes=[pb2] if kc in (0, DC - 1) else (), inc=kc == DC - 1)
            k.op("vector", lambda e, tc=tc, ps2=ps2: e.tensor_copy(
                    out=tstb[:, 0:512], in_=ps2[:, 1024:1536]),
                 reads=[pb2, buf(T + "tstbB")], writes=[buf(T + "tstbB")])
            k.dma(6, lambda e, tc=tc: e.dma_start(
                    out=kvag_in[tc * 128:(tc + 1) * 128, 512:1024], in_=tstb[:, 0:512]),
                  reads=[buf(T + "tstbB")], writes=[buf("kvag_in")])
        k.cc(lambda e: e.collective_compute(
                "AllGather", ALU.bypass, replica_groups=GROUPS4,
                ins=[kvag_in], outs=[kvag_out]),
             reads=[buf("kvag_in")], writes=[buf("kvag_out")])
        for mc in range(DC):
            wbf = [bbf0, bbf1][mc % 2]
            wbufs = pendb.pop(mc)
            if mc + 1 < DC:
                pendb[mc + 1] = stream_weight(T + f"q{(mc + 1) % 2}", bwq, mc + 1, DC,
                                              braw, [bbf0, bbf1][(mc + 1) % 2])
            ps2, pb2 = [(PS12, P12), (PS3, P3)][mc % 2]
            mm_acc(ps2, pb2, wbf, wbufs, DC,
                   lambda kc, o, cur: xpb2[:, kc * W + o:kc * W + o + cur], W, xb_bufs)
            k.op("vector", lambda e, mc=mc, ps2=ps2: e.tensor_copy(
                    out=qfm[:, mc * W:(mc + 1) * W], in_=ps2[:, 0:W]),
                 reads=[pb2], writes=[buf(T + f"q_{mc}")])
        q_bufs = [buf(T + f"q_{mc}") for mc in range(DC)]
        fence_b1 = fence(xb_bufs + [buf(T + n) for n in ("kf", "wv2", "tstbB")])
        gB1.free()
        gXp.free()

        gBA = Grp()
        kfmw = gBA.sb(T + "kfmw", [128, 4 * 2048], BF16)
        vwin = gBA.sb(T + "vwin", [128, 16 * 512], BF16)
        kvix = gBA.sb(T + "kvix", [128, 16], I32)
        gtst = gBA.sb(T + "gtst", [128, 1024], BF16)
        PT2a = gBA.sb(T + "PT2a", [128, 1152], BF16)
        PT2b = gBA.sb(T + "PT2b", [128, 1152], BF16)
        Pb2a = gBA.sb(T + "Pb2a", [128, 1152], BF16)
        Pb2b = gBA.sb(T + "Pb2b", [128, 1152], BF16)
        bmq = gBA.sb(T + "bmq", [128, 1152], BF16)
        bmq2 = gBA.sb(T + "bmq2", [128, 1152], BF16)
        smx2 = gBA.sb(T + "smx2", [128, 4])

        k.dma(6, lambda e: e.dma_start(out=kvix[:, :], in_=kvidx_i),
              reads=[fence_b1], writes=[buf(T + "kvix")])
        for wc in list(range(8, 16)) + list(range(8)):
            k.op("vector", lambda e: e.memset(gtst[:, :], 0.0),
                 reads=[fence_b1, buf(T + "gtstB")], writes=[buf(T + "gtstB")])
            if wc < 8:
                k.dma(5, lambda e, wc=wc: e.indirect_dma_start(
                        out=gtst[:, :], out_offset=None,
                        in_=kvag_out, in_offset=IndirectOffsetOnAxis(kvix[:, wc:wc + 1], 0),
                        bounds_check=4095, oob_is_err=False),
                      reads=[buf("kvag_out"), buf(T + "kvix"), buf(T + "gtstB")],
                      writes=[buf(T + "gtstB")])
            else:
                k.dma(5, lambda e, wc=wc: e.dma_start(
                        out=gtst[:, :],
                        in_=kvag_in[(wc - 8) * 128:(wc - 7) * 128, :]),
                      reads=[buf("kvag_in"), buf(T + "gtstB")],
                      writes=[buf(T + "gtstB")])
            k.op("vector", lambda e, wc=wc: e.tensor_copy(
                    out=vwin[:, wc * 512:(wc + 1) * 512], in_=gtst[:, 512:1024]),
                 reads=[buf(T + "gtstB")], writes=[buf(T + "vwin")])
            for c in range(4):
                k.op("tensor", lambda e, c=c: e.transpose(
                        PSB[:, c * 128:(c + 1) * 128], gtst[:, c * 128:(c + 1) * 128],
                        identb[:, :]),
                     reads=[buf(T + "gtstB"), buf("identb")],
                     writes=[PB] if c in (0, 3) else (), inc=c == 3)
            for c in range(4):
                k.op("vector", lambda e, c=c, wc=wc: e.tensor_copy(
                        out=kfmw[:, c * 2048 + wc * 128:c * 2048 + (wc + 1) * 128],
                        in_=PSB[:, c * 128:(c + 1) * 128]),
                     reads=[PB], writes=[buf(T + "kfmw")])
        bmqs = [bmq, bmq2]
        heads = [(qt, h) for qt in range(8) for h in range(16)]
        NH = len(heads)

        def b_qk(i):
            qt, h = heads[i]
            if h == 0:
                mb = bmqs[qt % 2]
                k.dma(2, lambda e, qt=qt, mb=mb: e.dma_start(
                        out=mb[:, :], in_=bmask_i[qt * 128:(qt + 1) * 128, :]),
                      reads=[buf(T + f"bmqB{qt % 2}")],
                      writes=[buf(T + f"bmqB{qt % 2}")])
            kvh = h // 4
            ps2, pb2 = [(PS12, P12), (PS3, P3)][i % 2]
            for (o, cur) in ((0, 512), (512, 512), (1024, 128)):
                k.op("tensor", lambda e, qt=qt, h=h, o=o, cur=cur, ps2=ps2,
                     kvh=kvh: e.matmul(
                        ps2[:, o:o + cur],
                        qfm[:, h * W + qt * 128:h * W + (qt + 1) * 128],
                        kfmw[:, kvh * 2048 + qt * 128 + o:kvh * 2048 + qt * 128 + o + cur],
                        start=True, stop=True),
                     reads=[q_bufs[h], buf(T + "kfmw")],
                     writes=[pb2], inc=o == 1024)

        def b_chain(i):
            qt, h = heads[i]
            ps2, pb2 = [(PS12, P12), (PS3, P3)][i % 2]
            Pb2 = [Pb2a, Pb2b][i % 2]
            BPB = buf(T + "Pb2" + "ab"[i % 2])
            mxc = 2 * (i % 2)
            mb = bmqs[qt % 2]
            k.op("vector", lambda e, ps2=ps2, mxc=mxc: e.reduce_max(
                    out=smx2[:, mxc:mxc + 1], in_=ps2[:, 0:1152], axis=AX.X,
                    negate=True),
                 reads=[pb2], writes=[buf(T + f"nmx{i % 2}")])
            k.op("vector", lambda e, mxc=mxc: e.tensor_scalar_mul(
                    smx2[:, mxc:mxc + 1], smx2[:, mxc:mxc + 1], ISQ),
                 reads=[buf(T + f"nmx{i % 2}")], writes=[buf(T + f"nmx{i % 2}")])
            k.op("scalar", lambda e, ps2=ps2, Pb2=Pb2, mxc=mxc: e.activation(
                    out=Pb2[:, :], in_=ps2[:, 0:1152], func=AF.Exp, scale=ISQ,
                    bias=smx2[:, mxc:mxc + 1]),
                 reads=[pb2, buf(T + f"nmx{i % 2}"), BPB], writes=[BPB])
            k.op("vector", lambda e, Pb2=Pb2, mxc=mxc, mb=mb: e.scalar_tensor_tensor(
                    out=Pb2[:, :], in0=Pb2[:, :], scalar=1.0, in1=mb[:, :],
                    op0=ALU.mult, op1=ALU.mult,
                    accum_out=smx2[:, mxc + 1:mxc + 2]),
                 reads=[BPB, buf(T + f"bmqB{qt % 2}")], writes=[BPB])
            k.op("vector", lambda e, mxc=mxc: e.reciprocal(
                    smx2[:, mxc + 1:mxc + 2], smx2[:, mxc + 1:mxc + 2]),
                 reads=[BPB], writes=[buf(T + f"rs{i % 2}")])
            k.op("scalar", lambda e, Pb2=Pb2, mxc=mxc: e.activation(
                    out=Pb2[:, :], in_=Pb2[:, :], func=AF.Copy,
                    scale=smx2[:, mxc + 1:mxc + 2]),
                 reads=[BPB, buf(T + f"rs{i % 2}")], writes=[BPB])

        def b_tail(i):
            qt, h = heads[i]
            kvh = h // 4
            ps2, pb2 = [(PS12, P12), (PS3, P3)][i % 2]
            ooff = 1536 if i % 2 == 0 else 1280
            PT2 = [PT2a, PT2b][i % 2]
            Pb2 = [Pb2a, Pb2b][i % 2]
            BPT = buf(T + "PT2" + "ab"[i % 2])
            BPB = buf(T + "Pb2" + "ab"[i % 2])
            for c in range(8):
                k.op("tensor", lambda e, c=c, Pb2=Pb2: e.transpose(
                        PSB[:, c * 128:(c + 1) * 128], Pb2[:, c * 128:(c + 1) * 128],
                        identb[:, :]),
                     reads=[BPB, buf("identb")],
                     writes=[PB] if c in (0, 7) else (), inc=c == 7)
            k.op("vector", lambda e, PT2=PT2: e.tensor_copy(out=PT2[:, 0:1024],
                                                            in_=PSB[:, 0:1024]),
                 reads=[PB], writes=[BPT])
            k.op("tensor", lambda e, Pb2=Pb2: e.transpose(
                    PSB[:, 0:128], Pb2[:, 1024:1152], identb[:, :]),
                 reads=[BPB, buf("identb"), PB], writes=[PB])
            k.op("vector", lambda e, PT2=PT2: e.tensor_copy(out=PT2[:, 1024:1152],
                                                            in_=PSB[:, 0:128]),
                 reads=[PB], writes=[BPT])
            for c in range(9):
                k.op("tensor", lambda e, c=c, qt=qt, kvh=kvh, ps2=ps2, PT2=PT2,
                     ooff=ooff: e.matmul(
                        ps2[:, ooff:ooff + 128],
                        vwin[:, (qt + c) * 512 + kvh * 128:(qt + c) * 512 + (kvh + 1) * 128],
                        PT2[:, c * 128:(c + 1) * 128],
                        start=(c == 0), stop=(c == 8)),
                     reads=[buf(T + "vwin"), BPT],
                     writes=[pb2] if c in (0, 8) else (), inc=c == 8)
            k.op("vector", lambda e, h=h, qt=qt, ps2=ps2, ooff=ooff: e.tensor_copy(
                    out=o2[:, h * W + qt * 128:h * W + (qt + 1) * 128],
                    in_=ps2[:, ooff:ooff + 128]),
                 reads=[pb2], writes=[buf(T + f"o2_{h}")])

        b_qk(0)
        for i in range(NH):
            if i + 1 < NH:
                b_qk(i + 1)
            b_chain(i)
            if i >= 1:
                b_tail(i - 1)
        b_tail(NH - 1)
        o2_bufs = [buf(T + f"o2_{h}") for h in range(16)]
        fence_ba = fence(q_bufs + [buf(T + n) for n in
                         ("kfmw", "vwin", "kvix", "gtstB", "PT2a", "PT2b", "Pb2a",
                          "Pb2b", "bmqB0", "bmqB1", "nmx0", "nmx1", "rs0", "rs1")])
        gBA.free()
        gBq.free()

        gXb = Grp("right")
        xn = gXb.sb(T + "xn", [128, DC * W])
        for dc in range(DC):
            k.dma(dc % 4, lambda e, dc=dc: e.dma_start(
                    out=xn[:, dc * W:(dc + 1) * W], in_=xst_dram[:, dc * W:(dc + 1) * W]),
                  reads=[buf("xst"), fence_ba], writes=[buf(T + f"xn_{dc}")])
        xn_bufs = [buf(T + f"xn_{dc}") for dc in range(DC)]
        pendo = {0: stream_weight(T + "wo0", bwo, 0, DC, braw, bbf0,
                                  extra=[fence_ba])}
        for mc in range(DC):
            wbf = [bbf0, bbf1][mc % 2]
            wbufs = pendo.pop(mc)
            if mc + 1 < DC:
                pendo[mc + 1] = stream_weight(T + f"wo{(mc + 1) % 2}", bwo, mc + 1, DC,
                                              braw, [bbf0, bbf1][(mc + 1) % 2])
            ps2, pb2 = [(PS12, P12), (PS3, P3)][mc % 2]
            mm_acc(ps2, pb2, wbf, wbufs, DC,
                   lambda kc, o, cur: o2[:, kc * W + o:kc * W + o + cur], W, o2_bufs)
            k.op("vector", lambda e, mc=mc, ps2=ps2: e.tensor_add(
                    xn[:, mc * W:(mc + 1) * W], xn[:, mc * W:(mc + 1) * W], ps2[:, 0:W]),
                 reads=[pb2, buf(T + f"xn_{mc}")], writes=[buf(T + f"xn_{mc}")])
        fence_bo = fence(o2_bufs + [buf(n) for n in list(B) if n.startswith(("raw_", "bf_"))])
        gBo.free()
        return xn, xn_bufs, fence_bo, gXb

    XT, XB, XF, XG = x, x_bufs, fence_a5, g_a5
    if "ffn1" in STAGES:
        _g = XG
        XT, XB, XF, XG = emit_ffn("f1_", XT, XB, XF, lambda g=_g: g.free())
    if "attn_b" in STAGES:
        _g = XG
        XT, XB, XF, XG = emit_attn_b("b_", XT, XB, XF, lambda g=_g: g.free())
    if "ffn2" in STAGES:
        _g = XG
        XT, XB, XF, XG = emit_ffn("f2_", XT, XB, XF, lambda g=_g: g.free())

    g_out = Grp()
    outb = g_out.sb("outb", [128, D])
    for tc in range(8):
        for dc in range(DC):
            ps = PS3 if dc < 8 else PS12
            psb = P3 if dc < 8 else P12
            col = (dc % 8) * 128
            last = dc in (7, 15)
            first = dc in (0, 8)
            k.op("tensor", lambda e, tc=tc, dc=dc, ps=ps, col=col: e.transpose(
                    ps[:, col:col + 128],
                    XT[:, dc * W + tc * 128:dc * W + (tc + 1) * 128], identf[:, :]),
                 reads=XB + [buf("identf")],
                 writes=[psb] if (last or first) else (), inc=last)
        k.op("vector", lambda e: e.tensor_copy(out=outb[:, 0:1024], in_=PS3[:, 0:1024]),
             reads=[P3], writes=[buf("outb")])
        k.op("vector", lambda e: e.tensor_copy(out=outb[:, 1024:2048], in_=PS12[:, 0:1024]),
             reads=[P12], writes=[buf("outb")])
        k.dma(6, lambda e, tc=tc: e.dma_start(
                out=out_t[tc * 128:(tc + 1) * 128, :], in_=outb[:, :]),
              reads=[buf("outb")], writes=[buf("out")])

    k.finish("sync")
    k.finish("gpsimd")

    assert peak[0] < 214 * 1024, f"SBUF peak {peak[0]} bytes/partition"

    g_out.free(); XG.free(); gFper.free(); g_persist.free()
    with nc.Block() as block:
        k.build(block)
    return nc


def pack_w(Wm, KC, MC):
    Kd, Md = Wm.shape
    assert Kd == KC * 128 and Md == MC * 128
    return np.ascontiguousarray(
        Wm.reshape(KC, 128, MC, 128).transpose(2, 1, 0, 3).reshape(MC, 128, KC * 128))


def vec_t(v, MC):
    return np.ascontiguousarray(v.reshape(MC, 128).T)


_prog_cache = {}


def build_inmaps(ii):
    x_in = ii["L0_attn_a_x"].astype(np.float32)

    shared = {
        "wl": pack_w(ii["L0_attn_a_Wl"], DC, HC),
        "wr": pack_w(ii["L0_attn_a_Wr"], DC, HC),
        "wa": pack_w(ii["L0_attn_a_Wa"], HC, HC),
        "wi": pack_w(ii["L0_attn_a_Wi"], HC, HC),
        "wo": pack_w(ii["L0_attn_a_Wo"], HC, DC),
        "bl": vec_t(ii["L0_attn_a_bl"], HC),
        "br": vec_t(ii["L0_attn_a_br"], HC),
        "ba": vec_t(ii["L0_attn_a_ba"], HC),
        "bi": vec_t(ii["L0_attn_a_bi"], HC),
        "bo": vec_t(ii["L0_attn_a_bo"], DC),
        "cb": vec_t(ii["L0_attn_a_conv_b"], HC),
        "lam": vec_t(ii["L0_attn_a_lam"], HC),
        "idf": np.eye(128, dtype=np.float32),
        "idb": np.eye(128, dtype=np.float32).astype(ml_dtypes.bfloat16),
    }
    cw = ii["L0_attn_a_conv_w"][:, 0, :]
    shared["cw"] = np.ascontiguousarray(
        cw.reshape(4, HC, 128).transpose(2, 1, 0).reshape(128, HC * 4))
    shared["wrt"] = vec_t(ii["L0_ffn_w_router"], DC)
    shared["g1"] = vec_t(ii["L0_ffn_g1"], DC)
    shared["g2"] = vec_t(ii["L0_ffn_g2"], DC)
    blk = np.zeros((128, 4), np.float32)
    for p in range(128):
        blk[p, p // 32] = 1.0
    shared["blkid"] = blk
    shared["fmask"] = np.ascontiguousarray(ii["L0_ffn_mask"].astype(np.float32))
    fr = ii["L0_ffn_freqs_cis"].astype(np.float32)
    shared["fcos"] = np.ascontiguousarray(fr[:, :, 0].T)
    shared["fsin"] = np.ascontiguousarray(fr[:, :, 1].T)
    shared["bwq"] = pack_w(ii["L0_attn_b_Wq"], DC, DC)
    shared["bwk"] = pack_w(ii["L0_attn_b_Wk"], DC, 4)
    shared["bwv"] = np.ascontiguousarray(
        ii["L0_attn_b_Wv"].astype(np.float32)
        .reshape(8, 2, 128, 512).transpose(0, 2, 1, 3).reshape(8, 128, 1024))
    shared["bwo"] = pack_w(ii["L0_attn_b_Wo"], DC, DC)
    bmasks = []
    kvidxs = []
    r_ = np.arange(128)
    j_ = np.arange(1152)
    for q in range(4):
        mm_ = np.full((1024, 1152), -1.0e9, np.float32)
        for i in range(8):
            base = 1024 * q + 128 * i
            ktg = (1024 * (q - 1) + 128 * i) + j_[None, :]
            valid = (j_[None, :] >= r_[:, None]) & (j_[None, :] <= r_[:, None] + 1024) \
                    & (ktg >= 0)
            blkm = np.where(valid, 1.0, 0.0).astype(np.float32)
            mm_[i * 128:(i + 1) * 128] = blkm
        bmasks.append(mm_.astype(ml_dtypes.bfloat16))
        kv = np.empty((128, 16), np.int32)
        for wc in range(16):
            gi = 1024 * (q - 1) + 128 * wc + r_
            kv[:, wc] = np.where(gi >= 0, gi, 8192).astype(np.int32)
        kvidxs.append(kv)

    fslices = []
    for q in range(4):
        fs = {
            "fwq": pack_w(np.ascontiguousarray(ii["L0_ffn_wq"][:, 512 * q:512 * (q + 1)]),
                          DC, 4),
            "fwk": pack_w(np.ascontiguousarray(ii["L0_ffn_wk"][:, 512 * q:512 * (q + 1)]),
                          DC, 4),
            "fwv": np.ascontiguousarray(
                ii["L0_ffn_wv"][:, 512 * q:512 * (q + 1)].astype(np.float32)
                .reshape(4, 4, 128, 2, 256).transpose(3, 0, 2, 1, 4)
                .reshape(2, 4, 128, 1024)),
            "fwo": pack_w(np.ascontiguousarray(ii["L0_ffn_wo"][512 * q:512 * (q + 1), :]),
                          4, DC),
            "fw1": pack_w(np.ascontiguousarray(ii["L0_ffn_w1"][:, 2048 * q:2048 * (q + 1)]),
                          DC, 16),
            "fw3": pack_w(np.ascontiguousarray(ii["L0_ffn_w3"][:, 2048 * q:2048 * (q + 1)]),
                          DC, 16),
            "fw2": np.ascontiguousarray(
                ii["L0_ffn_w2"][2048 * q:2048 * (q + 1), :].astype(np.float32)
                .reshape(8, 2, 128, 4, 512).transpose(3, 0, 2, 1, 4)
                .reshape(4, 8, 128, 1024)),
        }
        fslices.append(fs)

    xT = [np.ascontiguousarray(x_in[b].T) for b in range(2)]

    in_maps = []
    for c in range(NCORES):
        b, q = c // 4, c % 4
        m = dict(shared)
        x0 = np.zeros((D, WH), np.float32)
        lo = 1024 * q - 3
        if lo < 0:
            x0[:, 3:] = xT[b][:, 0:1024 * (q + 1)]
        else:
            x0[:, :] = xT[b][:, lo:1024 * (q + 1)]
        m["x0"] = x0
        m["hv"] = np.array([[0.0 if q == 0 else 1.0]], np.float32)
        qs = np.zeros((1, 4), np.float32)
        qs[0, :q] = 1.0
        m["qsel"] = qs
        m["qsel4"] = np.ascontiguousarray(qs.reshape(4, 1))
        m.update(fslices[q])
        m["bmask"] = bmasks[q]
        m["kvidx"] = kvidxs[q]
        in_maps.append(m)
    return in_maps


def kernel(**inputs):
    ii = {kk: np.asarray(v) for kk, v in inputs.items()}
    in_maps = build_inmaps(ii)

    if "prog" not in _prog_cache:
        _prog_cache["prog"] = build_program()
    nc = _prog_cache["prog"]
    res = run_bass_kernel_spmd(nc, in_maps, core_ids=list(range(NCORES)))

    y = np.empty((2, 4096, D), np.float32)
    for c in range(NCORES):
        b, q = c // 4, c % 4
        y[b, 1024 * q:1024 * (q + 1), :] = res.results[c]["out"]
    return y


# version 16
# speedup vs baseline: 1.1424x; 1.1424x over previous
import numpy as np
import ml_dtypes
import concourse.bass as bass
import concourse.mybir as mybir
from concourse.bass import IndirectOffsetOnAxis
from concourse.bass_utils import run_bass_kernel_spmd
from contextlib import ExitStack

F32 = mybir.dt.float32
BF16 = mybir.dt.bfloat16
I32 = mybir.dt.int32
AF = mybir.ActivationFunctionType
ALU = mybir.AluOpType
AX = mybir.AxisListType

NCORES = 8
D = 2048
DC = D // 128
HID = 2560
HC = HID // 128
W = 1024
WH = W + 3
C_RGLRU = 8.0
GELU_C0 = 0.7978845608028654
GELU_C1 = GELU_C0 * 0.044715
STAGES = ("attn_a", "ffn1", "attn_b", "ffn2")
GROUPS4 = [[0, 1, 2, 3], [4, 5, 6, 7]]

SLOTQ = {0: "sync", 1: "sync", 2: "sync", 3: "sync", 4: "scalar",
         5: "gpsimd", 6: "gpsimd", 7: "sync",
         8: "sync", 9: "sync", 10: "sync", 11: "sync", 12: "scalar",
         13: "gpsimd", 14: "gpsimd", 15: "sync"}


class Buf:
    __slots__ = ("name", "ready", "readers")

    def __init__(self, name="buf"):
        self.name = name
        self.ready = None
        self.readers = []


class K:
    ENGINES = ("tensor", "vector", "scalar", "gpsimd", "sync")

    def __init__(self, nc, n_dma_sems=16):
        self.nc = nc
        self.sems = {}
        self.counts = {}
        self.waited = {}
        self.queues = {e: [] for e in self.ENGINES}
        self._sem_ctxs = []
        for e in self.ENGINES:
            self._alloc_sem("s_" + e, 1)
        for i in range(n_dma_sems):
            self._alloc_sem(f"s_dma{i}", 16)
        self._alloc_sem("s_cc", 1)
        self.eng_sem = {e: "s_" + e for e in self.ENGINES}
        self.pending = {}

    def _alloc_sem(self, name, step):
        ctx = self.nc.semaphore(name)
        h = ctx.__enter__()
        self._sem_ctxs.append(ctx)
        self.sems[name] = (h, step)
        self.counts[name] = 0

    def _emit_wait(self, engine, sem_name, count):
        if count <= 0:
            return
        if engine == "tensor" and sem_name == "s_tensor":
            return
        key = (engine, sem_name)
        if self.waited.get(key, 0) >= count:
            return
        self.waited[key] = count
        h, _ = self.sems[sem_name]
        self.queues[engine].append(("wait", h, count))

    def _deps(self, engine, reads, writes):
        for b in reads or ():
            if b.ready:
                self._emit_wait(engine, *b.ready)
        for b in writes or ():
            if b.ready:
                self._emit_wait(engine, *b.ready)
            for r in b.readers:
                self._emit_wait(engine, *r)

    def _record(self, sem_name, count, reads, writes):
        for b in reads or ():
            b.readers.append((sem_name, count))
        for b in writes or ():
            b.ready = (sem_name, count)
            b.readers = []

    def op(self, engine, fn, reads=(), writes=(), sem=None, inc=True):
        sem_name = sem or self.eng_sem[engine]
        self._deps(engine, reads, writes)
        h, step = self.sems[sem_name]
        if inc:
            self.counts[sem_name] += step
            cnt = self.counts[sem_name]
            self.queues[engine].append(("op", fn, h, step))
            pend = self.pending.pop(sem_name, [])
            self._record(sem_name, cnt, list(reads) + pend, writes)
        else:
            self.pending.setdefault(sem_name, []).extend(reads)
            self.queues[engine].append(("op", fn, None, 0))

    def dma(self, slot, fn, reads=(), writes=()):
        engine = SLOTQ[slot]
        sem_name = f"s_dma{slot}"
        self._emit_wait(engine, sem_name, self.counts[sem_name])
        self.op(engine, fn, reads=reads, writes=writes, sem=sem_name)

    def cc(self, fn, reads=(), writes=()):
        self.op("gpsimd", fn, reads=reads, writes=writes, sem="s_cc")

    def finish(self, engine="sync"):
        for name in self.sems:
            c = self.counts[name]
            if c > 0:
                self._emit_wait(engine, name, c)

    def build(self, block):
        for e in self.ENGINES:
            q = self.queues[e]
            if not q:
                continue
            section = getattr(block, e)

            def make(q=q):
                def _(eng):
                    for item in q:
                        if item[0] == "wait":
                            _, h, count = item
                            eng.wait_ge(h, count)
                        else:
                            _, fn, h, step = item
                            inst = fn(eng)
                            if h is not None:
                                inst.then_inc(h, step)
                return _

            section(make())


def build_program():
    nc = bass.Bass("TRN2", target_bir_lowering=False, debug=False,
                   dynamic_dma_scratch_size=20480)

    def dram_in(name, shape, dt=F32):
        return nc.dram_tensor(name, shape, dt, kind="ExternalInput").ap()

    x0p = dram_in("x0", [D, WH])
    wlp = dram_in("wl", [HC, 128, DC * 128])
    wrp = dram_in("wr", [HC, 128, DC * 128])
    wap = dram_in("wa", [HC, 128, HC * 128])
    wip = dram_in("wi", [HC, 128, HC * 128])
    wop = dram_in("wo", [DC, 128, HC * 128])
    bl_i = dram_in("bl", [128, HC]); br_i = dram_in("br", [128, HC])
    ba_i = dram_in("ba", [128, HC]); bi_i = dram_in("bi", [128, HC])
    bo_i = dram_in("bo", [128, DC]); cb_i = dram_in("cb", [128, HC])
    cw_i = dram_in("cw", [128, HC * 4]); lam_i = dram_in("lam", [128, HC])
    hv_i = dram_in("hv", [1, 1]); qsel_i = dram_in("qsel", [1, 4])
    idf_i = dram_in("idf", [128, 128]); idb_i = dram_in("idb", [128, 128], BF16)
    wrt_i = dram_in("wrt", [128, DC]); g1_i = dram_in("g1", [128, DC])
    g2_i = dram_in("g2", [128, DC]); blk_i = dram_in("blkid", [128, 4])
    qsel4_i = dram_in("qsel4", [4, 1])
    fwq = dram_in("fwq", [4, 128, DC * 128]); fwk = dram_in("fwk", [4, 128, DC * 128])
    fwv = dram_in("fwv", [2, 4, 128, 1024]); fwo = dram_in("fwo", [DC, 128, 512])
    fw1 = dram_in("fw1", [16, 128, DC * 128]); fw3 = dram_in("fw3", [16, 128, DC * 128])
    fw2 = dram_in("fw2", [4, 8, 128, 1024])
    fmask = dram_in("fmask", [1024, 1024])
    bwq = dram_in("bwq", [16, 128, DC * 128]); bwk = dram_in("bwk", [4, 128, DC * 128])
    bwv = dram_in("bwv", [8, 128, 1024]); bwo = dram_in("bwo", [DC, 128, DC * 128])
    bmask_i = dram_in("bmask", [1024, 1152], BF16)
    kvidx_i = dram_in("kvidx", [128, 16], I32)
    fcos_i = dram_in("fcos", [64, 1024]); fsin_i = dram_in("fsin", [64, 1024])

    out_t = nc.dram_tensor("out", [W, D], F32, kind="ExternalOutput").ap()

    relay_in = nc.dram_tensor("relay_in", [1, 2 * HID], F32).ap()
    relay_out = nc.dram_tensor("relay_out", [4, 2 * HID], F32).ap()
    lg_in = nc.dram_tensor("lg_in", [1, 1024], F32).ap()
    lg_out = nc.dram_tensor("lg_out", [4, 1024], F32).ap()
    idx_dram = nc.dram_tensor("idx_dram", [1, 1024], I32).ap()
    rw_dram = nc.dram_tensor("rw_dram", [1, 1024], F32).ap()
    cpt_in = nc.dram_tensor("cpt_in", [1024, 2048], BF16).ap()
    cpt_out = nc.dram_tensor("cpt_out", [1024, 2048], BF16).ap()
    at_in = nc.dram_tensor("at_in", [2048, 1024], BF16).ap()
    at_out = nc.dram_tensor("at_out", [2048, 1024], BF16).ap()
    fo_in = nc.dram_tensor("fo_in", [1024, 2048], BF16).ap()
    fo_out = nc.dram_tensor("fo_out", [1024, 2048], BF16).ap()
    xo_dram = nc.dram_tensor("xo_dram", [1024, 2048], BF16).ap()
    xst_dram = nc.dram_tensor("xst_dram", [128, DC * W], F32).ap()
    lcp_dram = nc.dram_tensor("lcp_dram", [HC * 128, W], BF16).ap()
    kvag_in = nc.dram_tensor("kvag_in", [1024, 1024], BF16).ap()
    kvag_out = nc.dram_tensor("kvag_out", [4096, 1024], BF16).ap()

    live = [0]; peak = [0]

    def mk_sb(stack, name, shape, dt=F32):
        t = stack.enter_context(nc.sbuf_tensor(name, shape, dt))
        bpe = {F32: 4, BF16: 2, I32: 4}[dt]
        live[0] += shape[1] * bpe
        peak[0] = max(peak[0], live[0])
        return t

    class Grp:
        def __init__(self, side="left"):
            self.stack = ExitStack()
            self.bytes = 0
            self.side = side

        def sb(self, name, shape, dt=F32):
            before = live[0]
            t = self.stack.enter_context(nc.sbuf_tensor(name, shape, dt, side=self.side))
            bpe = {F32: 4, BF16: 2, I32: 4}[dt]
            live[0] += shape[1] * bpe
            peak[0] = max(peak[0], live[0])
            self.bytes += live[0] - before
            return t

        def free(self):
            self.stack.close()
            live[0] -= self.bytes

    g_persist = Grp()
    sb = g_persist.sb

    identf = sb("identf", [128, 128])
    identb = sb("identb", [128, 128], BF16)
    ones_b = sb("ones_b", [128, 1], BF16)
    ones_row = sb("ones_row", [1, 128])
    zeros_bw = sb("zeros_bw", [128, W], BF16)
    bl_t = sb("bl_t", [128, HC]); br_t = sb("br_t", [128, HC])
    ba_t = sb("ba_t", [128, HC]); bi_t = sb("bi_t", [128, HC])
    bo_t = sb("bo_t", [128, DC]); cb_t = sb("cb_t", [128, HC])
    cw_t = sb("cw_t", [128, HC * 4]); lam_t = sb("lam_t", [128, HC])
    slam_t = sb("slam_t", [128, HC]); slam2_t = sb("slam2_t", [128, HC])
    hv_b = sb("hv_b", [128, 1])
    qsel_b = sb("qsel_b", [128, 4]); omq_b = sb("omq_b", [128, 4])
    qsel_s = sb("qsel_s", [1, 4]); tiny = sb("tiny", [128, 32])
    fsc = sb("fsc", [128, 8])
    cst = sb("cst", [128, 4])
    atot_t = sb("atot_t", [128, HC]); hlast_t = sb("hlast_t", [128, HC])
    rel = sb("rel", [128, 8 * HC]); Hst = sb("Hst", [128, HC])
    e0 = sb("e0", [128, W]); e1 = sb("e1", [128, W])
    g_attA = Grp()
    e2b = g_attA.sb("e2b", [128, W])
    ubuf0 = g_attA.sb("ubuf0", [128, W]); rin0 = g_attA.sb("rin0", [128, WH])
    cacc0 = g_attA.sb("cacc0", [128, W])
    wrawA = g_attA.sb("wrawA", [128, HC * 128])
    wbfA0 = g_attA.sb("wbfA0", [128, HC * 128], BF16)
    wbfA1 = g_attA.sb("wbfA1", [128, HC * 128], BF16)

    PS12 = g_persist.stack.enter_context(nc.psum_tensor("PS12", [128, 2048], F32))
    PS3 = g_persist.stack.enter_context(nc.psum_tensor("PS3", [128, 1536], F32))
    PSB = g_persist.stack.enter_context(nc.psum_tensor("PSB", [128, 1024], BF16))

    k = K(nc)
    B = {}

    def buf(name):
        if name not in B:
            B[name] = Buf(name)
        return B[name]

    P12, P3, PB = buf("PS12"), buf("PS3"), buf("PSB")
    FSC = buf("fsc")
    n_fence = [0]

    def fence(dead_bufs):
        ep = Buf(f"fence{n_fence[0]}")
        col = n_fence[0] % 8
        n_fence[0] += 1
        k.op("vector", lambda e, col=col: e.tensor_copy(
                out=fsc[:, col:col + 1], in_=fsc[:, col:col + 1]),
             reads=[FSC], writes=list(dead_bufs) + [ep, FSC])
        return ep

    for nm, tt_, ap_ in [("bl_t", bl_t, bl_i), ("br_t", br_t, br_i), ("ba_t", ba_t, ba_i),
                         ("bi_t", bi_t, bi_i), ("bo_t", bo_t, bo_i), ("cb_t", cb_t, cb_i),
                         ("cw_t", cw_t, cw_i), ("lam_t", lam_t, lam_i),
                         ("identf", identf, idf_i), ("identb", identb, idb_i),
                         ("qsel_s", qsel_s, qsel_i)]:
        k.dma(7, lambda e, t=tt_, a=ap_: e.dma_start(out=t[:, :], in_=a), writes=[buf(nm)])
    k.dma(7, lambda e: e.dma_start(out=tiny[0:1, 0:1], in_=hv_i), writes=[buf("hv_raw")])
    cONES_B, cONES_R, cZER = buf("cob"), buf("cor"), buf("czw")
    k.op("vector", lambda e: e.memset(ones_b[:, :], 1.0), writes=[cONES_B])
    k.op("vector", lambda e: e.memset(ones_row[:, :], 1.0), writes=[cONES_R])
    k.op("vector", lambda e: e.memset(zeros_bw[:, :], 0.0), writes=[cZER])
    k.op("vector", lambda e: e.memset(fsc[:, :], 0.0), writes=[FSC])
    k.op("vector", lambda e: e.memset(cst[:, 0:1], 1e-5), writes=[buf("cst")])
    k.op("vector", lambda e: e.memset(cst[:, 1:2], 1e-6), writes=[buf("cst")])
    k.op("vector", lambda e: e.memset(cst[:, 2:3], 1.0), writes=[buf("cst")])
    k.op("tensor", lambda e: e.matmul(PS3[:, 0:1], ones_row[:, :], tiny[0:1, 0:1],
                                      start=True, stop=True),
         reads=[cONES_R, buf("hv_raw")], writes=[P3])
    k.op("vector", lambda e: e.tensor_copy(out=hv_b[:, :], in_=PS3[:, 0:1]),
         reads=[P3], writes=[buf("hv_b")])
    k.op("tensor", lambda e: e.matmul(PS3[:, 4:8], ones_row[:, :], qsel_s[:, :],
                                      start=True, stop=True),
         reads=[cONES_R, buf("qsel_s")], writes=[P3])
    k.op("vector", lambda e: e.tensor_copy(out=qsel_b[:, :], in_=PS3[:, 4:8]),
         reads=[P3], writes=[buf("qsel_b")])
    k.op("vector", lambda e: e.tensor_scalar(out=omq_b[:, :], in0=qsel_b[:, :], scalar1=-1.0,
                                             scalar2=1.0, op0=ALU.mult, op1=ALU.add),
         reads=[buf("qsel_b")], writes=[buf("omq_b")])
    k.op("vector", lambda e: e.tensor_scalar_mul(slam_t[:, :], lam_t[:, :], -1.0),
         reads=[buf("lam_t")], writes=[buf("slam")])
    k.op("scalar", lambda e: e.activation(out=tiny[:, 1:1 + HC], in_=slam_t[:, :], func=AF.Abs),
         reads=[buf("slam")], writes=[buf("t_az")])
    k.op("scalar", lambda e: e.activation(out=tiny[:, 1:1 + HC], in_=tiny[:, 1:1 + HC],
                                          func=AF.Exp, scale=-1.0),
         reads=[buf("t_az")], writes=[buf("t_az")])
    k.op("scalar", lambda e: e.activation(out=tiny[:, 1:1 + HC], in_=tiny[:, 1:1 + HC],
                                          func=AF.Ln, bias=cst[:, 2:3]),
         reads=[buf("t_az"), buf("cst")], writes=[buf("t_az")])
    k.op("vector", lambda e: e.tensor_scalar_max(slam_t[:, :], slam_t[:, :], 0.0),
         reads=[buf("slam")], writes=[buf("slam")])
    k.op("vector", lambda e: e.tensor_add(slam_t[:, :], slam_t[:, :], tiny[:, 1:1 + HC]),
         reads=[buf("slam"), buf("t_az")], writes=[buf("slam")])
    k.op("vector", lambda e: e.tensor_scalar_mul(slam_t[:, :], slam_t[:, :], -C_RGLRU),
         reads=[buf("slam")], writes=[buf("slam")])
    k.op("vector", lambda e: e.tensor_scalar_mul(slam2_t[:, :], slam_t[:, :], 2.0),
         reads=[buf("slam")], writes=[buf("slam2")])


    WS_SLOTS = [0, 1, 2, 3, 8, 9, 10, 11, 7, 15]

    def stream_weight(stream, src_ap, m, kc_n, wraw, wbf, cast_eng="scalar", extra=(),
                      soff=0):
        ncols = kc_n * 128
        half = (ncols // 256) * 128
        def gspan(pfx, tid, o0, o1):
            return [buf(f"{pfx}_{tid}_{g}") for g in range(o0 // 128, o1 // 128)]
        allbf = gspan("bf", id(wbf), 0, ncols)
        for h in range(2):
            o0, o1 = h * half, ncols if h else half
            slot = WS_SLOTS[(soff * 4 + h * 2 + (m % 2)) % 10]
            rb = gspan("raw", id(wraw), o0, o1)
            bb = gspan("bf", id(wbf), o0, o1)
            k.dma(slot, lambda e, o0=o0, o1=o1: e.dma_start(
                    out=wraw[:, o0:o1], in_=src_ap[m, :, o0:o1]),
                  reads=list(extra), writes=rb)
            ce = ("scalar", "vector") if cast_eng == "scalar" else ("gpsimd", "vector")
            ce = ce[h]
            if ce == "scalar":
                k.op("scalar", lambda e, o0=o0, o1=o1: e.activation(
                        out=wbf[:, o0:o1], in_=wraw[:, o0:o1], func=AF.Copy),
                     reads=rb, writes=bb)
            else:
                k.op(ce, lambda e, o0=o0, o1=o1: e.tensor_copy(
                        out=wbf[:, o0:o1], in_=wraw[:, o0:o1]),
                     reads=rb, writes=bb)
        return allbf

    def nslices(nw):
        out, o = [], 0
        while o < nw:
            cur = min(512, nw - o)
            out.append((o, cur))
            o += cur
        return out

    def mm_acc(ps, psb, wbf, wbufs, kc_n, rhs_fn, nw, rhs_bufs):
        sl = nslices(nw)
        for kc in range(kc_n):
            for si, (o, cur) in enumerate(sl):
                last = (kc == kc_n - 1) and (si == len(sl) - 1)
                first = kc == 0 and si == 0
                k.op("tensor",
                     lambda e, kc=kc, o=o, cur=cur: e.matmul(
                         ps[:, o:o + cur], wbf[:, kc * 128:(kc + 1) * 128],
                         rhs_fn(kc, o, cur), start=(kc == 0), stop=(kc == kc_n - 1)),
                     reads=list(wbufs) + list(rhs_bufs),
                     writes=[psb] if (last or first) else (), inc=last)

    g_xp = Grp("right")
    xpb = g_xp.sb("xpb", [128, DC * WH], BF16)
    g_a1 = Grp("right")
    x0st0 = g_a1.sb("x0st0", [128, WH])
    x0st1 = g_a1.sb("x0st1", [128, WH])
    sqa = g_a1.sb("sqa", [128, WH], BF16)
    rstd_bc = g_a1.sb("rstd_bc", [128, WH])
    nmrs_bc = g_a1.sb("nmrs_bc", [128, WH])
    stat = g_a1.sb("stat", [1, 3 * WH])

    nsl_wh = [(0, 512), (512, 512), (1024, 3)]
    s0 = stat[0:1, 0:WH]; s1 = stat[0:1, WH:2 * WH]; s2 = stat[0:1, 2 * WH:3 * WH]
    for dc in range(DC):
        x0c_ = [x0st0, x0st1][dc % 2]
        k.dma((dc % 4) * 2, lambda e, dc=dc, x0c_=x0c_: e.dma_start(
                out=x0c_[:, :], in_=x0p[dc * 128:(dc + 1) * 128, :]),
              writes=[buf(f"x0s_{dc % 2}")])
        k.op("vector", lambda e, dc=dc, x0c_=x0c_: e.tensor_copy(
                out=xpb[:, dc * WH:(dc + 1) * WH], in_=x0c_[:, :]),
             reads=[buf(f"x0s_{dc % 2}")], writes=[buf(f"xpb_{dc}")])
        k.op("scalar", lambda e, dc=dc: e.activation(
                out=sqa[:, :], in_=xpb[:, dc * WH:(dc + 1) * WH], func=AF.Square),
             reads=[buf(f"xpb_{dc}")], writes=[buf("sq_0")])
        for si, (o, cur) in enumerate(nsl_wh):
            last = dc == DC - 1 and si == 2
            k.op("tensor", lambda e, dc=dc, o=o, cur=cur: e.matmul(
                    PS12[0:1, o:o + cur], ones_b[:, :],
                    xpb[:, dc * WH + o:dc * WH + o + cur],
                    start=(dc == 0), stop=(dc == DC - 1)),
                 reads=[buf(f"xpb_{dc}"), cONES_B], writes=[P12] if last else ())
            k.op("tensor", lambda e, o=o, cur=cur, dc=dc: e.matmul(
                    PS3[0:1, o:o + cur], ones_b[:, :], sqa[:, o:o + cur],
                    start=(dc == 0), stop=(dc == DC - 1)),
                 reads=[buf("sq_0"), cONES_B], writes=[P3] if last else ())
    k.op("vector", lambda e: e.tensor_scalar_mul(s0, PS12[0:1, 0:WH], 1.0 / D),
         reads=[P12], writes=[buf("st0")])
    k.op("vector", lambda e: e.tensor_scalar_mul(s1, PS3[0:1, 0:WH], 1.0 / D),
         reads=[P3], writes=[buf("st1")])
    k.op("vector", lambda e: e.tensor_mul(s2, s0, s0),
         reads=[buf("st0")], writes=[buf("st2")])
    k.op("vector", lambda e: e.tensor_sub(s1, s1, s2),
         reads=[buf("st1"), buf("st2")], writes=[buf("st1")])
    k.op("scalar", lambda e: e.activation(out=s1, in_=s1, func=AF.Sqrt, bias=cst[0:1, 0:1]),
         reads=[buf("st1"), buf("cst")], writes=[buf("st1")])
    k.op("vector", lambda e: e.reciprocal(s1, s1),
         reads=[buf("st1")], writes=[buf("st1")])
    k.op("vector", lambda e: e.tensor_mul(s0, s0, s1),
         reads=[buf("st0"), buf("st1")], writes=[buf("st0")])
    k.op("vector", lambda e: e.tensor_scalar_mul(s0, s0, -1.0),
         reads=[buf("st0")], writes=[buf("st0")])
    for si, (o, cur) in enumerate(nsl_wh):
        last = si == 2
        k.op("tensor", lambda e, o=o, cur=cur: e.matmul(
                PS12[:, o:o + cur], ones_row[:, :], s1[0:1, o:o + cur],
                start=True, stop=True),
             reads=[cONES_R, buf("st1")], writes=[P12] if last else (), inc=last)
        k.op("tensor", lambda e, o=o, cur=cur: e.matmul(
                PS3[:, o:o + cur], ones_row[:, :], s0[0:1, o:o + cur],
                start=True, stop=True),
             reads=[cONES_R, buf("st0")], writes=[P3] if last else (), inc=last)
    k.op("vector", lambda e: e.tensor_copy(out=rstd_bc[:, :], in_=PS12[:, 0:WH]),
         reads=[P12], writes=[buf("rstd_bc")])
    k.op("vector", lambda e: e.tensor_copy(out=nmrs_bc[:, :], in_=PS3[:, 0:WH]),
         reads=[P3], writes=[buf("nmrs_bc")])
    for dc in range(DC):
        x0c_ = [x0st0, x0st1][dc % 2]
        k.dma((dc % 4) * 2, lambda e, dc=dc, x0c_=x0c_: e.dma_start(
                out=x0c_[:, :], in_=x0p[dc * 128:(dc + 1) * 128, :]),
              writes=[buf(f"x0s_{dc % 2}")])
        eng = "vector" if dc % 2 == 0 else "gpsimd"
        sc = [e0, e1][dc % 2]
        k.op(eng, lambda e, dc=dc, sc=sc, x0c_=x0c_: e.tensor_mul(
                sc[:, :], x0c_[:, 3:WH], rstd_bc[:, 3:WH]),
             reads=[buf(f"x0s_{dc % 2}"), buf("rstd_bc")], writes=[buf(f"sc{dc % 2}")])
        k.op(eng, lambda e, dc=dc, sc=sc: e.tensor_add(
                xpb[:, dc * WH + 3:(dc + 1) * WH], sc[:, :], nmrs_bc[:, 3:WH]),
             reads=[buf(f"sc{dc % 2}"), buf("nmrs_bc")], writes=[buf(f"xpb_{dc}")])
        k.op(eng, lambda e, dc=dc, sc=sc, x0c_=x0c_: e.tensor_mul(
                sc[:, 0:3], x0c_[:, 0:3], rstd_bc[:, 0:3]),
             reads=[buf(f"x0s_{dc % 2}"), buf("rstd_bc"), buf(f"xpb_{dc}")],
             writes=[buf(f"sc{dc % 2}")])
        k.op(eng, lambda e, dc=dc, sc=sc: e.tensor_add(
                xpb[:, dc * WH:dc * WH + 3], sc[:, 0:3], nmrs_bc[:, 0:3]),
             reads=[buf(f"sc{dc % 2}")], writes=[buf(f"xpb_h{dc}")])

    xpb_bufs = [buf(f"xpb_{dc}") for dc in range(DC)] + [buf(f"xpb_h{dc}") for dc in range(DC)]
    fence_a1 = fence([buf("x0s_0"), buf("x0s_1"), buf("sq_0"), buf("rstd_bc"),
                      buf("nmrs_bc"), buf("st0"), buf("st1"), buf("st2"),
                      buf("sc0"), buf("sc1")])
    g_a1.free()

    g_left = Grp()
    left = g_left.sb("left", [128, HC * W], BF16)
    g_conv = Grp()
    convb = g_conv.sb("convb", [128, HC * W], BF16)
    g_a2t = Grp("right")

    if "attn_a" in STAGES:
        pend = {0: stream_weight("wl0", wlp, 0, DC, wrawA, wbfA0)}
        for m in range(HC):
            wbf = [wbfA0, wbfA1][m % 2]
            wbufs = pend.pop(m)
            if m + 1 < HC:
                pend[m + 1] = stream_weight(f"wl{(m + 1) % 2}", wlp, m + 1, DC, wrawA,
                                            [wbfA0, wbfA1][(m + 1) % 2])
            else:
                pend[0] = stream_weight("wr0", wrp, 0, DC, wrawA, wbfA0)
            ps, psb = [(PS3, P3), (PS12, P12)][m % 2]
            mm_acc(ps, psb, wbf, wbufs, DC,
                   lambda kc, o, cur: xpb[:, kc * WH + 3 + o:kc * WH + 3 + o + cur],
                   W, xpb_bufs)
            k.op("scalar", lambda e, ps=ps, m=m: e.activation(
                    out=left[:, m * W:(m + 1) * W], in_=ps[:, 0:W],
                    func=AF.Gelu_apprx_tanh, bias=bl_t[:, m:m + 1]),
                 reads=[psb, buf("bl_t"), fence_a1], writes=[buf(f"left_{m}")])

        for m in range(HC):
            wbf = [wbfA0, wbfA1][m % 2]
            wbufs = pend.pop(m)
            if m + 1 < HC:
                pend[m + 1] = stream_weight(f"wr{(m + 1) % 2}", wrp, m + 1, DC, wrawA,
                                            [wbfA0, wbfA1][(m + 1) % 2])
            ps, psb = [(PS12, P12), (PS3, P3)][m % 2]
            mm_acc(ps, psb, wbf, wbufs, DC,
                   lambda kc, o, cur: xpb[:, kc * WH + o:kc * WH + o + cur],
                   WH, xpb_bufs)
            rin = rin0
            rb = buf("rin0")
            k.op("scalar", lambda e, ps=ps, rin=rin, m=m: e.activation(
                    out=rin[:, :], in_=ps[:, 0:WH], func=AF.Identity, bias=br_t[:, m:m + 1]),
                 reads=[psb, buf("br_t"), fence_a1], writes=[rb])
            k.op("vector", lambda e, rin=rin: e.tensor_scalar(
                    out=rin[:, 0:3], in0=rin[:, 0:3], scalar1=hv_b[:, 0:1], scalar2=None,
                    op0=ALU.mult),
                 reads=[rb, buf("hv_b")], writes=[rb])
            cacc = cacc0
            cb = buf("cacc0")
            k.op("vector", lambda e, rin=rin, cacc=cacc, m=m: e.tensor_scalar(
                    out=cacc[:, :], in0=rin[:, 0:W], scalar1=cw_t[:, 4 * m:4 * m + 1],
                    scalar2=None, op0=ALU.mult),
                 reads=[rb, buf("cw_t")], writes=[cb])
            for j in (1, 2, 3):
                eng = "vector"
                k.op(eng, lambda e, rin=rin, cacc=cacc, m=m, j=j: e.scalar_tensor_tensor(
                        out=cacc[:, :], in0=rin[:, j:j + W],
                        scalar=cw_t[:, 4 * m + j:4 * m + j + 1],
                        in1=cacc[:, :], op0=ALU.mult, op1=ALU.add),
                     reads=[rb, buf("cw_t"), cb], writes=[cb])
            k.op("vector", lambda e, cacc=cacc, m=m: e.tensor_scalar(
                    out=convb[:, m * W:(m + 1) * W], in0=cacc[:, :],
                    scalar1=cb_t[:, m:m + 1], scalar2=None, op0=ALU.add),
                 reads=[cb, buf("cb_t"), fence_a1], writes=[buf(f"convb_{m}")])

    conv_bufs = [buf(f"convb_{m}") for m in range(HC)]
    left_bufs = [buf(f"left_{m}") for m in range(HC)]
    fence_a2 = fence(xpb_bufs + [buf("u0"), buf("rin0"), buf("cacc0"), buf("e2b0")])
    g_a2t.free()
    g_xp.free()

    g_a3 = Grp("right")
    hbuf = g_a3.sb("hbuf", [128, W])
    cpbuf = g_a3.sb("cpbuf", [128, W])
    wrawI = g_a3.sb("wrawI", [128, HC * 128])
    wbfI = g_a3.sb("wbfI", [128, HC * 128], BF16)
    wbfI2 = g_a3.sb("wbfI2", [128, HC * 128], BF16)
    lcst = g_a3.sb("lcst", [128, W], BF16)

    if "attn_a" in STAGES:
        pendA = {0: stream_weight("wa0", wap, 0, HC, wrawA, wbfA0, "scalar")}
        pendI = {0: stream_weight("wi0", wip, 0, HC, wrawI, wbfI, "gpsimd",
                                  extra=[fence_a2], soff=1)}
        for m in range(HC):
            wbA = [wbfA0, wbfA1][m % 2]
            wbI = [wbfI, wbfI2][m % 2]
            wbufsA = pendA.pop(m)
            wbufsI = pendI.pop(m)
            if m + 1 < HC:
                pendA[m + 1] = stream_weight(f"wa{(m + 1) % 2}", wap, m + 1, HC, wrawA,
                                             [wbfA0, wbfA1][(m + 1) % 2], "scalar")
                pendI[m + 1] = stream_weight(f"wi{(m + 1) % 2}", wip, m + 1, HC, wrawI,
                                             [wbfI, wbfI2][(m + 1) % 2], "gpsimd",
                                             soff=1)
            psR, psRb = [(PS3, P3), (PS12, P12)][m % 2]
            psI, psIb = [(PS12, P12), (PS3, P3)][m % 2]
            for kc in range(HC):
                for o in (0, 512):
                    last = kc == HC - 1 and o == 512
                    first = kc == 0 and o == 0
                    k.op("tensor", lambda e, kc=kc, o=o, wbA=wbA, psR=psR: e.matmul(
                            psR[:, o:o + 512], wbA[:, kc * 128:(kc + 1) * 128],
                            convb[:, kc * W + o:kc * W + o + 512],
                            start=(kc == 0), stop=(kc == HC - 1)),
                         reads=wbufsA + conv_bufs,
                         writes=[psRb] if (last or first) else (), inc=last)
                    k.op("tensor", lambda e, kc=kc, o=o, wbI=wbI, psI=psI: e.matmul(
                            psI[:, o:o + 512], wbI[:, kc * 128:(kc + 1) * 128],
                            convb[:, kc * W + o:kc * W + o + 512],
                            start=(kc == 0), stop=(kc == HC - 1)),
                         reads=wbufsI + conv_bufs + [fence_a2],
                         writes=[psIb] if (last or first) else (), inc=last)
            k.op("scalar", lambda e, psR=psR, m=m: e.activation(
                    out=rin0[:, 0:W], in_=psR[:, 0:W], func=AF.Sigmoid,
                    bias=ba_t[:, m:m + 1]),
                 reads=[psRb, buf("ba_t"), fence_a2], writes=[buf("rin0")])
            k.op("scalar", lambda e, psI=psI, m=m: e.activation(
                    out=cacc0[:, :], in_=psI[:, 0:W], func=AF.Sigmoid,
                    bias=bi_t[:, m:m + 1]),
                 reads=[psIb, buf("bi_t"), fence_a2], writes=[buf("cacc0")])
            k.op("scalar", lambda e, m=m: e.activation(
                    out=e1[:, :], in_=rin0[:, 0:W], func=AF.Exp,
                    scale=slam_t[:, m:m + 1]),
                 reads=[buf("rin0"), buf("slam")], writes=[buf("e1")])
            k.op("scalar", lambda e, m=m: e.activation(
                    out=e2b[:, :], in_=rin0[:, 0:W], func=AF.Exp,
                    scale=slam2_t[:, m:m + 1]),
                 reads=[buf("rin0"), buf("slam2")], writes=[buf("e2b")])
            k.op("vector", lambda e: e.tensor_scalar(out=e2b[:, :], in0=e2b[:, :],
                                                     scalar1=-1.0, scalar2=1.0,
                                                     op0=ALU.mult, op1=ALU.add),
                 reads=[buf("e2b")], writes=[buf("e2b")])
            k.op("vector", lambda e: e.tensor_scalar_max(e2b[:, :], e2b[:, :], 1e-12),
                 reads=[buf("e2b")], writes=[buf("e2b")])
            k.op("scalar", lambda e: e.activation(out=e2b[:, :], in_=e2b[:, :], func=AF.Sqrt),
                 reads=[buf("e2b")], writes=[buf("e2b")])
            k.op("gpsimd", lambda e, m=m: e.tensor_mul(
                    ubuf0[:, :], cacc0[:, :], convb[:, m * W:(m + 1) * W]),
                 reads=[buf("cacc0"), conv_bufs[m]], writes=[buf("ubuf0")])
            k.op("gpsimd", lambda e: e.tensor_mul(ubuf0[:, :], ubuf0[:, :], e2b[:, :]),
                 reads=[buf("ubuf0"), buf("e2b")], writes=[buf("ubuf0")])
            k.op("vector", lambda e: e.tensor_tensor_scan(
                    out=hbuf[:, :], data0=e1[:, :], data1=ubuf0[:, :], initial=0.0,
                    op0=ALU.mult, op1=ALU.add),
                 reads=[buf("e1"), buf("ubuf0")], writes=[buf("hbuf")])
            k.op("vector", lambda e: e.tensor_tensor_scan(
                    out=cpbuf[:, :], data0=e1[:, :], data1=zeros_bw[:, :], initial=1.0,
                    op0=ALU.mult, op1=ALU.add),
                 reads=[buf("e1"), cZER], writes=[buf("cpbuf")])
            k.op("vector", lambda e, m=m: e.reduce_sum(
                    out=atot_t[:, m:m + 1], in_=rin0[:, 0:W], axis=AX.X),
                 reads=[buf("rin0")], writes=[buf("atot")])
            k.op("scalar", lambda e, m=m: e.activation(
                    out=atot_t[:, m:m + 1], in_=atot_t[:, m:m + 1], func=AF.Exp,
                    scale=slam_t[:, m:m + 1]),
                 reads=[buf("atot"), buf("slam")], writes=[buf("atot")])
            k.op("vector", lambda e, m=m: e.tensor_copy(
                    out=hlast_t[:, m:m + 1], in_=hbuf[:, W - 1:W]),
                 reads=[buf("hbuf")], writes=[buf("hlast")])
            k.op("gpsimd", lambda e, m=m: e.tensor_mul(
                    lcst[:, :], left[:, m * W:(m + 1) * W], cpbuf[:, :]),
                 reads=[left_bufs[m], buf("cpbuf"), fence_a2, buf("lcst")],
                 writes=[buf("lcst")])
            k.dma(5, lambda e, m=m: e.dma_start(
                    out=lcp_dram[m * 128:(m + 1) * 128, :], in_=lcst[:, :]),
                  reads=[buf("lcst")], writes=[buf("lcpd")])
            k.op("gpsimd", lambda e, m=m: e.tensor_mul(
                    left[:, m * W:(m + 1) * W], left[:, m * W:(m + 1) * W], hbuf[:, :]),
                 reads=[left_bufs[m], buf("hbuf")], writes=[left_bufs[m]])

        k.dma(6, lambda e: e.dma_start(
                out=relay_in[0:1, 0:HID].rearrange("a (p c) -> (a p) c", p=128),
                in_=atot_t[:, :]),
              reads=[buf("atot")], writes=[buf("relay_in")])
        k.dma(6, lambda e: e.dma_start(
                out=relay_in[0:1, HID:2 * HID].rearrange("a (p c) -> (a p) c", p=128),
                in_=hlast_t[:, :]),
              reads=[buf("hlast")], writes=[buf("relay_in")])
        k.cc(lambda e: e.collective_compute(
                "AllGather", ALU.bypass, replica_groups=GROUPS4,
                ins=[relay_in], outs=[relay_out]),
             reads=[buf("relay_in")], writes=[buf("relay_out")])
        for j in range(4):
            k.dma(6, lambda e, j=j: e.dma_start(
                    out=rel[:, j * HC:(j + 1) * HC],
                    in_=relay_out[j:j + 1, 0:HID].rearrange("a (p c) -> (a p) c", p=128)),
                  reads=[buf("relay_out")], writes=[buf("rel")])
            k.dma(6, lambda e, j=j: e.dma_start(
                    out=rel[:, (4 + j) * HC:(5 + j) * HC],
                    in_=relay_out[j:j + 1, HID:2 * HID].rearrange("a (p c) -> (a p) c", p=128)),
                  reads=[buf("relay_out")], writes=[buf("rel")])
        k.op("vector", lambda e: e.memset(Hst[:, :], 0.0), writes=[buf("Hst")])
        for j in range(4):
            k.op("vector", lambda e, j=j: e.tensor_scalar(
                    out=e0[:, 0:HC], in0=rel[:, j * HC:(j + 1) * HC],
                    scalar1=qsel_b[:, j:j + 1], scalar2=omq_b[:, j:j + 1],
                    op0=ALU.mult, op1=ALU.add),
                 reads=[buf("rel"), buf("qsel_b"), buf("omq_b")], writes=[buf("e0")])
            k.op("vector", lambda e, j=j: e.tensor_scalar(
                    out=e1[:, 0:HC], in0=rel[:, (4 + j) * HC:(5 + j) * HC],
                    scalar1=qsel_b[:, j:j + 1], scalar2=None, op0=ALU.mult),
                 reads=[buf("rel"), buf("qsel_b")], writes=[buf("e1")])
            k.op("vector", lambda e: e.tensor_mul(Hst[:, :], Hst[:, :], e0[:, 0:HC]),
                 reads=[buf("Hst"), buf("e0")], writes=[buf("Hst")])
            k.op("vector", lambda e: e.tensor_add(Hst[:, :], Hst[:, :], e1[:, 0:HC]),
                 reads=[buf("Hst"), buf("e1")], writes=[buf("Hst")])
        for m in range(HC):
            k.dma(5, lambda e, m=m: e.dma_start(
                    out=lcst[:, :], in_=lcp_dram[m * 128:(m + 1) * 128, :]),
                  reads=[buf("lcpd"), buf("lcst")], writes=[buf("lcst")])
            k.op("vector", lambda e, m=m: e.scalar_tensor_tensor(
                    out=left[:, m * W:(m + 1) * W], in0=lcst[:, :],
                    scalar=Hst[:, m:m + 1], in1=left[:, m * W:(m + 1) * W],
                    op0=ALU.mult, op1=ALU.add),
                 reads=[buf("lcst"), buf("Hst"), left_bufs[m]],
                 writes=[left_bufs[m]])

    fence_a3 = fence(conv_bufs + [buf("lcst"), buf("hbuf"), buf("cpbuf")] +
                     [buf(n) for n in list(B) if n.startswith(("raw_", "bf_"))])
    g_a3.free()
    g_conv.free()

    g_a5 = Grp("right")
    x = g_a5.sb("x", [128, DC * W])
    g_a5r = Grp("right")
    x0c0 = g_a5r.sb("x0c0", [128, W])
    x0c1 = x0c0

    if "attn_a" in STAGES:
        pend = {0: stream_weight("wo0", wop, 0, HC, wrawA, wbfA0)}
        for mo in range(DC):
            wbf = [wbfA0, wbfA1][mo % 2]
            wbufs = pend.pop(mo)
            if mo + 1 < DC:
                pend[mo + 1] = stream_weight(f"wo{(mo + 1) % 2}", wop, mo + 1, HC, wrawA,
                                             [wbfA0, wbfA1][(mo + 1) % 2])
            ps, psb = [(PS3, P3), (PS12, P12)][mo % 2]
            mm_acc(ps, psb, wbf, wbufs, HC,
                   lambda kc, o, cur: left[:, kc * W + o:kc * W + o + cur],
                   W, left_bufs)
            x0c = x0c0
            k.dma(4, lambda e, mo=mo, x0c=x0c: e.dma_start(
                    out=x0c[:, :], in_=x0p[mo * 128:(mo + 1) * 128, 3:WH]),
                  reads=[fence_a3, buf("x0c0")], writes=[buf("x0c0")])
            k.op("vector", lambda e, ps=ps, mo=mo, x0c=x0c: e.scalar_tensor_tensor(
                    out=x[:, mo * W:(mo + 1) * W], in0=ps[:, 0:W],
                    scalar=bo_t[:, mo:mo + 1], in1=x0c[:, :], op0=ALU.add, op1=ALU.add),
                 reads=[psb, buf("bo_t"), buf("x0c0"), fence_a3],
                 writes=[buf(f"x_{mo}")])
    else:
        for mo in range(DC):
            k.dma(4, lambda e, mo=mo: e.dma_start(
                    out=x[:, mo * W:(mo + 1) * W], in_=x0p[mo * 128:(mo + 1) * 128, 3:WH]),
                  reads=[fence_a3], writes=[buf(f"x_{mo}")])

    x_bufs = [buf(f"x_{mo}") for mo in range(DC)]
    fence_a5 = fence(left_bufs +
                     [buf(n) for n in list(B) if n.startswith(("raw_", "bf_"))] +
                     [buf("x0c0")])
    g_a5r.free()
    g_left.free()
    g_attA.free()

    gFper = Grp()
    wrt_t = gFper.sb("wrt_t", [128, DC]); g1_t = gFper.sb("g1_t", [128, DC])
    g2_t = gFper.sb("g2_t", [128, DC])
    blk_t = gFper.sb("blk_t", [128, 4]); qsel4_t = gFper.sb("qsel4_t", [4, 1])
    ones_f = gFper.sb("ones_f", [128, 1])
    cfm = gFper.sb("cfm", [128, 1024], BF16); sfm = gFper.sb("sfm", [128, 1024], BF16)
    fones = gFper.sb("fones", [1, 1024])
    crawf = gFper.sb("crawf", [128, 1024])
    cONESF = buf("cONESF")
    k.op("vector", lambda e: e.memset(ones_f[:, :], 1.0), reads=[fence_a5], writes=[cONESF])
    k.op("vector", lambda e: e.memset(fones[:, :], 1.0), reads=[fence_a5], writes=[buf("fones")])
    for nm, tt_, ap_ in [("wrt_t", wrt_t, wrt_i), ("g1_t", g1_t, g1_i), ("g2_t", g2_t, g2_i),
                         ("blk_t", blk_t, blk_i), ("qsel4_t", qsel4_t, qsel4_i)]:
        k.dma(7, lambda e, t=tt_, a=ap_: e.dma_start(out=t[:, :], in_=a),
              reads=[fence_a5], writes=[buf(nm)])
    k.dma(7, lambda e: e.dma_start(out=crawf[0:64, :], in_=fcos_i),
          reads=[fence_a5], writes=[buf("crawf")])
    k.dma(7, lambda e: e.dma_start(out=crawf[64:128, :], in_=fcos_i),
          reads=[fence_a5], writes=[buf("crawf")])
    k.op("vector", lambda e: e.tensor_copy(out=cfm[:, :], in_=crawf[:, :]),
         reads=[buf("crawf"), fence_a5], writes=[buf("cfm")])
    k.dma(7, lambda e: e.dma_start(out=crawf[0:64, :], in_=fsin_i),
          reads=[buf("cfm")], writes=[buf("crawf")])
    k.dma(7, lambda e: e.dma_start(out=crawf[64:128, :], in_=fsin_i),
          reads=[buf("cfm")], writes=[buf("crawf")])
    k.op("vector", lambda e: e.tensor_copy(out=sfm[:, :], in_=crawf[:, :]),
         reads=[buf("crawf")], writes=[buf("sfm")])

    def fraw_bufs(fr, o0, o1):
        return [buf(f"raw_{id(fr)}_{g}") for g in range(o0 // 128, o1 // 128)]

    def emit_ffn(tag, xt, x_bufs, fence_in, free_x):
        T = tag
        gF1 = Grp()
        xpbf = gF1.sb(T + "xpbf", [128, DC * W], BF16)
        rstdb = gF1.sb(T + "rstdb", [128, W])
        nmrsb = gF1.sb(T + "nmrsb", [128, W])
        statf = gF1.sb(T + "statf", [1, 3 * W])
        sqf = gF1.sb(T + "sqf", [128, W], BF16)
        xc = gF1.sb(T + "xc", [128, W])
        lgt = gF1.sb(T + "lgt", [128, 36])
        cscr = gF1.sb(T + "cscr", [128, 36])
        lohi = gF1.sb(T + "lohi", [1, 12])
        llocal = gF1.sb(T + "llocal", [1, 1024])
        lrow2 = gF1.sb(T + "lrow2", [1, 2 * 1024])
        idxi_row = gF1.sb(T + "idxi_row", [1, 1024], I32)
        trs = gF1.sb(T + "trs", [128, 128])
        scst = gF1.sb(T + "scst", [128, 2048], BF16)
        idxt = gF1.sb(T + "idxt", [128, 8], I32)
        TRS = buf(T + "trs")

        s0 = statf[0:1, 0:W]; s1 = statf[0:1, W:2 * W]; s2 = statf[0:1, 2 * W:3 * W]
        for dc in range(DC):
            k.op("vector", lambda e, dc=dc: e.tensor_copy(
                    out=xpbf[:, dc * W:(dc + 1) * W], in_=xt[:, dc * W:(dc + 1) * W]),
                 reads=[x_bufs[dc], fence_in], writes=[buf(T + f"xb_{dc}")])
            k.op("scalar", lambda e, dc=dc: e.activation(
                    out=sqf[:, :], in_=xpbf[:, dc * W:(dc + 1) * W], func=AF.Square),
                 reads=[buf(T + f"xb_{dc}")], writes=[buf(T + "sq")])
            for o in (0, 512):
                k.op("tensor", lambda e, dc=dc, o=o: e.matmul(
                        PS12[0:1, o:o + 512], ones_b[:, :],
                        xpbf[:, dc * W + o:dc * W + o + 512],
                        start=(dc == 0), stop=(dc == DC - 1)),
                     reads=[buf(T + f"xb_{dc}"), cONES_B], writes=[P12])
                k.op("tensor", lambda e, o=o, dc=dc: e.matmul(
                        PS3[0:1, o:o + 512], ones_b[:, :], sqf[:, o:o + 512],
                        start=(dc == 0), stop=(dc == DC - 1)),
                     reads=[buf(T + "sq"), cONES_B], writes=[P3])
        k.op("vector", lambda e: e.tensor_scalar_mul(s0, PS12[0:1, 0:W], 1.0 / D),
             reads=[P12], writes=[buf(T + "st0")])
        k.op("vector", lambda e: e.tensor_scalar_mul(s1, PS3[0:1, 0:W], 1.0 / D),
             reads=[P3], writes=[buf(T + "st1")])
        k.op("vector", lambda e: e.tensor_mul(s2, s0, s0),
             reads=[buf(T + "st0")], writes=[buf(T + "st2")])
        k.op("vector", lambda e: e.tensor_sub(s1, s1, s2),
             reads=[buf(T + "st1"), buf(T + "st2")], writes=[buf(T + "st1")])
        k.op("scalar", lambda e: e.activation(out=s1, in_=s1, func=AF.Sqrt,
                                              bias=cst[0:1, 0:1]),
             reads=[buf(T + "st1"), buf("cst")], writes=[buf(T + "st1")])
        k.op("vector", lambda e: e.reciprocal(s1, s1),
             reads=[buf(T + "st1")], writes=[buf(T + "st1")])
        k.op("vector", lambda e: e.tensor_mul(s0, s0, s1),
             reads=[buf(T + "st0"), buf(T + "st1")], writes=[buf(T + "st0")])
        k.op("vector", lambda e: e.tensor_scalar_mul(s0, s0, -1.0),
             reads=[buf(T + "st0")], writes=[buf(T + "st0")])
        for o in (0, 512):
            k.op("tensor", lambda e, o=o: e.matmul(
                    PS12[:, o:o + 512], ones_row[:, :], s1[0:1, o:o + 512],
                    start=True, stop=True),
                 reads=[cONES_R, buf(T + "st1")], writes=[P12])
            k.op("tensor", lambda e, o=o: e.matmul(
                    PS3[:, o:o + 512], ones_row[:, :], s0[0:1, o:o + 512],
                    start=True, stop=True),
                 reads=[cONES_R, buf(T + "st0")], writes=[P3])
        k.op("vector", lambda e: e.tensor_copy(out=rstdb[:, :], in_=PS12[:, 0:W]),
             reads=[P12], writes=[buf(T + "rstdb")])
        k.op("vector", lambda e: e.tensor_copy(out=nmrsb[:, :], in_=PS3[:, 0:W]),
             reads=[P3], writes=[buf(T + "nmrsb")])
        for dc in range(DC):
            k.op("vector", lambda e, dc=dc: e.tensor_mul(
                    xc[:, :], xt[:, dc * W:(dc + 1) * W], rstdb[:, :]),
                 reads=[x_bufs[dc], buf(T + "rstdb"), buf(T + "xcb")],
                 writes=[buf(T + "xcb")])
            k.op("vector", lambda e, dc=dc: e.tensor_add(xc[:, :], xc[:, :], nmrsb[:, :]),
                 reads=[buf(T + "xcb"), buf(T + "nmrsb")], writes=[buf(T + "xcb")])
            for o in (0, 512):
                k.op("tensor", lambda e, dc=dc, o=o: e.matmul(
                        PS12[0:1, o:o + 512], wrt_t[:, dc:dc + 1], xc[:, o:o + 512],
                        start=(dc == 0), stop=(dc == DC - 1)),
                     reads=[buf("wrt_t"), buf(T + "xcb")], writes=[P12])
            k.op("gpsimd", lambda e, dc=dc: e.tensor_copy(
                    out=xpbf[:, dc * W:(dc + 1) * W], in_=xc[:, :]),
                 reads=[buf(T + "xcb")], writes=[buf(T + f"xb_{dc}")])
            k.op("gpsimd", lambda e, dc=dc: e.tensor_add(
                    xt[:, dc * W:(dc + 1) * W], xt[:, dc * W:(dc + 1) * W], xc[:, :]),
                 reads=[buf(T + "xcb"), x_bufs[dc]], writes=[x_bufs[dc]])
            k.dma((dc % 4) * 2, lambda e, dc=dc: e.dma_start(
                    out=xst_dram[:, dc * W:(dc + 1) * W], in_=xt[:, dc * W:(dc + 1) * W]),
                  reads=[x_bufs[dc]], writes=[buf("xst")])
        xp_bufs = [buf(T + f"xb_{dc}") for dc in range(DC)]
        k.op("vector", lambda e: e.tensor_copy(out=llocal[:, :], in_=PS12[0:1, 0:1024]),
             reads=[P12], writes=[buf(T + "llocal")])
        k.dma(6, lambda e: e.dma_start(out=lg_in, in_=llocal[:, :]),
              reads=[buf(T + "llocal")], writes=[buf("lg_in")])
        k.cc(lambda e: e.collective_compute(
                "AllGather", ALU.bypass, replica_groups=GROUPS4,
                ins=[lg_in], outs=[lg_out]),
             reads=[buf("lg_in")], writes=[buf("lg_out")])
        k.dma(6, lambda e: e.dma_start(
                out=lgt[:, 0:32], in_=lg_out.rearrange("a (p c) -> (a p) c", p=128)),
              reads=[buf("lg_out")], writes=[buf(T + "lgt")])
        for ch in range(8):
            for hh in range(2):
                k.dma(2, lambda e, ch=ch, hh=hh: e.dma_start(
                        out=cpt_in[ch * 128:(ch + 1) * 128, hh * 1024:(hh + 1) * 1024],
                        in_=zeros_bw[:, :]),
                      reads=[cZER], writes=[buf("cpt_in")])
        k.op("vector", lambda e: e.memset(trs[:, :], 0.0), writes=[TRS])
        k.op("vector", lambda e: e.tensor_reduce(
                out=trs[:, 0:1], in_=lgt[:, 0:32], axis=AX.X, op=ALU.min),
             reads=[buf(T + "lgt")], writes=[TRS])
        k.op("vector", lambda e: e.reduce_max(out=trs[:, 32:33], in_=lgt[:, 0:32], axis=AX.X),
             reads=[buf(T + "lgt")], writes=[TRS])
        k.op("tensor", lambda e: e.transpose(PS3[:, 0:128], trs[:, :], identf[:, :]),
             reads=[TRS, buf("identf")], writes=[P3])
        k.op("vector", lambda e: e.tensor_reduce(
                out=lohi[0:1, 0:1], in_=PS3[0:1, 0:128], axis=AX.X, op=ALU.min),
             reads=[P3], writes=[buf(T + "lo")])
        k.op("vector", lambda e: e.reduce_max(
                out=lohi[0:1, 1:2], in_=PS3[32:33, 0:128], axis=AX.X),
             reads=[P3], writes=[buf(T + "hi")])
        k.op("vector", lambda e: e.tensor_scalar_add(lohi[0:1, 0:1], lohi[0:1, 0:1], -1.0),
             reads=[buf(T + "lo")], writes=[buf(T + "lo")])
        for it in range(48):
            k.op("vector", lambda e: e.tensor_add(lohi[0:1, 2:3], lohi[0:1, 0:1],
                                                  lohi[0:1, 1:2]),
                 reads=[buf(T + "lo"), buf(T + "hi")], writes=[buf(T + "mid")])
            k.op("vector", lambda e: e.tensor_scalar_mul(lohi[0:1, 2:3], lohi[0:1, 2:3], 0.5),
                 reads=[buf(T + "mid")], writes=[buf(T + "mid")])
            k.op("tensor", lambda e: e.matmul(PS3[:, 140:141], ones_row[:, :],
                                              lohi[0:1, 2:3], start=True, stop=True),
                 reads=[cONES_R, buf(T + "mid")], writes=[P3])
            k.op("vector", lambda e: e.tensor_scalar(
                    out=cscr[:, 0:32], in0=lgt[:, 0:32], scalar1=PS3[:, 140:141],
                    scalar2=0.0, op0=ALU.is_gt, op1=ALU.add, accum_out=cscr[:, 34:35]),
                 reads=[buf(T + "lgt"), P3], writes=[buf(T + "csum")])
            k.op("tensor", lambda e: e.matmul(PS3[0:1, 141:142], cscr[:, 34:35],
                                              ones_f[:, :], start=True, stop=True),
                 reads=[buf(T + "csum"), cONESF], writes=[P3])
            k.op("vector", lambda e: e.tensor_single_scalar(
                    out=lohi[0:1, 3:4], in_=PS3[0:1, 141:142], scalar=1023.5,
                    op=ALU.is_gt),
                 reads=[P3], writes=[buf(T + "ge")])
            k.op("vector", lambda e: e.tensor_sub(lohi[0:1, 4:5], lohi[0:1, 2:3],
                                                  lohi[0:1, 0:1]),
                 reads=[buf(T + "mid"), buf(T + "lo")], writes=[buf(T + "dm")])
            k.op("vector", lambda e: e.scalar_tensor_tensor(
                    out=lohi[0:1, 0:1], in0=lohi[0:1, 4:5], scalar=lohi[0:1, 3:4],
                    in1=lohi[0:1, 0:1], op0=ALU.mult, op1=ALU.add),
                 reads=[buf(T + "dm"), buf(T + "ge"), buf(T + "lo")],
                 writes=[buf(T + "lo")])
            k.op("vector", lambda e: e.tensor_scalar(
                    out=lohi[0:1, 5:6], in0=lohi[0:1, 3:4], scalar1=-1.0, scalar2=1.0,
                    op0=ALU.mult, op1=ALU.add),
                 reads=[buf(T + "ge")], writes=[buf(T + "omge")])
            k.op("vector", lambda e: e.tensor_sub(lohi[0:1, 6:7], lohi[0:1, 2:3],
                                                  lohi[0:1, 1:2]),
                 reads=[buf(T + "mid"), buf(T + "hi")], writes=[buf(T + "dmh")])
            k.op("vector", lambda e: e.scalar_tensor_tensor(
                    out=lohi[0:1, 1:2], in0=lohi[0:1, 6:7], scalar=lohi[0:1, 5:6],
                    in1=lohi[0:1, 1:2], op0=ALU.mult, op1=ALU.add),
                 reads=[buf(T + "dmh"), buf(T + "omge"), buf(T + "hi")],
                 writes=[buf(T + "hi")])
        k.op("tensor", lambda e: e.matmul(PS3[:, 140:141], ones_row[:, :],
                                          lohi[0:1, 0:1], start=True, stop=True),
             reads=[cONES_R, buf(T + "lo")], writes=[P3])
        k.op("vector", lambda e: e.tensor_copy(out=cscr[:, 35:36], in_=PS3[:, 140:141]),
             reads=[P3], writes=[buf(T + "lobc")])
        k.op("vector", lambda e: e.tensor_scalar(
                out=cscr[:, 0:32], in0=lgt[:, 0:32], scalar1=cscr[:, 35:36], scalar2=None,
                op0=ALU.is_gt),
             reads=[buf(T + "lgt"), buf(T + "lobc")], writes=[buf(T + "mglob")])
        k.op("tensor", lambda e: e.matmul(PS3[0:4, 256:288], blk_t[:, :], cscr[:, 0:32],
                                          start=True, stop=True),
             reads=[buf("blk_t"), buf(T + "mglob")], writes=[P3])
        k.op("vector", lambda e: e.reduce_sum(out=trs[0:4, 1:2], in_=PS3[0:4, 256:288],
                                              axis=AX.X),
             reads=[P3], writes=[TRS])
        k.op("tensor", lambda e: e.matmul(PS3[0:1, 142:143], trs[0:4, 1:2],
                                          qsel4_t[:, :], start=True, stop=True),
             reads=[TRS, buf("qsel4_t")], writes=[P3])
        k.op("vector", lambda e: e.tensor_scalar_add(lohi[0:1, 7:8], PS3[0:1, 142:143],
                                                     -1.0),
             reads=[P3], writes=[buf(T + "cb1")])
        k.op("vector", lambda e: e.tensor_scalar(
                out=cscr[:, 0:32], in0=cscr[:, 0:32], scalar1=1.0e9, scalar2=-1.0e9,
                op0=ALU.mult, op1=ALU.add),
             reads=[buf(T + "mglob")], writes=[buf(T + "mglob")])
        k.op("vector", lambda e: e.tensor_add(cscr[:, 0:32], cscr[:, 0:32], lgt[:, 0:32]),
             reads=[buf(T + "mglob"), buf(T + "lgt")], writes=[buf(T + "mlg")])
        k.op("vector", lambda e: e.reduce_max(out=trs[:, 64:65], in_=cscr[:, 0:32],
                                              axis=AX.X),
             reads=[buf(T + "mlg")], writes=[TRS])
        k.op("tensor", lambda e: e.transpose(PS3[:, 0:128], trs[:, :], identf[:, :]),
             reads=[TRS, buf("identf")], writes=[P3])
        k.op("vector", lambda e: e.reduce_max(out=lohi[0:1, 8:9], in_=PS3[64:65, 0:128],
                                              axis=AX.X),
             reads=[P3], writes=[buf(T + "Mv")])
        k.op("vector", lambda e: e.tensor_scalar_mul(lohi[0:1, 9:10], lohi[0:1, 8:9], -1.0),
             reads=[buf(T + "Mv")], writes=[buf(T + "Mneg")])
        k.op("tensor", lambda e: e.matmul(PS3[:, 140:141], ones_row[:, :], lohi[0:1, 9:10],
                                          start=True, stop=True),
             reads=[cONES_R, buf(T + "Mneg")], writes=[P3])
        k.op("vector", lambda e: e.tensor_copy(out=cscr[:, 35:36], in_=PS3[:, 140:141]),
             reads=[P3, buf(T + "lobc")], writes=[buf(T + "Mnb")])
        k.op("scalar", lambda e: e.activation(
                out=cscr[:, 0:32], in_=cscr[:, 0:32], func=AF.Exp, bias=cscr[:, 35:36],
                accum_out=cscr[:, 34:35]),
             reads=[buf(T + "mlg"), buf(T + "Mnb"), buf(T + "csum")],
             writes=[buf(T + "esum")]),
        k.op("tensor", lambda e: e.matmul(PS3[0:1, 141:142], cscr[:, 34:35], ones_f[:, :],
                                          start=True, stop=True),
             reads=[buf(T + "esum"), cONESF], writes=[P3])
        k.op("vector", lambda e: e.reciprocal(lohi[0:1, 10:11], PS3[0:1, 141:142]),
             reads=[P3], writes=[buf(T + "Zi")])
        k.op("vector", lambda e: e.tensor_single_scalar(
                out=lrow2[0:1, 0:1024], in_=llocal[:, :], scalar=lohi[0:1, 0:1],
                op=ALU.is_gt),
             reads=[buf(T + "llocal"), buf(T + "lo")], writes=[buf(T + "mmy")])
        k.op("vector", lambda e: e.tensor_tensor_scan(
                out=lrow2[0:1, 1024:2048], data0=fones[:, :], data1=lrow2[0:1, 0:1024],
                initial=0.0, op0=ALU.mult, op1=ALU.add),
             reads=[buf(T + "mmy"), buf("fones")], writes=[buf(T + "cmy")])
        k.op("vector", lambda e: e.tensor_single_scalar(
                out=lrow2[0:1, 1024:2048], in_=lrow2[0:1, 1024:2048], scalar=lohi[0:1, 7:8],
                op=ALU.add),
             reads=[buf(T + "cmy"), buf(T + "cb1")], writes=[buf(T + "cmy")])
        k.op("vector", lambda e: e.tensor_scalar(
                out=lrow2[0:1, 0:1024], in0=lrow2[0:1, 0:1024], scalar1=-4096.0,
                scalar2=4096.0, op0=ALU.mult, op1=ALU.add),
             reads=[buf(T + "mmy")], writes=[buf(T + "mmy")])
        k.op("vector", lambda e: e.tensor_add(
                lrow2[0:1, 1024:2048], lrow2[0:1, 1024:2048], lrow2[0:1, 0:1024]),
             reads=[buf(T + "cmy"), buf(T + "mmy")], writes=[buf(T + "cmy")])
        k.op("vector", lambda e: e.tensor_copy(out=idxi_row[0:1, :],
                                               in_=lrow2[0:1, 1024:2048]),
             reads=[buf(T + "cmy")], writes=[buf(T + "idxi")])
        k.dma(6, lambda e: e.dma_start(out=idx_dram, in_=idxi_row[0:1, :]),
              reads=[buf(T + "idxi")], writes=[buf("idx_dram")])
        k.op("scalar", lambda e: e.activation(
                out=lrow2[0:1, 0:1024], in_=llocal[:, :], func=AF.Exp,
                bias=lohi[0:1, 9:10]),
             reads=[buf(T + "llocal"), buf(T + "Mneg")],
             writes=[buf(T + "rw"), buf(T + "mmy")])
        k.op("vector", lambda e: e.tensor_single_scalar(
                out=lrow2[0:1, 0:1024], in_=lrow2[0:1, 0:1024], scalar=lohi[0:1, 10:11],
                op=ALU.mult),
             reads=[buf(T + "rw"), buf(T + "Zi")], writes=[buf(T + "rw")])
        k.dma(6, lambda e: e.dma_start(out=rw_dram, in_=lrow2[0:1, 0:1024]),
              reads=[buf(T + "rw")], writes=[buf("rw_dram")])
        for ch in range(8):
            k.dma(6, lambda e, ch=ch: e.dma_start(
                    out=idxt[:, ch:ch + 1],
                    in_=idx_dram[0:1, ch * 128:(ch + 1) * 128].rearrange(
                        "a (p c) -> (a p) c", p=128)),
                  reads=[buf("idx_dram")], writes=[buf(T + "idxt")])
            for dc in range(DC):
                col = (dc % 8) * 128
                k.op("tensor", lambda e, ch=ch, dc=dc, col=col: e.transpose(
                        PSB[:, col:col + 128],
                        xpbf[:, dc * W + ch * 128:dc * W + (ch + 1) * 128],
                        identb[:, :]),
                     reads=xp_bufs + [buf("identb")],
                     writes=[PB] if dc in (0, 7, 8, 15) else (), inc=dc in (7, 15))
                if dc in (7, 15):
                    for d2 in range(dc - 7, dc + 1):
                        k.op("vector", lambda e, d2=d2: e.tensor_copy(
                                out=scst[:, d2 * 128:(d2 + 1) * 128],
                                in_=PSB[:, (d2 % 8) * 128:(d2 % 8 + 1) * 128]),
                             reads=[PB], writes=[buf(T + "scst")])
            k.dma(5, lambda e, ch=ch: e.indirect_dma_start(
                    out=cpt_in, out_offset=IndirectOffsetOnAxis(idxt[:, ch:ch + 1], 0),
                    in_=scst[:, :], in_offset=None,
                    bounds_check=1023, oob_is_err=False),
                  reads=[buf(T + "scst"), buf(T + "idxt"), buf("cpt_in")],
                  writes=[buf("cpt_in")])
        k.cc(lambda e: e.collective_compute(
                "AllReduce", ALU.add, replica_groups=GROUPS4,
                ins=[cpt_in], outs=[cpt_out]),
             reads=[buf("cpt_in")], writes=[buf("cpt_out")])

        fence_f1 = fence(xp_bufs + list(x_bufs) + [buf(T + n) for n in
                         ("rstdb", "nmrsb", "st0", "st1", "st2", "sq", "xcb",
                          "lgt", "llocal", "trs", "scst", "mglob",
                          "mlg", "esum", "csum", "idxt", "idxi", "mmy", "cmy", "rw",
                          "lo", "hi", "mid", "ge", "dm", "omge", "dmh", "cb1",
                          "Mv", "Mneg", "Zi", "lobc", "Mnb")])
        gF1.free()
        free_x()

        gFil = Grp()
        filt = gFil.sb(T + "filt", [128, DC * W], BF16)
        gFA = Grp()
        hn = gFA.sb(T + "hn", [128, DC * W], BF16)
        qr = gFA.sb(T + "qr", [128, 4 * W], BF16)
        kr = gFA.sb(T + "kr", [128, 4 * W], BF16)
        vtok = gFA.sb(T + "vtok", [128, 8 * 512], BF16)
        obuf = gFA.sb(T + "obuf", [128, 4 * W], BF16)
        PT = gFA.sb(T + "PT", [128, W], BF16)
        Pb = gFA.sb(T + "Pb", [128, W], BF16)
        PTd = gFA.sb(T + "PTd", [128, W], BF16)
        Pbd = gFA.sb(T + "Pbd", [128, W], BF16)
        msk2 = gFA.sb(T + "msk2", [128, W])
        wobf = gFA.sb(T + "wobf", [128, 512], BF16)
        ftst = gFA.sb(T + "ftst", [128, 2048], BF16)
        fraw = gFA.sb(T + "fraw", [128, DC * 128])
        fbf0 = gFA.sb(T + "fbf0", [128, DC * 128], BF16)
        fbf1 = gFA.sb(T + "fbf1", [128, DC * 128], BF16)
        atst = gFA.sb(T + "atst", [128, W], BF16)
        rsc = gFA.sb(T + "rsc", [128, W])
        lrow_f = gFA.sb(T + "lrow_f", [1, W])
        rstb2 = gFA.sb(T + "rstb2", [128, W])
        smx = gFA.sb(T + "smx", [128, 4])
        gWv = Grp()
        wvres = gWv.sb(T + "wvres", [128, 16 * 256], BF16)
        qraw = e1
        rwork = e0[0:64, :]
        rwork2 = e0[64:128, :]

        for ch in range(8):
            k.dma(2, lambda e, ch=ch: e.dma_start(
                    out=ftst[:, :], in_=cpt_out[ch * 128:(ch + 1) * 128, :]),
                  reads=[buf("cpt_out"), fence_f1, buf(T + "ftstB")],
                  writes=[buf(T + "ftstB")])
            for dc in range(DC):
                col = (dc % 8) * 128
                k.op("tensor", lambda e, dc=dc, col=col: e.transpose(
                        PSB[:, col:col + 128], ftst[:, dc * 128:(dc + 1) * 128],
                        identb[:, :]),
                     reads=[buf(T + "ftstB"), buf("identb")],
                     writes=[PB] if dc in (0, 7, 8, 15) else (), inc=dc in (7, 15))
                if dc in (7, 15):
                    for d2 in range(dc - 7, dc + 1):
                        k.op("vector", lambda e, d2=d2, ch=ch: e.tensor_copy(
                                out=filt[:, d2 * W + ch * 128:d2 * W + (ch + 1) * 128],
                                in_=PSB[:, (d2 % 8) * 128:(d2 % 8 + 1) * 128]),
                             reads=[PB, fence_f1], writes=[buf(T + f"filt_{d2}")])
        filt_bufs = [buf(T + f"filt_{dc}") for dc in range(DC)]

        for dc in range(DC):
            k.op("scalar", lambda e, dc=dc: e.activation(
                    out=Pb[:, :], in_=filt[:, dc * W:(dc + 1) * W], func=AF.Square),
                 reads=[filt_bufs[dc]], writes=[buf(T + "Pb")])
            for o in (0, 512):
                k.op("tensor", lambda e, o=o, dc=dc: e.matmul(
                        PS3[0:1, o:o + 512], ones_b[:, :], Pb[:, o:o + 512],
                        start=(dc == 0), stop=(dc == DC - 1)),
                     reads=[buf(T + "Pb"), cONES_B], writes=[P3])
        k.op("vector", lambda e: e.tensor_scalar_mul(lrow_f[0:1, :], PS3[0:1, 0:W], 1.0 / D),
             reads=[P3], writes=[buf(T + "ms1")])
        k.op("scalar", lambda e: e.activation(out=lrow_f[0:1, :], in_=lrow_f[0:1, :],
                                              func=AF.Sqrt, bias=cst[0:1, 1:2]),
             reads=[buf(T + "ms1"), buf("cst")], writes=[buf(T + "ms1")])
        k.op("vector", lambda e: e.reciprocal(lrow_f[0:1, :], lrow_f[0:1, :]),
             reads=[buf(T + "ms1")], writes=[buf(T + "ms1")])
        for o in (0, 512):
            k.op("tensor", lambda e, o=o: e.matmul(
                    PS12[:, o:o + 512], ones_row[:, :], lrow_f[0:1, o:o + 512],
                    start=True, stop=True),
                 reads=[cONES_R, buf(T + "ms1")], writes=[P12])
        k.op("vector", lambda e: e.tensor_copy(out=rstb2[:, :], in_=PS12[:, 0:W]),
             reads=[P12], writes=[buf(T + "rstb2")])
        for dc in range(DC):
            k.op("vector", lambda e, dc=dc: e.scalar_tensor_tensor(
                    out=hn[:, dc * W:(dc + 1) * W], in0=filt[:, dc * W:(dc + 1) * W],
                    scalar=g1_t[:, dc:dc + 1], in1=rstb2[:, :],
                    op0=ALU.mult, op1=ALU.mult),
                 reads=[filt_bufs[dc], buf("g1_t"), buf(T + "rstb2")],
                 writes=[buf(T + f"hn_{dc}")])
        hn_bufs = [buf(T + f"hn_{dc}") for dc in range(DC)]

        def qk_head(wsrc, dst, dname, pendq, nxt):
            for mh in range(4):
                wbf = [fbf0, fbf1][mh % 2]
                wbufs = pendq.pop(mh)
                if mh + 1 < 4:
                    pendq[mh + 1] = stream_weight(T + f"{dname}{(mh + 1) % 2}", wsrc,
                                                  mh + 1, DC, fraw,
                                                  [fbf0, fbf1][(mh + 1) % 2])
                elif nxt is not None:
                    pendq[0] = stream_weight(T + "k0", nxt, 0, DC, fraw, fbf0)
                ps2, pb2 = [(PS12, P12), (PS3, P3)][mh % 2]
                mm_acc(ps2, pb2, wbf, wbufs, DC,
                       lambda kc, o, cur: hn[:, kc * W + o:kc * W + o + cur], W, hn_bufs)
                k.op("scalar", lambda e, ps2=ps2: e.activation(
                        out=qraw[:, :], in_=ps2[:, 0:W], func=AF.Copy),
                     reads=[pb2, buf("e0"), buf(T + "rsc")], writes=[buf("e1")])
                k.op("vector", lambda e: e.tensor_mul(e0[0:64, :], qraw[0:64, :],
                                                      cfm[0:64, :]),
                     reads=[buf("e1"), buf("cfm"), buf("e0")], writes=[buf("e0")])
                k.op("gpsimd", lambda e: e.tensor_mul(rsc[0:64, :], qraw[64:128, :],
                                                      sfm[64:128, :]),
                     reads=[buf("e1"), buf("sfm"), buf(T + "rsc")], writes=[buf(T + "rsc")])
                k.op("vector", lambda e, mh=mh, dst=dst: e.tensor_sub(
                        dst[0:64, mh * W:(mh + 1) * W], e0[0:64, :], rsc[0:64, :]),
                     reads=[buf("e0"), buf(T + "rsc")],
                     writes=[buf(T + f"{dname}_{mh}")])
                k.op("gpsimd", lambda e: e.tensor_mul(e0[0:64, :], qraw[0:64, :],
                                                      sfm[0:64, :]),
                     reads=[buf("e1"), buf("sfm"), buf("e0")],
                     writes=[buf("e0")])
                k.op("vector", lambda e: e.tensor_mul(rsc[0:64, :], qraw[64:128, :],
                                                      cfm[64:128, :]),
                     reads=[buf("e1"), buf("cfm"), buf(T + "rsc")],
                     writes=[buf(T + "rsc")])
                k.op("vector", lambda e, mh=mh, dst=dst: e.tensor_add(
                        dst[64:128, mh * W:(mh + 1) * W], e0[0:64, :], rsc[0:64, :]),
                     reads=[buf("e0"), buf(T + "rsc")],
                     writes=[buf(T + f"{dname}_{mh}")])

        pendq = {0: stream_weight(T + "qr0", fwq, 0, DC, fraw, fbf0)}
        qk_head(fwq, qr, "qr", pendq, fwk)
        qk_head(fwk, kr, "kr", pendq, None)
        qr_bufs = [buf(T + f"qr_{mh}") for mh in range(4)]
        kr_bufs = [buf(T + f"kr_{mh}") for mh in range(4)]

        for hf in range(2):
            for j in range(4):
                ho = (j % 2) * 1024
                k.dma((2 * hf + j) % 4, lambda e, j=j, hf=hf, ho=ho: e.dma_start(
                        out=fraw[:, ho:ho + 1024], in_=fwv[hf, j]),
                      writes=fraw_bufs(fraw, ho, ho + 1024))
                k.op("scalar", lambda e, j=j, ho=ho: e.activation(
                        out=wvres[:, j * 1024:(j + 1) * 1024], in_=fraw[:, ho:ho + 1024],
                        func=AF.Copy),
                     reads=fraw_bufs(fraw, ho, ho + 1024), writes=[buf(T + "wvr")])
            for tc in range(8):
                ps2, pb2 = [(PS12, P12), (PS3, P3)][tc % 2]
                for kc in range(DC):
                    k.op("tensor", lambda e, tc=tc, kc=kc, ps2=ps2: e.matmul(
                            ps2[:, 1024:1280],
                            hn[:, kc * W + tc * 128:kc * W + (tc + 1) * 128],
                            wvres[:, kc * 256:(kc + 1) * 256],
                            start=(kc == 0), stop=(kc == DC - 1)),
                         reads=hn_bufs + [buf(T + "wvr")],
                         writes=[pb2] if kc in (0, DC - 1) else (), inc=kc == DC - 1)
                k.op("vector", lambda e, tc=tc, hf=hf, ps2=ps2: e.tensor_copy(
                        out=vtok[:, tc * 512 + hf * 256:tc * 512 + (hf + 1) * 256],
                        in_=ps2[:, 1024:1280]),
                     reads=[pb2], writes=[buf(T + f"vt_{tc}")])
        vt_bufs = [buf(T + f"vt_{tc}") for tc in range(8)]
        fence_wv = fence([buf(T + "wvr")])
        gWv.free()

        fmsks = [e0, msk2]
        fheads = [(qt, h) for qt in range(8) for h in range(4)]
        NFH = len(fheads)

        def f_qk(i):
            qt, h = fheads[i]
            if h == 0:
                mk = fmsks[qt % 2]
                mkb = [buf("e0"), buf(T + "msk2")][qt % 2]
                k.dma(2, lambda e, qt=qt, mk=mk: e.dma_start(
                        out=mk[:, :], in_=fmask[qt * 128:(qt + 1) * 128, :]),
                      reads=[mkb], writes=[mkb])
            ps2, pb2 = [(PS12, P12), (PS3, P3)][i % 2]
            for o in (0, 512):
                k.op("tensor", lambda e, qt=qt, h=h, o=o, ps2=ps2: e.matmul(
                        ps2[:, o:o + 512],
                        qr[:, h * W + qt * 128:h * W + (qt + 1) * 128],
                        kr[:, h * W + o:h * W + o + 512],
                        start=True, stop=True),
                     reads=[qr_bufs[h], kr_bufs[h]],
                     writes=[pb2], inc=o == 512)

        def f_chain(i):
            qt, h = fheads[i]
            ps2, pb2 = [(PS12, P12), (PS3, P3)][i % 2]
            st = [e1, rsc][i % 2]
            stb = [buf("e1"), buf(T + "rsc")][i % 2]
            mk = fmsks[qt % 2]
            mkb = [buf("e0"), buf(T + "msk2")][qt % 2]
            Pbx = [Pb, Pbd][i % 2]
            BPB = buf(T + "Pb" + "d" * (i % 2))
            mxc = 2 * (i % 2)
            k.op("vector", lambda e, ps2=ps2, st=st, mk=mk: e.scalar_tensor_tensor(
                    out=st[:, :], in0=ps2[:, 0:W], scalar=0.08838834764831845,
                    in1=mk[:, :], op0=ALU.mult, op1=ALU.add),
                 reads=[pb2, mkb], writes=[stb])
            k.op("vector", lambda e, st=st, mxc=mxc: e.reduce_max(
                    out=smx[:, mxc:mxc + 1], in_=st[:, :], axis=AX.X, negate=True),
                 reads=[stb], writes=[buf(T + f"nmx{i % 2}")])
            k.op("scalar", lambda e, st=st, Pbx=Pbx, mxc=mxc: e.activation(
                    out=Pbx[:, :], in_=st[:, :], func=AF.Exp, bias=smx[:, mxc:mxc + 1],
                    accum_out=smx[:, mxc + 1:mxc + 2]),
                 reads=[stb, buf(T + f"nmx{i % 2}"), BPB], writes=[BPB])
            k.op("vector", lambda e, mxc=mxc: e.reciprocal(
                    smx[:, mxc + 1:mxc + 2], smx[:, mxc + 1:mxc + 2]),
                 reads=[BPB], writes=[buf(T + f"rsum{i % 2}")])
            k.op("scalar", lambda e, Pbx=Pbx, mxc=mxc: e.activation(
                    out=Pbx[:, :], in_=Pbx[:, :], func=AF.Copy,
                    scale=smx[:, mxc + 1:mxc + 2]),
                 reads=[BPB, buf(T + f"rsum{i % 2}")], writes=[BPB])

        def f_tail(i):
            qt, h = fheads[i]
            ps2, pb2 = [(PS12, P12), (PS3, P3)][i % 2]
            Pbx = [Pb, Pbd][i % 2]
            PTx = [PT, PTd][i % 2]
            BPB = buf(T + "Pb" + "d" * (i % 2))
            BPT = buf(T + "PT" + "d" * (i % 2))
            for c in range(8):
                k.op("tensor", lambda e, c=c, Pbx=Pbx: e.transpose(
                        PSB[:, c * 128:(c + 1) * 128], Pbx[:, c * 128:(c + 1) * 128],
                        identb[:, :]),
                     reads=[BPB, buf("identb")],
                     writes=[PB] if c in (0, 7) else (), inc=c == 7)
            k.op("vector", lambda e, PTx=PTx: e.tensor_copy(out=PTx[:, :], in_=PSB[:, 0:W]),
                 reads=[PB], writes=[BPT])
            for c in range(8):
                k.op("tensor", lambda e, c=c, h=h, ps2=ps2, PTx=PTx: e.matmul(
                        ps2[:, 1024:1152],
                        vtok[:, c * 512 + h * 128:c * 512 + (h + 1) * 128],
                        PTx[:, c * 128:(c + 1) * 128],
                        start=(c == 0), stop=(c == 7)),
                     reads=[vt_bufs[c], BPT],
                     writes=[pb2] if c in (0, 7) else (), inc=c == 7)
            k.op("vector", lambda e, h=h, qt=qt, ps2=ps2: e.tensor_copy(
                    out=obuf[:, h * W + qt * 128:h * W + (qt + 1) * 128],
                    in_=ps2[:, 1024:1152]),
                 reads=[pb2], writes=[buf(T + f"ob_{h}")])

        f_qk(0)
        for i in range(NFH):
            if i + 1 < NFH:
                f_qk(i + 1)
            f_chain(i)
            if i >= 1:
                f_tail(i - 1)
        f_tail(NFH - 1)
        ob_bufs = [buf(T + f"ob_{h}") for h in range(4)]

        for mc in range(DC):
            k.dma(mc % 4, lambda e, mc=mc: e.dma_start(out=fraw[:, 0:512], in_=fwo[mc]),
                  reads=[buf(T + "wor")], writes=fraw_bufs(fraw, 0, 512))
            k.op("scalar", lambda e: e.activation(out=wobf[:, :], in_=fraw[:, 0:512],
                                                  func=AF.Copy),
                 reads=fraw_bufs(fraw, 0, 512), writes=[buf(T + "wor")])
            ps2, pb2 = [(PS12, P12), (PS3, P3)][mc % 2]
            for kc in range(4):
                for o in (0, 512):
                    k.op("tensor", lambda e, kc=kc, o=o, ps2=ps2: e.matmul(
                            ps2[:, o:o + 512], wobf[:, kc * 128:(kc + 1) * 128],
                            obuf[:, kc * W + o:kc * W + o + 512],
                            start=(kc == 0), stop=(kc == 3)),
                         reads=[buf(T + "wor")] + ob_bufs,
                         writes=[pb2] if (kc == 0 and o == 0) or (kc == 3 and o == 512)
                         else (), inc=(kc == 3 and o == 512))
            k.op("vector", lambda e, ps2=ps2: e.tensor_copy(out=atst[:, :], in_=ps2[:, 0:W]),
                 reads=[pb2], writes=[buf(T + "atst")])
            k.dma(6, lambda e, mc=mc: e.dma_start(
                    out=at_in[mc * 128:(mc + 1) * 128, :], in_=atst[:, :]),
                  reads=[buf(T + "atst")], writes=[buf("at_in")])
        k.cc(lambda e: e.collective_compute(
                "AllReduce", ALU.add, replica_groups=GROUPS4,
                ins=[at_in], outs=[at_out]),
             reads=[buf("at_in")], writes=[buf("at_out")])
        for dc in range(DC):
            k.dma(2, lambda e, dc=dc: e.dma_start(
                    out=atst[:, :], in_=at_out[dc * 128:(dc + 1) * 128, :]),
                  reads=[buf("at_out"), buf(T + "atst")], writes=[buf(T + "atst")])
            k.op("vector", lambda e, dc=dc: e.tensor_add(
                    filt[:, dc * W:(dc + 1) * W], filt[:, dc * W:(dc + 1) * W],
                    atst[:, :]),
                 reads=[filt_bufs[dc], buf(T + "atst")], writes=[filt_bufs[dc]])

        fence_fa = fence(hn_bufs + qr_bufs + kr_bufs + vt_bufs + ob_bufs +
                         [buf("e0"), buf("e1"), fence_wv] +
                         [buf(T + n) for n in ("PT", "Pb", "PTd", "Pbd", "msk2",
                                               "rsc", "wor", "ftstB", "atst",
                                               "nmx0", "nmx1",
                                               "rsum0", "rsum1", "ms1", "rstb2")])
        gFA.free()

        gFB = Grp()
        h2 = gFB.sb(T + "h2", [128, DC * W], BF16)
        gbuf = gFB.sb(T + "g", [128, 16 * W], BF16)
        w2h = gFB.sb(T + "w2h", [128, 16 * 512], BF16)
        fraw2 = gFB.sb(T + "fraw2", [128, DC * 128])
        fbf20 = gFB.sb(T + "fbf20", [128, DC * 128], BF16)
        fbf21 = gFB.sb(T + "fbf21", [128, DC * 128], BF16)
        fbf22 = gFB.sb(T + "fbf22", [128, DC * 128], BF16)
        sq2 = gFB.sb(T + "sq2", [128, W], BF16)
        rstb3 = gFB.sb(T + "rstb3", [128, W])
        lrowb = gFB.sb(T + "lrowb", [1, W])
        tst2 = gFB.sb(T + "tst2", [128, 2048], BF16)

        for dc in range(DC):
            k.op("scalar", lambda e, dc=dc: e.activation(
                    out=sq2[:, :], in_=filt[:, dc * W:(dc + 1) * W], func=AF.Square),
                 reads=[filt_bufs[dc], fence_fa], writes=[buf(T + "sq2")])
            for o in (0, 512):
                k.op("tensor", lambda e, o=o, dc=dc: e.matmul(
                        PS3[0:1, o:o + 512], ones_b[:, :], sq2[:, o:o + 512],
                        start=(dc == 0), stop=(dc == DC - 1)),
                     reads=[buf(T + "sq2"), cONES_B], writes=[P3])
        k.op("vector", lambda e: e.tensor_scalar_mul(lrowb[:, :], PS3[0:1, 0:W], 1.0 / D),
             reads=[P3], writes=[buf(T + "ms2")])
        k.op("scalar", lambda e: e.activation(out=lrowb[:, :], in_=lrowb[:, :],
                                              func=AF.Sqrt, bias=cst[0:1, 1:2]),
             reads=[buf(T + "ms2"), buf("cst")], writes=[buf(T + "ms2")])
        k.op("vector", lambda e: e.reciprocal(lrowb[:, :], lrowb[:, :]),
             reads=[buf(T + "ms2")], writes=[buf(T + "ms2")])
        for o in (0, 512):
            k.op("tensor", lambda e, o=o: e.matmul(
                    PS12[:, o:o + 512], ones_row[:, :], lrowb[0:1, o:o + 512],
                    start=True, stop=True),
                 reads=[cONES_R, buf(T + "ms2")], writes=[P12])
        k.op("vector", lambda e: e.tensor_copy(out=rstb3[:, :], in_=PS12[:, 0:W]),
             reads=[P12], writes=[buf(T + "rstb3")])
        for dc in range(DC):
            k.op("vector", lambda e, dc=dc: e.scalar_tensor_tensor(
                    out=h2[:, dc * W:(dc + 1) * W], in0=filt[:, dc * W:(dc + 1) * W],
                    scalar=g2_t[:, dc:dc + 1], in1=rstb3[:, :],
                    op0=ALU.mult, op1=ALU.mult),
                 reads=[filt_bufs[dc], buf("g2_t"), buf(T + "rstb3"), fence_fa],
                 writes=[buf(T + f"h2_{dc}")])
        h2_bufs = [buf(T + f"h2_{dc}") for dc in range(DC)]

        fbrot = [fbf20, fbf21, fbf22]
        pend1 = {0: stream_weight(T + "w10", fw1, 0, DC, fraw2, fbrot[0], "scalar",
                                  extra=[fence_fa])}
        pend3 = {0: stream_weight(T + "w30", fw3, 0, DC, fraw2, fbrot[1], "scalar",
                                  extra=[fence_fa], soff=1)}
        for m in range(16):
            wb1 = fbrot[(2 * m) % 3]
            wb3 = fbrot[(2 * m + 1) % 3]
            wbufs1 = pend1.pop(m)
            wbufs3 = pend3.pop(m)
            mm_acc(PS12, P12, wb1, wbufs1, DC,
                   lambda kc, o, cur: h2[:, kc * W + o:kc * W + o + cur], W, h2_bufs)
            if m + 1 < 16:
                pend1[m + 1] = stream_weight(T + f"w1{(m + 1) % 2}", fw1, m + 1, DC,
                                             fraw2, fbrot[(2 * m + 2) % 3], "scalar")
                pend3[m + 1] = stream_weight(T + f"w3{(m + 1) % 2}", fw3, m + 1, DC,
                                             fraw2, fbrot[(2 * m + 3) % 3], "scalar",
                                             soff=1)
            mm_acc(PS3, P3, wb3, wbufs3, DC,
                   lambda kc, o, cur: h2[:, kc * W + o:kc * W + o + cur], W, h2_bufs)
            k.op("scalar", lambda e: e.activation(out=e0[:, :], in_=PS12[:, 0:W],
                                                  func=AF.Silu),
                 reads=[P12], writes=[buf("e0")])
            k.op("vector", lambda e, m=m: e.tensor_mul(
                    gbuf[:, m * W:(m + 1) * W], e0[:, :], PS3[:, 0:W]),
                 reads=[buf("e0"), P3], writes=[buf(T + f"g_{m}")])
        g_bufs = [buf(T + f"g_{m}") for m in range(16)]

        for quar in range(4):
            for j in range(8):
                ho = (j % 2) * 1024
                k.dma(j % 4, lambda e, j=j, quar=quar, ho=ho: e.dma_start(
                        out=fraw2[:, ho:ho + 1024], in_=fw2[quar, j]),
                      writes=fraw_bufs(fraw2, ho, ho + 1024))
                k.op("scalar", lambda e, j=j, ho=ho: e.activation(
                        out=w2h[:, j * 1024:(j + 1) * 1024], in_=fraw2[:, ho:ho + 1024],
                        func=AF.Copy),
                     reads=fraw_bufs(fraw2, ho, ho + 1024), writes=[buf(T + "w2c")])
            for tc in range(8):
                ps2, pb2 = [(PS12, P12), (PS3, P3)][tc % 2]
                for kc in range(16):
                    k.op("tensor", lambda e, tc=tc, kc=kc, ps2=ps2: e.matmul(
                            ps2[:, 0:512],
                            gbuf[:, kc * W + tc * 128:kc * W + (tc + 1) * 128],
                            w2h[:, kc * 512:(kc + 1) * 512],
                            start=(kc == 0), stop=(kc == 15)),
                         reads=g_bufs + [buf(T + "w2c")],
                         writes=[pb2] if kc in (0, 15) else (),
                         inc=kc == 15)
                k.op("vector", lambda e, ps2=ps2: e.tensor_copy(
                        out=tst2[:, 0:512], in_=ps2[:, 0:512]),
                     reads=[pb2, buf(T + "tst2")], writes=[buf(T + "tst2")])
                k.dma(6, lambda e, tc=tc, quar=quar: e.dma_start(
                        out=fo_in[tc * 128:(tc + 1) * 128, quar * 512:(quar + 1) * 512],
                        in_=tst2[:, 0:512]),
                      reads=[buf(T + "tst2")], writes=[buf("fo_in")])
        k.cc(lambda e: e.collective_compute(
                "AllReduce", ALU.add, replica_groups=GROUPS4,
                ins=[fo_in], outs=[fo_out]),
             reads=[buf("fo_in")], writes=[buf("fo_out")])

        for ch in range(8):
            k.dma(2, lambda e, ch=ch: e.dma_start(
                    out=tst2[:, :], in_=fo_out[ch * 128:(ch + 1) * 128, :]),
                  reads=[buf("fo_out"), buf(T + "tst2")], writes=[buf(T + "tst2")])
            for dc in range(DC):
                col = (dc % 8) * 128
                k.op("tensor", lambda e, dc=dc, ch=ch, col=col: e.transpose(
                        PSB[:, col:col + 128],
                        filt[:, dc * W + ch * 128:dc * W + (ch + 1) * 128], identb[:, :]),
                     reads=filt_bufs + [buf("identb")],
                     writes=[PB] if dc in (0, 7, 8, 15) else (), inc=dc in (7, 15))
                if dc in (7, 15):
                    for d2 in range(dc - 7, dc + 1):
                        k.op("vector", lambda e, d2=d2: e.tensor_add(
                                tst2[:, d2 * 128:(d2 + 1) * 128],
                                tst2[:, d2 * 128:(d2 + 1) * 128],
                                PSB[:, (d2 % 8) * 128:(d2 % 8 + 1) * 128]),
                             reads=[PB, buf(T + "tst2")], writes=[buf(T + "tst2")])
            k.dma(6, lambda e, ch=ch: e.dma_start(
                    out=xo_dram[ch * 128:(ch + 1) * 128, :], in_=tst2[:, :]),
                  reads=[buf(T + "tst2")], writes=[buf("xo_dram")])

        fence_fb = fence(h2_bufs + g_bufs + filt_bufs +
                         [buf(T + n) for n in ("tst2", "w2c", "sq2", "rstb3", "ms2")])
        gFB.free()
        gFil.free()

        gXn = Grp("right")
        xn = gXn.sb(T + "xn", [128, DC * W])
        gFe = Grp()
        gst = gFe.sb(T + "gst", [128, 2048], BF16)
        gsb2 = gFe.sb(T + "gsb2", [128, 2048], BF16)
        rwt = gFe.sb(T + "rwt", [128, 8])
        idxt2 = gFe.sb(T + "idxt2", [128, 8], I32)

        for dc in range(DC):
            k.dma(dc % 4 + 8 * (dc // 4 % 2), lambda e, dc=dc: e.dma_start(
                    out=xn[:, dc * W:(dc + 1) * W], in_=xst_dram[:, dc * W:(dc + 1) * W]),
                  reads=[buf("xst"), fence_fb], writes=[buf(T + f"xn_{dc}")])
        xn_bufs = [buf(T + f"xn_{dc}") for dc in range(DC)]
        for ch in range(8):
            k.dma(6, lambda e, ch=ch: e.dma_start(
                    out=idxt2[:, ch:ch + 1],
                    in_=idx_dram[0:1, ch * 128:(ch + 1) * 128].rearrange(
                        "a (p c) -> (a p) c", p=128)),
                  reads=[buf("idx_dram"), fence_fb], writes=[buf(T + "idxt2")])
            k.dma(6, lambda e, ch=ch: e.dma_start(
                    out=rwt[:, ch:ch + 1],
                    in_=rw_dram[0:1, ch * 128:(ch + 1) * 128].rearrange(
                        "a (p c) -> (a p) c", p=128)),
                  reads=[buf("rw_dram"), fence_fb], writes=[buf(T + "rwt")])
        for ch in range(8):
            k.op("vector", lambda e: e.memset(gst[:, :], 0.0),
                 reads=[fence_fb, buf(T + "gstB")], writes=[buf(T + "gstB")])
            k.dma(5, lambda e, ch=ch: e.indirect_dma_start(
                    out=gst[:, :], out_offset=None,
                    in_=xo_dram, in_offset=IndirectOffsetOnAxis(idxt2[:, ch:ch + 1], 0),
                    bounds_check=1023, oob_is_err=False),
                  reads=[buf("xo_dram"), buf(T + "idxt2"), buf(T + "gstB")],
                  writes=[buf(T + "gstB")])
            k.op("vector", lambda e, ch=ch: e.tensor_scalar(
                    out=gsb2[:, :], in0=gst[:, :], scalar1=rwt[:, ch:ch + 1],
                    scalar2=None, op0=ALU.mult),
                 reads=[buf(T + "gstB"), buf(T + "rwt")], writes=[buf(T + "gsb2")])
            for dc in range(DC):
                col = (dc % 8) * 128
                k.op("tensor", lambda e, dc=dc, col=col: e.transpose(
                        PSB[:, col:col + 128], gsb2[:, dc * 128:(dc + 1) * 128],
                        identb[:, :]),
                     reads=[buf(T + "gsb2"), buf("identb")],
                     writes=[PB] if dc in (0, 7, 8, 15) else (), inc=dc in (7, 15))
                if dc in (7, 15):
                    for d2 in range(dc - 7, dc + 1):
                        k.op("vector", lambda e, d2=d2, ch=ch: e.tensor_add(
                                xn[:, d2 * W + ch * 128:d2 * W + (ch + 1) * 128],
                                xn[:, d2 * W + ch * 128:d2 * W + (ch + 1) * 128],
                                PSB[:, (d2 % 8) * 128:(d2 % 8 + 1) * 128]),
                             reads=[PB, buf(T + f"xn_{d2}")], writes=[buf(T + f"xn_{d2}")])
        fence_fe = fence([buf(T + n) for n in ("gstB", "gsb2", "rwt", "idxt2")])
        gFe.free()
        return xn, xn_bufs, fence_fe, gXn


    def emit_attn_b(tag, xt, x_bufs, fence_in, free_x):
        T = tag
        ISQ = 0.08838834764831845
        for dc in range(DC):
            k.dma(dc % 2, lambda e, dc=dc: e.dma_start(
                    out=xst_dram[:, dc * W:(dc + 1) * W], in_=xt[:, dc * W:(dc + 1) * W]),
                  reads=[x_bufs[dc], fence_in], writes=[buf("xst")])
        fence_x = fence(list(x_bufs))
        free_x()

        gXp = Grp("right")
        xpb2 = gXp.sb(T + "xpb2", [128, DC * W], BF16)
        gLN = Grp()
        xs0 = gLN.sb(T + "xs0", [128, W])
        xs1 = gLN.sb(T + "xs1", [128, W])
        sqb = gLN.sb(T + "sqb", [128, W], BF16)
        statb = gLN.sb(T + "statb", [1, 3 * W])

        s0 = statb[0:1, 0:W]; s1 = statb[0:1, W:2 * W]; s2 = statb[0:1, 2 * W:3 * W]
        for dc in range(DC):
            xs = [xs0, xs1][dc % 2]
            k.dma((dc % 4) * 2, lambda e, dc=dc, xs=xs: e.dma_start(
                    out=xs[:, :], in_=xst_dram[:, dc * W:(dc + 1) * W]),
                  reads=[buf("xst"), fence_x], writes=[buf(T + f"xs_{dc % 2}")])
            k.op("vector", lambda e, dc=dc, xs=xs: e.tensor_copy(
                    out=xpb2[:, dc * W:(dc + 1) * W], in_=xs[:, :]),
                 reads=[buf(T + f"xs_{dc % 2}"), fence_x], writes=[buf(T + f"xb_{dc}")])
            k.op("scalar", lambda e, dc=dc: e.activation(
                    out=sqb[:, :], in_=xpb2[:, dc * W:(dc + 1) * W], func=AF.Square),
                 reads=[buf(T + f"xb_{dc}")], writes=[buf(T + "sq")])
            for o in (0, 512):
                k.op("tensor", lambda e, dc=dc, o=o: e.matmul(
                        PS12[0:1, o:o + 512], ones_b[:, :],
                        xpb2[:, dc * W + o:dc * W + o + 512],
                        start=(dc == 0), stop=(dc == DC - 1)),
                     reads=[buf(T + f"xb_{dc}"), cONES_B], writes=[P12])
                k.op("tensor", lambda e, o=o, dc=dc: e.matmul(
                        PS3[0:1, o:o + 512], ones_b[:, :], sqb[:, o:o + 512],
                        start=(dc == 0), stop=(dc == DC - 1)),
                     reads=[buf(T + "sq"), cONES_B], writes=[P3])
        k.op("vector", lambda e: e.tensor_scalar_mul(s0, PS12[0:1, 0:W], 1.0 / D),
             reads=[P12], writes=[buf(T + "st0")])
        k.op("vector", lambda e: e.tensor_scalar_mul(s1, PS3[0:1, 0:W], 1.0 / D),
             reads=[P3], writes=[buf(T + "st1")])
        k.op("vector", lambda e: e.tensor_mul(s2, s0, s0),
             reads=[buf(T + "st0")], writes=[buf(T + "st2")])
        k.op("vector", lambda e: e.tensor_sub(s1, s1, s2),
             reads=[buf(T + "st1"), buf(T + "st2")], writes=[buf(T + "st1")])
        k.op("scalar", lambda e: e.activation(out=s1, in_=s1, func=AF.Sqrt,
                                              bias=cst[0:1, 0:1]),
             reads=[buf(T + "st1"), buf("cst")], writes=[buf(T + "st1")])
        k.op("vector", lambda e: e.reciprocal(s1, s1),
             reads=[buf(T + "st1")], writes=[buf(T + "st1")])
        k.op("vector", lambda e: e.tensor_mul(s0, s0, s1),
             reads=[buf(T + "st0"), buf(T + "st1")], writes=[buf(T + "st0")])
        k.op("vector", lambda e: e.tensor_scalar_mul(s0, s0, -1.0),
             reads=[buf(T + "st0")], writes=[buf(T + "st0")])
        for o in (0, 512):
            k.op("tensor", lambda e, o=o: e.matmul(
                    PS12[:, o:o + 512], ones_row[:, :], s1[0:1, o:o + 512],
                    start=True, stop=True),
                 reads=[cONES_R, buf(T + "st1")], writes=[P12])
            k.op("tensor", lambda e, o=o: e.matmul(
                    PS3[:, o:o + 512], ones_row[:, :], s0[0:1, o:o + 512],
                    start=True, stop=True),
                 reads=[cONES_R, buf(T + "st0")], writes=[P3])
        k.op("vector", lambda e: e.tensor_copy(out=e0[:, :], in_=PS12[:, 0:W]),
             reads=[P12, buf("e0")], writes=[buf("e0")])
        k.op("vector", lambda e: e.tensor_copy(out=e1[:, :], in_=PS3[:, 0:W]),
             reads=[P3, buf("e1")], writes=[buf("e1")])
        for dc in range(DC):
            xs = [xs0, xs1][dc % 2]
            k.dma((dc % 4) * 2, lambda e, dc=dc, xs=xs: e.dma_start(
                    out=xs[:, :], in_=xst_dram[:, dc * W:(dc + 1) * W]),
                  reads=[buf("xst"), buf(T + f"xs_{dc % 2}")],
                  writes=[buf(T + f"xs_{dc % 2}")])
            eng = "vector" if dc % 2 == 0 else "gpsimd"
            k.op(eng, lambda e, dc=dc, xs=xs: e.tensor_mul(
                    xpb2[:, dc * W:(dc + 1) * W], xs[:, :], e0[:, :]),
                 reads=[buf(T + f"xs_{dc % 2}"), buf("e0"), buf(T + f"xb_{dc}")],
                 writes=[buf(T + f"xb_{dc}")])
            k.op(eng, lambda e, dc=dc: e.tensor_add(
                    xpb2[:, dc * W:(dc + 1) * W], xpb2[:, dc * W:(dc + 1) * W], e1[:, :]),
                 reads=[buf(T + f"xb_{dc}"), buf("e1")], writes=[buf(T + f"xb_{dc}")])
        xb_bufs = [buf(T + f"xb_{dc}") for dc in range(DC)]
        fence_ln = fence([buf(T + "xs_0"), buf(T + "xs_1"), buf(T + "sq"),
                          buf(T + "st0"), buf(T + "st1"), buf(T + "st2"),
                          buf("e0"), buf("e1")])
        gLN.free()

        gBo = Grp()
        o2 = gBo.sb(T + "o2", [128, DC * W], BF16)
        braw = gBo.sb(T + "braw", [128, DC * 128])
        bbf0 = gBo.sb(T + "bbf0", [128, DC * 128], BF16)
        bbf1 = gBo.sb(T + "bbf1", [128, DC * 128], BF16)
        tstb = gBo.sb(T + "tstb", [128, W], BF16)
        gBq = Grp()
        qfm = gBq.sb(T + "qfm", [128, DC * W], BF16)
        gB1 = Grp()
        kfm = gB1.sb(T + "kfm", [128, 4 * W], BF16)
        wvres2 = gB1.sb(T + "wvres2", [128, 16 * 512], BF16)

        pendb = {0: stream_weight(T + "k0", bwk, 0, DC, braw, bbf0,
                                  extra=[fence_ln])}
        for mc in range(4):
            wbf = [bbf0, bbf1][mc % 2]
            wbufs = pendb.pop(mc)
            if mc + 1 < 4:
                pendb[mc + 1] = stream_weight(T + f"k{(mc + 1) % 2}", bwk, mc + 1, DC,
                                              braw, [bbf0, bbf1][(mc + 1) % 2])
            else:
                pendb[0] = stream_weight(T + "q0", bwq, 0, DC, braw, bbf0)
            ps2, pb2 = [(PS12, P12), (PS3, P3)][mc % 2]
            mm_acc(ps2, pb2, wbf, wbufs, DC,
                   lambda kc, o, cur: xpb2[:, kc * W + o:kc * W + o + cur], W, xb_bufs)
            k.op("vector", lambda e, mc=mc, ps2=ps2: e.tensor_copy(
                    out=kfm[:, mc * W:(mc + 1) * W], in_=ps2[:, 0:W]),
                 reads=[pb2], writes=[buf(T + "kf")])
        for tc in range(8):
            for c in range(4):
                k.op("tensor", lambda e, tc=tc, c=c: e.transpose(
                        PSB[:, c * 128:(c + 1) * 128],
                        kfm[:, c * W + tc * 128:c * W + (tc + 1) * 128], identb[:, :]),
                     reads=[buf(T + "kf"), buf("identb")],
                     writes=[PB] if c in (0, 3) else (), inc=c == 3)
            k.op("vector", lambda e, tc=tc: e.tensor_copy(out=tstb[:, 0:512],
                                                          in_=PSB[:, 0:512]),
                 reads=[PB, buf(T + "tstbB")], writes=[buf(T + "tstbB")])
            k.dma(6, lambda e, tc=tc: e.dma_start(
                    out=kvag_in[tc * 128:(tc + 1) * 128, 0:512], in_=tstb[:, 0:512]),
                  reads=[buf(T + "tstbB")], writes=[buf("kvag_in")])
        for j in range(8):
            ho = (j % 2) * 1024
            k.dma(j % 4, lambda e, j=j, ho=ho: e.dma_start(
                    out=braw[:, ho:ho + 1024], in_=bwv[j]),
                  writes=fraw_bufs(braw, ho, ho + 1024))
            k.op("scalar", lambda e, j=j, ho=ho: e.activation(
                    out=wvres2[:, j * 1024:(j + 1) * 1024], in_=braw[:, ho:ho + 1024],
                    func=AF.Copy),
                 reads=fraw_bufs(braw, ho, ho + 1024), writes=[buf(T + "wv2")])
        for tc in range(8):
            ps2, pb2 = [(PS12, P12), (PS3, P3)][tc % 2]
            for kc in range(DC):
                k.op("tensor", lambda e, tc=tc, kc=kc, ps2=ps2: e.matmul(
                        ps2[:, 1024:1536],
                        xpb2[:, kc * W + tc * 128:kc * W + (tc + 1) * 128],
                        wvres2[:, kc * 512:(kc + 1) * 512],
                        start=(kc == 0), stop=(kc == DC - 1)),
                     reads=xb_bufs + [buf(T + "wv2")],
                     writes=[pb2] if kc in (0, DC - 1) else (), inc=kc == DC - 1)
            k.op("vector", lambda e, tc=tc, ps2=ps2: e.tensor_copy(
                    out=tstb[:, 0:512], in_=ps2[:, 1024:1536]),
                 reads=[pb2, buf(T + "tstbB")], writes=[buf(T + "tstbB")])
            k.dma(6, lambda e, tc=tc: e.dma_start(
                    out=kvag_in[tc * 128:(tc + 1) * 128, 512:1024], in_=tstb[:, 0:512]),
                  reads=[buf(T + "tstbB")], writes=[buf("kvag_in")])
        k.cc(lambda e: e.collective_compute(
                "AllGather", ALU.bypass, replica_groups=GROUPS4,
                ins=[kvag_in], outs=[kvag_out]),
             reads=[buf("kvag_in")], writes=[buf("kvag_out")])
        for mc in range(DC):
            wbf = [bbf0, bbf1][mc % 2]
            wbufs = pendb.pop(mc)
            if mc + 1 < DC:
                pendb[mc + 1] = stream_weight(T + f"q{(mc + 1) % 2}", bwq, mc + 1, DC,
                                              braw, [bbf0, bbf1][(mc + 1) % 2])
            ps2, pb2 = [(PS12, P12), (PS3, P3)][mc % 2]
            mm_acc(ps2, pb2, wbf, wbufs, DC,
                   lambda kc, o, cur: xpb2[:, kc * W + o:kc * W + o + cur], W, xb_bufs)
            k.op("vector", lambda e, mc=mc, ps2=ps2: e.tensor_copy(
                    out=qfm[:, mc * W:(mc + 1) * W], in_=ps2[:, 0:W]),
                 reads=[pb2], writes=[buf(T + f"q_{mc}")])
        q_bufs = [buf(T + f"q_{mc}") for mc in range(DC)]
        fence_b1 = fence(xb_bufs + [buf(T + n) for n in ("kf", "wv2", "tstbB")])
        gB1.free()
        gXp.free()

        gBA = Grp()
        kfmw = gBA.sb(T + "kfmw", [128, 4 * 2048], BF16)
        vwin = gBA.sb(T + "vwin", [128, 16 * 512], BF16)
        kvix = gBA.sb(T + "kvix", [128, 16], I32)
        gtst = gBA.sb(T + "gtst", [128, 1024], BF16)
        PT2a = gBA.sb(T + "PT2a", [128, 1152], BF16)
        PT2b = gBA.sb(T + "PT2b", [128, 1152], BF16)
        Pb2a = gBA.sb(T + "Pb2a", [128, 1152], BF16)
        Pb2b = gBA.sb(T + "Pb2b", [128, 1152], BF16)
        bmq = gBA.sb(T + "bmq", [128, 1152], BF16)
        bmq2 = gBA.sb(T + "bmq2", [128, 1152], BF16)
        smx2 = gBA.sb(T + "smx2", [128, 4])

        k.dma(6, lambda e: e.dma_start(out=kvix[:, :], in_=kvidx_i),
              reads=[fence_b1], writes=[buf(T + "kvix")])
        for wc in list(range(8, 16)) + list(range(8)):
            k.op("vector", lambda e: e.memset(gtst[:, :], 0.0),
                 reads=[fence_b1, buf(T + "gtstB")], writes=[buf(T + "gtstB")])
            if wc < 8:
                k.dma(5, lambda e, wc=wc: e.indirect_dma_start(
                        out=gtst[:, :], out_offset=None,
                        in_=kvag_out, in_offset=IndirectOffsetOnAxis(kvix[:, wc:wc + 1], 0),
                        bounds_check=4095, oob_is_err=False),
                      reads=[buf("kvag_out"), buf(T + "kvix"), buf(T + "gtstB")],
                      writes=[buf(T + "gtstB")])
            else:
                k.dma(5, lambda e, wc=wc: e.dma_start(
                        out=gtst[:, :],
                        in_=kvag_in[(wc - 8) * 128:(wc - 7) * 128, :]),
                      reads=[buf("kvag_in"), buf(T + "gtstB")],
                      writes=[buf(T + "gtstB")])
            k.op("vector", lambda e, wc=wc: e.tensor_copy(
                    out=vwin[:, wc * 512:(wc + 1) * 512], in_=gtst[:, 512:1024]),
                 reads=[buf(T + "gtstB")], writes=[buf(T + "vwin")])
            for c in range(4):
                k.op("tensor", lambda e, c=c: e.transpose(
                        PSB[:, c * 128:(c + 1) * 128], gtst[:, c * 128:(c + 1) * 128],
                        identb[:, :]),
                     reads=[buf(T + "gtstB"), buf("identb")],
                     writes=[PB] if c in (0, 3) else (), inc=c == 3)
            for c in range(4):
                k.op("vector", lambda e, c=c, wc=wc: e.tensor_copy(
                        out=kfmw[:, c * 2048 + wc * 128:c * 2048 + (wc + 1) * 128],
                        in_=PSB[:, c * 128:(c + 1) * 128]),
                     reads=[PB], writes=[buf(T + "kfmw")])
        bmqs = [bmq, bmq2]
        heads = [(qt, h) for qt in range(8) for h in range(16)]
        NH = len(heads)

        def b_qk(i):
            qt, h = heads[i]
            if h == 0:
                mb = bmqs[qt % 2]
                k.dma(2, lambda e, qt=qt, mb=mb: e.dma_start(
                        out=mb[:, :], in_=bmask_i[qt * 128:(qt + 1) * 128, :]),
                      reads=[buf(T + f"bmqB{qt % 2}")],
                      writes=[buf(T + f"bmqB{qt % 2}")])
            kvh = h // 4
            ps2, pb2 = [(PS12, P12), (PS3, P3)][i % 2]
            for (o, cur) in ((0, 512), (512, 512), (1024, 128)):
                k.op("tensor", lambda e, qt=qt, h=h, o=o, cur=cur, ps2=ps2,
                     kvh=kvh: e.matmul(
                        ps2[:, o:o + cur],
                        qfm[:, h * W + qt * 128:h * W + (qt + 1) * 128],
                        kfmw[:, kvh * 2048 + qt * 128 + o:kvh * 2048 + qt * 128 + o + cur],
                        start=True, stop=True),
                     reads=[q_bufs[h], buf(T + "kfmw")],
                     writes=[pb2], inc=o == 1024)

        def b_chain(i):
            qt, h = heads[i]
            ps2, pb2 = [(PS12, P12), (PS3, P3)][i % 2]
            Pb2 = [Pb2a, Pb2b][i % 2]
            BPB = buf(T + "Pb2" + "ab"[i % 2])
            mxc = 2 * (i % 2)
            mb = bmqs[qt % 2]
            k.op("vector", lambda e, ps2=ps2, mxc=mxc: e.reduce_max(
                    out=smx2[:, mxc:mxc + 1], in_=ps2[:, 0:1152], axis=AX.X,
                    negate=True),
                 reads=[pb2], writes=[buf(T + f"nmx{i % 2}")])
            k.op("vector", lambda e, mxc=mxc: e.tensor_scalar_mul(
                    smx2[:, mxc:mxc + 1], smx2[:, mxc:mxc + 1], ISQ),
                 reads=[buf(T + f"nmx{i % 2}")], writes=[buf(T + f"nmx{i % 2}")])
            k.op("scalar", lambda e, ps2=ps2, Pb2=Pb2, mxc=mxc: e.activation(
                    out=Pb2[:, :], in_=ps2[:, 0:1152], func=AF.Exp, scale=ISQ,
                    bias=smx2[:, mxc:mxc + 1]),
                 reads=[pb2, buf(T + f"nmx{i % 2}"), BPB], writes=[BPB])
            k.op("vector", lambda e, Pb2=Pb2, mxc=mxc, mb=mb: e.scalar_tensor_tensor(
                    out=Pb2[:, :], in0=Pb2[:, :], scalar=1.0, in1=mb[:, :],
                    op0=ALU.mult, op1=ALU.mult,
                    accum_out=smx2[:, mxc + 1:mxc + 2]),
                 reads=[BPB, buf(T + f"bmqB{qt % 2}")], writes=[BPB])
            k.op("vector", lambda e, mxc=mxc: e.reciprocal(
                    smx2[:, mxc + 1:mxc + 2], smx2[:, mxc + 1:mxc + 2]),
                 reads=[BPB], writes=[buf(T + f"rs{i % 2}")])
            k.op("scalar", lambda e, Pb2=Pb2, mxc=mxc: e.activation(
                    out=Pb2[:, :], in_=Pb2[:, :], func=AF.Copy,
                    scale=smx2[:, mxc + 1:mxc + 2]),
                 reads=[BPB, buf(T + f"rs{i % 2}")], writes=[BPB])

        def b_tail(i):
            qt, h = heads[i]
            kvh = h // 4
            ps2, pb2 = [(PS12, P12), (PS3, P3)][i % 2]
            ooff = 1536 if i % 2 == 0 else 1280
            PT2 = [PT2a, PT2b][i % 2]
            Pb2 = [Pb2a, Pb2b][i % 2]
            BPT = buf(T + "PT2" + "ab"[i % 2])
            BPB = buf(T + "Pb2" + "ab"[i % 2])
            for c in range(8):
                k.op("tensor", lambda e, c=c, Pb2=Pb2: e.transpose(
                        PSB[:, c * 128:(c + 1) * 128], Pb2[:, c * 128:(c + 1) * 128],
                        identb[:, :]),
                     reads=[BPB, buf("identb")],
                     writes=[PB] if c in (0, 7) else (), inc=c == 7)
            k.op("vector", lambda e, PT2=PT2: e.tensor_copy(out=PT2[:, 0:1024],
                                                            in_=PSB[:, 0:1024]),
                 reads=[PB], writes=[BPT])
            k.op("tensor", lambda e, Pb2=Pb2: e.transpose(
                    PSB[:, 0:128], Pb2[:, 1024:1152], identb[:, :]),
                 reads=[BPB, buf("identb"), PB], writes=[PB])
            k.op("vector", lambda e, PT2=PT2: e.tensor_copy(out=PT2[:, 1024:1152],
                                                            in_=PSB[:, 0:128]),
                 reads=[PB], writes=[BPT])
            for c in range(9):
                k.op("tensor", lambda e, c=c, qt=qt, kvh=kvh, ps2=ps2, PT2=PT2,
                     ooff=ooff: e.matmul(
                        ps2[:, ooff:ooff + 128],
                        vwin[:, (qt + c) * 512 + kvh * 128:(qt + c) * 512 + (kvh + 1) * 128],
                        PT2[:, c * 128:(c + 1) * 128],
                        start=(c == 0), stop=(c == 8)),
                     reads=[buf(T + "vwin"), BPT],
                     writes=[pb2] if c in (0, 8) else (), inc=c == 8)
            k.op("vector", lambda e, h=h, qt=qt, ps2=ps2, ooff=ooff: e.tensor_copy(
                    out=o2[:, h * W + qt * 128:h * W + (qt + 1) * 128],
                    in_=ps2[:, ooff:ooff + 128]),
                 reads=[pb2], writes=[buf(T + f"o2_{h}")])

        b_qk(0)
        for i in range(NH):
            if i + 1 < NH:
                b_qk(i + 1)
            b_chain(i)
            if i >= 1:
                b_tail(i - 1)
        b_tail(NH - 1)
        o2_bufs = [buf(T + f"o2_{h}") for h in range(16)]
        fence_ba = fence(q_bufs + [buf(T + n) for n in
                         ("kfmw", "vwin", "kvix", "gtstB", "PT2a", "PT2b", "Pb2a",
                          "Pb2b", "bmqB0", "bmqB1", "nmx0", "nmx1", "rs0", "rs1")])
        gBA.free()
        gBq.free()

        gXb = Grp("right")
        xn = gXb.sb(T + "xn", [128, DC * W])
        for dc in range(DC):
            k.dma(dc % 4, lambda e, dc=dc: e.dma_start(
                    out=xn[:, dc * W:(dc + 1) * W], in_=xst_dram[:, dc * W:(dc + 1) * W]),
                  reads=[buf("xst"), fence_ba], writes=[buf(T + f"xn_{dc}")])
        xn_bufs = [buf(T + f"xn_{dc}") for dc in range(DC)]
        pendo = {0: stream_weight(T + "wo0", bwo, 0, DC, braw, bbf0,
                                  extra=[fence_ba])}
        for mc in range(DC):
            wbf = [bbf0, bbf1][mc % 2]
            wbufs = pendo.pop(mc)
            if mc + 1 < DC:
                pendo[mc + 1] = stream_weight(T + f"wo{(mc + 1) % 2}", bwo, mc + 1, DC,
                                              braw, [bbf0, bbf1][(mc + 1) % 2])
            ps2, pb2 = [(PS12, P12), (PS3, P3)][mc % 2]
            mm_acc(ps2, pb2, wbf, wbufs, DC,
                   lambda kc, o, cur: o2[:, kc * W + o:kc * W + o + cur], W, o2_bufs)
            k.op("vector", lambda e, mc=mc, ps2=ps2: e.tensor_add(
                    xn[:, mc * W:(mc + 1) * W], xn[:, mc * W:(mc + 1) * W], ps2[:, 0:W]),
                 reads=[pb2, buf(T + f"xn_{mc}")], writes=[buf(T + f"xn_{mc}")])
        fence_bo = fence(o2_bufs + [buf(n) for n in list(B) if n.startswith(("raw_", "bf_"))])
        gBo.free()
        return xn, xn_bufs, fence_bo, gXb

    XT, XB, XF, XG = x, x_bufs, fence_a5, g_a5
    if "ffn1" in STAGES:
        _g = XG
        XT, XB, XF, XG = emit_ffn("f1_", XT, XB, XF, lambda g=_g: g.free())
    if "attn_b" in STAGES:
        _g = XG
        XT, XB, XF, XG = emit_attn_b("b_", XT, XB, XF, lambda g=_g: g.free())
    if "ffn2" in STAGES:
        _g = XG
        XT, XB, XF, XG = emit_ffn("f2_", XT, XB, XF, lambda g=_g: g.free())

    g_out = Grp()
    outb = g_out.sb("outb", [128, D])
    for tc in range(8):
        for dc in range(DC):
            ps = PS3 if dc < 8 else PS12
            psb = P3 if dc < 8 else P12
            col = (dc % 8) * 128
            last = dc in (7, 15)
            first = dc in (0, 8)
            k.op("tensor", lambda e, tc=tc, dc=dc, ps=ps, col=col: e.transpose(
                    ps[:, col:col + 128],
                    XT[:, dc * W + tc * 128:dc * W + (tc + 1) * 128], identf[:, :]),
                 reads=XB + [buf("identf")],
                 writes=[psb] if (last or first) else (), inc=last)
        k.op("vector", lambda e: e.tensor_copy(out=outb[:, 0:1024], in_=PS3[:, 0:1024]),
             reads=[P3], writes=[buf("outb")])
        k.op("vector", lambda e: e.tensor_copy(out=outb[:, 1024:2048], in_=PS12[:, 0:1024]),
             reads=[P12], writes=[buf("outb")])
        k.dma(6, lambda e, tc=tc: e.dma_start(
                out=out_t[tc * 128:(tc + 1) * 128, :], in_=outb[:, :]),
              reads=[buf("outb")], writes=[buf("out")])

    k.finish("sync")
    k.finish("gpsimd")

    assert peak[0] < 214 * 1024, f"SBUF peak {peak[0]} bytes/partition"

    g_out.free(); XG.free(); gFper.free(); g_persist.free()
    with nc.Block() as block:
        k.build(block)
    return nc


def pack_w(Wm, KC, MC):
    Kd, Md = Wm.shape
    assert Kd == KC * 128 and Md == MC * 128
    return np.ascontiguousarray(
        Wm.reshape(KC, 128, MC, 128).transpose(2, 1, 0, 3).reshape(MC, 128, KC * 128))


def vec_t(v, MC):
    return np.ascontiguousarray(v.reshape(MC, 128).T)


_prog_cache = {}


def build_inmaps(ii):
    x_in = ii["L0_attn_a_x"].astype(np.float32)

    shared = {
        "wl": pack_w(ii["L0_attn_a_Wl"], DC, HC),
        "wr": pack_w(ii["L0_attn_a_Wr"], DC, HC),
        "wa": pack_w(ii["L0_attn_a_Wa"], HC, HC),
        "wi": pack_w(ii["L0_attn_a_Wi"], HC, HC),
        "wo": pack_w(ii["L0_attn_a_Wo"], HC, DC),
        "bl": vec_t(ii["L0_attn_a_bl"], HC),
        "br": vec_t(ii["L0_attn_a_br"], HC),
        "ba": vec_t(ii["L0_attn_a_ba"], HC),
        "bi": vec_t(ii["L0_attn_a_bi"], HC),
        "bo": vec_t(ii["L0_attn_a_bo"], DC),
        "cb": vec_t(ii["L0_attn_a_conv_b"], HC),
        "lam": vec_t(ii["L0_attn_a_lam"], HC),
        "idf": np.eye(128, dtype=np.float32),
        "idb": np.eye(128, dtype=np.float32).astype(ml_dtypes.bfloat16),
    }
    cw = ii["L0_attn_a_conv_w"][:, 0, :]
    shared["cw"] = np.ascontiguousarray(
        cw.reshape(4, HC, 128).transpose(2, 1, 0).reshape(128, HC * 4))
    shared["wrt"] = vec_t(ii["L0_ffn_w_router"], DC)
    shared["g1"] = vec_t(ii["L0_ffn_g1"], DC)
    shared["g2"] = vec_t(ii["L0_ffn_g2"], DC)
    blk = np.zeros((128, 4), np.float32)
    for p in range(128):
        blk[p, p // 32] = 1.0
    shared["blkid"] = blk
    shared["fmask"] = np.ascontiguousarray(ii["L0_ffn_mask"].astype(np.float32))
    fr = ii["L0_ffn_freqs_cis"].astype(np.float32)
    shared["fcos"] = np.ascontiguousarray(fr[:, :, 0].T)
    shared["fsin"] = np.ascontiguousarray(fr[:, :, 1].T)
    shared["bwq"] = pack_w(ii["L0_attn_b_Wq"], DC, DC)
    shared["bwk"] = pack_w(ii["L0_attn_b_Wk"], DC, 4)
    shared["bwv"] = np.ascontiguousarray(
        ii["L0_attn_b_Wv"].astype(np.float32)
        .reshape(8, 2, 128, 512).transpose(0, 2, 1, 3).reshape(8, 128, 1024))
    shared["bwo"] = pack_w(ii["L0_attn_b_Wo"], DC, DC)
    bmasks = []
    kvidxs = []
    r_ = np.arange(128)
    j_ = np.arange(1152)
    for q in range(4):
        mm_ = np.full((1024, 1152), -1.0e9, np.float32)
        for i in range(8):
            base = 1024 * q + 128 * i
            ktg = (1024 * (q - 1) + 128 * i) + j_[None, :]
            valid = (j_[None, :] >= r_[:, None]) & (j_[None, :] <= r_[:, None] + 1024) \
                    & (ktg >= 0)
            blkm = np.where(valid, 1.0, 0.0).astype(np.float32)
            mm_[i * 128:(i + 1) * 128] = blkm
        bmasks.append(mm_.astype(ml_dtypes.bfloat16))
        kv = np.empty((128, 16), np.int32)
        for wc in range(16):
            gi = 1024 * (q - 1) + 128 * wc + r_
            kv[:, wc] = np.where(gi >= 0, gi, 8192).astype(np.int32)
        kvidxs.append(kv)

    fslices = []
    for q in range(4):
        fs = {
            "fwq": pack_w(np.ascontiguousarray(ii["L0_ffn_wq"][:, 512 * q:512 * (q + 1)]),
                          DC, 4),
            "fwk": pack_w(np.ascontiguousarray(ii["L0_ffn_wk"][:, 512 * q:512 * (q + 1)]),
                          DC, 4),
            "fwv": np.ascontiguousarray(
                ii["L0_ffn_wv"][:, 512 * q:512 * (q + 1)].astype(np.float32)
                .reshape(4, 4, 128, 2, 256).transpose(3, 0, 2, 1, 4)
                .reshape(2, 4, 128, 1024)),
            "fwo": pack_w(np.ascontiguousarray(ii["L0_ffn_wo"][512 * q:512 * (q + 1), :]),
                          4, DC),
            "fw1": pack_w(np.ascontiguousarray(ii["L0_ffn_w1"][:, 2048 * q:2048 * (q + 1)]),
                          DC, 16),
            "fw3": pack_w(np.ascontiguousarray(ii["L0_ffn_w3"][:, 2048 * q:2048 * (q + 1)]),
                          DC, 16),
            "fw2": np.ascontiguousarray(
                ii["L0_ffn_w2"][2048 * q:2048 * (q + 1), :].astype(np.float32)
                .reshape(8, 2, 128, 4, 512).transpose(3, 0, 2, 1, 4)
                .reshape(4, 8, 128, 1024)),
        }
        fslices.append(fs)

    xT = [np.ascontiguousarray(x_in[b].T) for b in range(2)]

    in_maps = []
    for c in range(NCORES):
        b, q = c // 4, c % 4
        m = dict(shared)
        x0 = np.zeros((D, WH), np.float32)
        lo = 1024 * q - 3
        if lo < 0:
            x0[:, 3:] = xT[b][:, 0:1024 * (q + 1)]
        else:
            x0[:, :] = xT[b][:, lo:1024 * (q + 1)]
        m["x0"] = x0
        m["hv"] = np.array([[0.0 if q == 0 else 1.0]], np.float32)
        qs = np.zeros((1, 4), np.float32)
        qs[0, :q] = 1.0
        m["qsel"] = qs
        m["qsel4"] = np.ascontiguousarray(qs.reshape(4, 1))
        m.update(fslices[q])
        m["bmask"] = bmasks[q]
        m["kvidx"] = kvidxs[q]
        in_maps.append(m)
    return in_maps


def kernel(**inputs):
    ii = {kk: np.asarray(v) for kk, v in inputs.items()}
    in_maps = build_inmaps(ii)

    if "prog" not in _prog_cache:
        _prog_cache["prog"] = build_program()
    nc = _prog_cache["prog"]
    res = run_bass_kernel_spmd(nc, in_maps, core_ids=list(range(NCORES)))

    y = np.empty((2, 4096, D), np.float32)
    for c in range(NCORES):
        b, q = c // 4, c % 4
        y[b, 1024 * q:1024 * (q + 1), :] = res.results[c]["out"]
    return y


# version 17
# speedup vs baseline: 1.1775x; 1.0308x over previous
import numpy as np
import ml_dtypes
import concourse.bass as bass
import concourse.mybir as mybir
from concourse.bass import IndirectOffsetOnAxis
from concourse.bass_utils import run_bass_kernel_spmd
from contextlib import ExitStack

F32 = mybir.dt.float32
BF16 = mybir.dt.bfloat16
I32 = mybir.dt.int32
AF = mybir.ActivationFunctionType
ALU = mybir.AluOpType
AX = mybir.AxisListType

NCORES = 8
D = 2048
DC = D // 128
HID = 2560
HC = HID // 128
W = 1024
WH = W + 3
C_RGLRU = 8.0
GELU_C0 = 0.7978845608028654
GELU_C1 = GELU_C0 * 0.044715
STAGES = ("attn_a", "ffn1", "attn_b", "ffn2")
GROUPS4 = [[0, 1, 2, 3], [4, 5, 6, 7]]

SLOTQ = {0: "sync", 1: "sync", 2: "sync", 3: "sync", 4: "scalar",
         5: "gpsimd", 6: "gpsimd", 7: "sync",
         8: "sync", 9: "sync", 10: "sync", 11: "sync", 12: "scalar",
         13: "gpsimd", 14: "gpsimd", 15: "sync"}


class Buf:
    __slots__ = ("name", "ready", "readers")

    def __init__(self, name="buf"):
        self.name = name
        self.ready = None
        self.readers = []


class K:
    ENGINES = ("tensor", "vector", "scalar", "gpsimd", "sync")

    def __init__(self, nc, n_dma_sems=16):
        self.nc = nc
        self.sems = {}
        self.counts = {}
        self.waited = {}
        self.queues = {e: [] for e in self.ENGINES}
        self._sem_ctxs = []
        for e in self.ENGINES:
            self._alloc_sem("s_" + e, 1)
        for i in range(n_dma_sems):
            self._alloc_sem(f"s_dma{i}", 16)
        self._alloc_sem("s_cc", 1)
        self.eng_sem = {e: "s_" + e for e in self.ENGINES}
        self.pending = {}

    def _alloc_sem(self, name, step):
        ctx = self.nc.semaphore(name)
        h = ctx.__enter__()
        self._sem_ctxs.append(ctx)
        self.sems[name] = (h, step)
        self.counts[name] = 0

    def _emit_wait(self, engine, sem_name, count):
        if count <= 0:
            return
        if engine == "tensor" and sem_name == "s_tensor":
            return
        key = (engine, sem_name)
        if self.waited.get(key, 0) >= count:
            return
        self.waited[key] = count
        h, _ = self.sems[sem_name]
        self.queues[engine].append(("wait", h, count))

    def _deps(self, engine, reads, writes):
        for b in reads or ():
            if b.ready:
                self._emit_wait(engine, *b.ready)
        for b in writes or ():
            if b.ready:
                self._emit_wait(engine, *b.ready)
            for r in b.readers:
                self._emit_wait(engine, *r)

    def _record(self, sem_name, count, reads, writes):
        for b in reads or ():
            b.readers.append((sem_name, count))
        for b in writes or ():
            b.ready = (sem_name, count)
            b.readers = []

    def op(self, engine, fn, reads=(), writes=(), sem=None, inc=True):
        sem_name = sem or self.eng_sem[engine]
        self._deps(engine, reads, writes)
        h, step = self.sems[sem_name]
        if inc:
            self.counts[sem_name] += step
            cnt = self.counts[sem_name]
            self.queues[engine].append(("op", fn, h, step))
            pend = self.pending.pop(sem_name, [])
            self._record(sem_name, cnt, list(reads) + pend, writes)
        else:
            self.pending.setdefault(sem_name, []).extend(reads)
            self.queues[engine].append(("op", fn, None, 0))

    def dma(self, slot, fn, reads=(), writes=()):
        engine = SLOTQ[slot]
        sem_name = f"s_dma{slot}"
        self._emit_wait(engine, sem_name, self.counts[sem_name])
        self.op(engine, fn, reads=reads, writes=writes, sem=sem_name)

    def cc(self, fn, reads=(), writes=()):
        self.op("gpsimd", fn, reads=reads, writes=writes, sem="s_cc")

    def finish(self, engine="sync"):
        for name in self.sems:
            c = self.counts[name]
            if c > 0:
                self._emit_wait(engine, name, c)

    def build(self, block):
        for e in self.ENGINES:
            q = self.queues[e]
            if not q:
                continue
            section = getattr(block, e)

            def make(q=q):
                def _(eng):
                    for item in q:
                        if item[0] == "wait":
                            _, h, count = item
                            eng.wait_ge(h, count)
                        else:
                            _, fn, h, step = item
                            inst = fn(eng)
                            if h is not None:
                                inst.then_inc(h, step)
                return _

            section(make())


def build_program():
    nc = bass.Bass("TRN2", target_bir_lowering=False, debug=False,
                   dynamic_dma_scratch_size=20480)

    def dram_in(name, shape, dt=F32):
        return nc.dram_tensor(name, shape, dt, kind="ExternalInput").ap()

    x0p = dram_in("x0", [D, WH])
    wlp = dram_in("wl", [HC, 128, DC * 128])
    wrp = dram_in("wr", [HC, 128, DC * 128])
    wap = dram_in("wa", [HC, 128, HC * 128])
    wip = dram_in("wi", [HC, 128, HC * 128])
    wop = dram_in("wo", [DC, 128, HC * 128])
    bl_i = dram_in("bl", [128, HC]); br_i = dram_in("br", [128, HC])
    ba_i = dram_in("ba", [128, HC]); bi_i = dram_in("bi", [128, HC])
    bo_i = dram_in("bo", [128, DC]); cb_i = dram_in("cb", [128, HC])
    cw_i = dram_in("cw", [128, HC * 4]); lam_i = dram_in("lam", [128, HC])
    hv_i = dram_in("hv", [1, 1]); qsel_i = dram_in("qsel", [1, 4])
    idf_i = dram_in("idf", [128, 128]); idb_i = dram_in("idb", [128, 128], BF16)
    wrt_i = dram_in("wrt", [128, DC]); g1_i = dram_in("g1", [128, DC])
    g2_i = dram_in("g2", [128, DC]); blk_i = dram_in("blkid", [128, 4])
    qsel4_i = dram_in("qsel4", [4, 1])
    fwq = dram_in("fwq", [4, 128, DC * 128]); fwk = dram_in("fwk", [4, 128, DC * 128])
    fwv = dram_in("fwv", [2, 4, 128, 1024]); fwo = dram_in("fwo", [DC, 128, 512])
    fw1 = dram_in("fw1", [16, 128, DC * 128]); fw3 = dram_in("fw3", [16, 128, DC * 128])
    fw2 = dram_in("fw2", [4, 8, 128, 1024])
    fmask = dram_in("fmask", [1024, 1024])
    bwq = dram_in("bwq", [16, 128, DC * 128]); bwk = dram_in("bwk", [4, 128, DC * 128])
    bwv = dram_in("bwv", [8, 128, 1024]); bwo = dram_in("bwo", [DC, 128, DC * 128])
    bmask_i = dram_in("bmask", [1024, 1152], BF16)
    kvidx_i = dram_in("kvidx", [128, 16], I32)
    fcos_i = dram_in("fcos", [64, 1024]); fsin_i = dram_in("fsin", [64, 1024])

    out_t = nc.dram_tensor("out", [W, D], F32, kind="ExternalOutput").ap()

    relay_in = nc.dram_tensor("relay_in", [1, 2 * HID], F32).ap()
    relay_out = nc.dram_tensor("relay_out", [4, 2 * HID], F32).ap()
    lg_in = nc.dram_tensor("lg_in", [1, 1024], F32).ap()
    lg_out = nc.dram_tensor("lg_out", [4, 1024], F32).ap()
    idx_dram = nc.dram_tensor("idx_dram", [1, 1024], I32).ap()
    rw_dram = nc.dram_tensor("rw_dram", [1, 1024], F32).ap()
    cpt_in = nc.dram_tensor("cpt_in", [1024, 2048], BF16).ap()
    cpt_out = nc.dram_tensor("cpt_out", [1024, 2048], BF16).ap()
    at_in = nc.dram_tensor("at_in", [2048, 1024], BF16).ap()
    at_out = nc.dram_tensor("at_out", [2048, 1024], BF16).ap()
    fo_in = nc.dram_tensor("fo_in", [1024, 2048], BF16).ap()
    fo_out = nc.dram_tensor("fo_out", [1024, 2048], BF16).ap()
    xo_dram = nc.dram_tensor("xo_dram", [1024, 2048], BF16).ap()
    xst_dram = nc.dram_tensor("xst_dram", [128, DC * W], F32).ap()
    lcp_dram = nc.dram_tensor("lcp_dram", [HC * 128, W], BF16).ap()
    kvag_in = nc.dram_tensor("kvag_in", [1024, 1024], BF16).ap()
    kvag_out = nc.dram_tensor("kvag_out", [4096, 1024], BF16).ap()

    live = [0]; peak = [0]

    def mk_sb(stack, name, shape, dt=F32):
        t = stack.enter_context(nc.sbuf_tensor(name, shape, dt))
        bpe = {F32: 4, BF16: 2, I32: 4}[dt]
        live[0] += shape[1] * bpe
        peak[0] = max(peak[0], live[0])
        return t

    class Grp:
        def __init__(self, side="left"):
            self.stack = ExitStack()
            self.bytes = 0
            self.side = side

        def sb(self, name, shape, dt=F32):
            before = live[0]
            t = self.stack.enter_context(nc.sbuf_tensor(name, shape, dt, side=self.side))
            bpe = {F32: 4, BF16: 2, I32: 4}[dt]
            live[0] += shape[1] * bpe
            peak[0] = max(peak[0], live[0])
            self.bytes += live[0] - before
            return t

        def free(self):
            self.stack.close()
            live[0] -= self.bytes

    g_persist = Grp()
    sb = g_persist.sb

    identf = sb("identf", [128, 128])
    identb = sb("identb", [128, 128], BF16)
    ones_b = sb("ones_b", [128, 1], BF16)
    ones_row = sb("ones_row", [1, 128])
    zeros_bw = sb("zeros_bw", [128, W], BF16)
    bl_t = sb("bl_t", [128, HC]); br_t = sb("br_t", [128, HC])
    ba_t = sb("ba_t", [128, HC]); bi_t = sb("bi_t", [128, HC])
    bo_t = sb("bo_t", [128, DC]); cb_t = sb("cb_t", [128, HC])
    cw_t = sb("cw_t", [128, HC * 4]); lam_t = sb("lam_t", [128, HC])
    slam_t = sb("slam_t", [128, HC]); slam2_t = sb("slam2_t", [128, HC])
    hv_b = sb("hv_b", [128, 1])
    qsel_b = sb("qsel_b", [128, 4]); omq_b = sb("omq_b", [128, 4])
    qsel_s = sb("qsel_s", [1, 4]); tiny = sb("tiny", [128, 32])
    fsc = sb("fsc", [128, 8])
    cst = sb("cst", [128, 4])
    atot_t = sb("atot_t", [128, HC]); hlast_t = sb("hlast_t", [128, HC])
    rel = sb("rel", [128, 8 * HC]); Hst = sb("Hst", [128, HC])
    e0 = sb("e0", [128, W]); e1 = sb("e1", [128, W])
    g_attA = Grp()
    e2b = g_attA.sb("e2b", [128, W])
    ubuf0 = g_attA.sb("ubuf0", [128, W]); rin0 = g_attA.sb("rin0", [128, WH])
    cacc0 = g_attA.sb("cacc0", [128, W])
    wrawA = g_attA.sb("wrawA", [128, HC * 128])
    wbfA0 = g_attA.sb("wbfA0", [128, HC * 128], BF16)
    wbfA1 = g_attA.sb("wbfA1", [128, HC * 128], BF16)

    PS12 = g_persist.stack.enter_context(nc.psum_tensor("PS12", [128, 2048], F32))
    PS3 = g_persist.stack.enter_context(nc.psum_tensor("PS3", [128, 1536], F32))
    PSB = g_persist.stack.enter_context(nc.psum_tensor("PSB", [128, 1024], BF16))

    k = K(nc)
    B = {}

    def buf(name):
        if name not in B:
            B[name] = Buf(name)
        return B[name]

    P12, P3, PB = buf("PS12"), buf("PS3"), buf("PSB")
    FSC = buf("fsc")
    n_fence = [0]

    def fence(dead_bufs):
        ep = Buf(f"fence{n_fence[0]}")
        col = n_fence[0] % 8
        n_fence[0] += 1
        k.op("vector", lambda e, col=col: e.tensor_copy(
                out=fsc[:, col:col + 1], in_=fsc[:, col:col + 1]),
             reads=[FSC], writes=list(dead_bufs) + [ep, FSC])
        return ep

    for nm, tt_, ap_ in [("bl_t", bl_t, bl_i), ("br_t", br_t, br_i), ("ba_t", ba_t, ba_i),
                         ("bi_t", bi_t, bi_i), ("bo_t", bo_t, bo_i), ("cb_t", cb_t, cb_i),
                         ("cw_t", cw_t, cw_i), ("lam_t", lam_t, lam_i),
                         ("identf", identf, idf_i), ("identb", identb, idb_i),
                         ("qsel_s", qsel_s, qsel_i)]:
        k.dma(7, lambda e, t=tt_, a=ap_: e.dma_start(out=t[:, :], in_=a), writes=[buf(nm)])
    k.dma(7, lambda e: e.dma_start(out=tiny[0:1, 0:1], in_=hv_i), writes=[buf("hv_raw")])
    cONES_B, cONES_R, cZER = buf("cob"), buf("cor"), buf("czw")
    k.op("vector", lambda e: e.memset(ones_b[:, :], 1.0), writes=[cONES_B])
    k.op("vector", lambda e: e.memset(ones_row[:, :], 1.0), writes=[cONES_R])
    k.op("vector", lambda e: e.memset(zeros_bw[:, :], 0.0), writes=[cZER])
    k.op("vector", lambda e: e.memset(fsc[:, :], 0.0), writes=[FSC])
    k.op("vector", lambda e: e.memset(cst[:, 0:1], 1e-5), writes=[buf("cst")])
    k.op("vector", lambda e: e.memset(cst[:, 1:2], 1e-6), writes=[buf("cst")])
    k.op("vector", lambda e: e.memset(cst[:, 2:3], 1.0), writes=[buf("cst")])
    k.op("tensor", lambda e: e.matmul(PS3[:, 0:1], ones_row[:, :], tiny[0:1, 0:1],
                                      start=True, stop=True),
         reads=[cONES_R, buf("hv_raw")], writes=[P3])
    k.op("vector", lambda e: e.tensor_copy(out=hv_b[:, :], in_=PS3[:, 0:1]),
         reads=[P3], writes=[buf("hv_b")])
    k.op("tensor", lambda e: e.matmul(PS3[:, 4:8], ones_row[:, :], qsel_s[:, :],
                                      start=True, stop=True),
         reads=[cONES_R, buf("qsel_s")], writes=[P3])
    k.op("vector", lambda e: e.tensor_copy(out=qsel_b[:, :], in_=PS3[:, 4:8]),
         reads=[P3], writes=[buf("qsel_b")])
    k.op("vector", lambda e: e.tensor_scalar(out=omq_b[:, :], in0=qsel_b[:, :], scalar1=-1.0,
                                             scalar2=1.0, op0=ALU.mult, op1=ALU.add),
         reads=[buf("qsel_b")], writes=[buf("omq_b")])
    k.op("vector", lambda e: e.tensor_scalar_mul(slam_t[:, :], lam_t[:, :], -1.0),
         reads=[buf("lam_t")], writes=[buf("slam")])
    k.op("scalar", lambda e: e.activation(out=tiny[:, 1:1 + HC], in_=slam_t[:, :], func=AF.Abs),
         reads=[buf("slam")], writes=[buf("t_az")])
    k.op("scalar", lambda e: e.activation(out=tiny[:, 1:1 + HC], in_=tiny[:, 1:1 + HC],
                                          func=AF.Exp, scale=-1.0),
         reads=[buf("t_az")], writes=[buf("t_az")])
    k.op("scalar", lambda e: e.activation(out=tiny[:, 1:1 + HC], in_=tiny[:, 1:1 + HC],
                                          func=AF.Ln, bias=cst[:, 2:3]),
         reads=[buf("t_az"), buf("cst")], writes=[buf("t_az")])
    k.op("vector", lambda e: e.tensor_scalar_max(slam_t[:, :], slam_t[:, :], 0.0),
         reads=[buf("slam")], writes=[buf("slam")])
    k.op("vector", lambda e: e.tensor_add(slam_t[:, :], slam_t[:, :], tiny[:, 1:1 + HC]),
         reads=[buf("slam"), buf("t_az")], writes=[buf("slam")])
    k.op("vector", lambda e: e.tensor_scalar_mul(slam_t[:, :], slam_t[:, :], -C_RGLRU),
         reads=[buf("slam")], writes=[buf("slam")])
    k.op("vector", lambda e: e.tensor_scalar_mul(slam2_t[:, :], slam_t[:, :], 2.0),
         reads=[buf("slam")], writes=[buf("slam2")])


    WS_SLOTS = [0, 1, 2, 3, 8, 9, 10, 11, 7, 15]

    def stream_weight(stream, src_ap, m, kc_n, wraw, wbf, cast_eng="scalar", extra=(),
                      soff=0):
        ncols = kc_n * 128
        half = (ncols // 256) * 128
        def gspan(pfx, tid, o0, o1):
            return [buf(f"{pfx}_{tid}_{g}") for g in range(o0 // 128, o1 // 128)]
        allbf = gspan("bf", id(wbf), 0, ncols)
        for h in range(2):
            o0, o1 = h * half, ncols if h else half
            slot = WS_SLOTS[(soff * 4 + h * 2 + (m % 2)) % 10]
            rb = gspan("raw", id(wraw), o0, o1)
            bb = gspan("bf", id(wbf), o0, o1)
            k.dma(slot, lambda e, o0=o0, o1=o1: e.dma_start(
                    out=wraw[:, o0:o1], in_=src_ap[m, :, o0:o1]),
                  reads=list(extra), writes=rb)
            ce = ("scalar", "vector") if cast_eng == "scalar" else ("gpsimd", "vector")
            ce = ce[h]
            if ce == "scalar":
                k.op("scalar", lambda e, o0=o0, o1=o1: e.activation(
                        out=wbf[:, o0:o1], in_=wraw[:, o0:o1], func=AF.Copy),
                     reads=rb, writes=bb)
            else:
                k.op(ce, lambda e, o0=o0, o1=o1: e.tensor_copy(
                        out=wbf[:, o0:o1], in_=wraw[:, o0:o1]),
                     reads=rb, writes=bb)
        return allbf

    def nslices(nw):
        out, o = [], 0
        while o < nw:
            cur = min(512, nw - o)
            out.append((o, cur))
            o += cur
        return out

    def mm_acc(ps, psb, wbf, wbufs, kc_n, rhs_fn, nw, rhs_bufs):
        sl = nslices(nw)
        for kc in range(kc_n):
            for si, (o, cur) in enumerate(sl):
                last = (kc == kc_n - 1) and (si == len(sl) - 1)
                first = kc == 0 and si == 0
                k.op("tensor",
                     lambda e, kc=kc, o=o, cur=cur: e.matmul(
                         ps[:, o:o + cur], wbf[:, kc * 128:(kc + 1) * 128],
                         rhs_fn(kc, o, cur), start=(kc == 0), stop=(kc == kc_n - 1)),
                     reads=list(wbufs) + list(rhs_bufs),
                     writes=[psb] if (last or first) else (), inc=last)

    g_xp = Grp("right")
    xpb = g_xp.sb("xpb", [128, DC * WH], BF16)
    g_a1 = Grp("right")
    x0st0 = g_a1.sb("x0st0", [128, WH])
    x0st1 = g_a1.sb("x0st1", [128, WH])
    sqa = g_a1.sb("sqa", [128, WH], BF16)
    rstd_bc = g_a1.sb("rstd_bc", [128, WH])
    nmrs_bc = g_a1.sb("nmrs_bc", [128, WH])
    stat = g_a1.sb("stat", [1, 3 * WH])

    nsl_wh = [(0, 512), (512, 512), (1024, 3)]
    s0 = stat[0:1, 0:WH]; s1 = stat[0:1, WH:2 * WH]; s2 = stat[0:1, 2 * WH:3 * WH]
    for dc in range(DC):
        x0c_ = [x0st0, x0st1][dc % 2]
        k.dma((dc % 4) * 2, lambda e, dc=dc, x0c_=x0c_: e.dma_start(
                out=x0c_[:, :], in_=x0p[dc * 128:(dc + 1) * 128, :]),
              writes=[buf(f"x0s_{dc % 2}")])
        k.op("vector", lambda e, dc=dc, x0c_=x0c_: e.tensor_copy(
                out=xpb[:, dc * WH:(dc + 1) * WH], in_=x0c_[:, :]),
             reads=[buf(f"x0s_{dc % 2}")], writes=[buf(f"xpb_{dc}")])
        k.op("scalar", lambda e, dc=dc: e.activation(
                out=sqa[:, :], in_=xpb[:, dc * WH:(dc + 1) * WH], func=AF.Square),
             reads=[buf(f"xpb_{dc}")], writes=[buf("sq_0")])
        for si, (o, cur) in enumerate(nsl_wh):
            last = dc == DC - 1 and si == 2
            k.op("tensor", lambda e, dc=dc, o=o, cur=cur: e.matmul(
                    PS12[0:1, o:o + cur], ones_b[:, :],
                    xpb[:, dc * WH + o:dc * WH + o + cur],
                    start=(dc == 0), stop=(dc == DC - 1)),
                 reads=[buf(f"xpb_{dc}"), cONES_B], writes=[P12] if last else ())
            k.op("tensor", lambda e, o=o, cur=cur, dc=dc: e.matmul(
                    PS3[0:1, o:o + cur], ones_b[:, :], sqa[:, o:o + cur],
                    start=(dc == 0), stop=(dc == DC - 1)),
                 reads=[buf("sq_0"), cONES_B], writes=[P3] if last else ())
    k.op("vector", lambda e: e.tensor_scalar_mul(s0, PS12[0:1, 0:WH], 1.0 / D),
         reads=[P12], writes=[buf("st0")])
    k.op("vector", lambda e: e.tensor_scalar_mul(s1, PS3[0:1, 0:WH], 1.0 / D),
         reads=[P3], writes=[buf("st1")])
    k.op("vector", lambda e: e.tensor_mul(s2, s0, s0),
         reads=[buf("st0")], writes=[buf("st2")])
    k.op("vector", lambda e: e.tensor_sub(s1, s1, s2),
         reads=[buf("st1"), buf("st2")], writes=[buf("st1")])
    k.op("scalar", lambda e: e.activation(out=s1, in_=s1, func=AF.Sqrt, bias=cst[0:1, 0:1]),
         reads=[buf("st1"), buf("cst")], writes=[buf("st1")])
    k.op("vector", lambda e: e.reciprocal(s1, s1),
         reads=[buf("st1")], writes=[buf("st1")])
    k.op("vector", lambda e: e.tensor_mul(s0, s0, s1),
         reads=[buf("st0"), buf("st1")], writes=[buf("st0")])
    k.op("vector", lambda e: e.tensor_scalar_mul(s0, s0, -1.0),
         reads=[buf("st0")], writes=[buf("st0")])
    for si, (o, cur) in enumerate(nsl_wh):
        last = si == 2
        k.op("tensor", lambda e, o=o, cur=cur: e.matmul(
                PS12[:, o:o + cur], ones_row[:, :], s1[0:1, o:o + cur],
                start=True, stop=True),
             reads=[cONES_R, buf("st1")], writes=[P12] if last else (), inc=last)
        k.op("tensor", lambda e, o=o, cur=cur: e.matmul(
                PS3[:, o:o + cur], ones_row[:, :], s0[0:1, o:o + cur],
                start=True, stop=True),
             reads=[cONES_R, buf("st0")], writes=[P3] if last else (), inc=last)
    k.op("vector", lambda e: e.tensor_copy(out=rstd_bc[:, :], in_=PS12[:, 0:WH]),
         reads=[P12], writes=[buf("rstd_bc")])
    k.op("vector", lambda e: e.tensor_copy(out=nmrs_bc[:, :], in_=PS3[:, 0:WH]),
         reads=[P3], writes=[buf("nmrs_bc")])
    for dc in range(DC):
        x0c_ = [x0st0, x0st1][dc % 2]
        k.dma((dc % 4) * 2, lambda e, dc=dc, x0c_=x0c_: e.dma_start(
                out=x0c_[:, :], in_=x0p[dc * 128:(dc + 1) * 128, :]),
              writes=[buf(f"x0s_{dc % 2}")])
        eng = "vector" if dc % 2 == 0 else "gpsimd"
        sc = [e0, e1][dc % 2]
        k.op(eng, lambda e, dc=dc, sc=sc, x0c_=x0c_: e.tensor_mul(
                sc[:, :], x0c_[:, 3:WH], rstd_bc[:, 3:WH]),
             reads=[buf(f"x0s_{dc % 2}"), buf("rstd_bc")], writes=[buf(f"sc{dc % 2}")])
        k.op(eng, lambda e, dc=dc, sc=sc: e.tensor_add(
                xpb[:, dc * WH + 3:(dc + 1) * WH], sc[:, :], nmrs_bc[:, 3:WH]),
             reads=[buf(f"sc{dc % 2}"), buf("nmrs_bc")], writes=[buf(f"xpb_{dc}")])
        k.op(eng, lambda e, dc=dc, sc=sc, x0c_=x0c_: e.tensor_mul(
                sc[:, 0:3], x0c_[:, 0:3], rstd_bc[:, 0:3]),
             reads=[buf(f"x0s_{dc % 2}"), buf("rstd_bc"), buf(f"xpb_{dc}")],
             writes=[buf(f"sc{dc % 2}")])
        k.op(eng, lambda e, dc=dc, sc=sc: e.tensor_add(
                xpb[:, dc * WH:dc * WH + 3], sc[:, 0:3], nmrs_bc[:, 0:3]),
             reads=[buf(f"sc{dc % 2}")], writes=[buf(f"xpb_h{dc}")])

    xpb_bufs = [buf(f"xpb_{dc}") for dc in range(DC)] + [buf(f"xpb_h{dc}") for dc in range(DC)]
    fence_a1 = fence([buf("x0s_0"), buf("x0s_1"), buf("sq_0"), buf("rstd_bc"),
                      buf("nmrs_bc"), buf("st0"), buf("st1"), buf("st2"),
                      buf("sc0"), buf("sc1")])
    g_a1.free()

    g_left = Grp()
    left = g_left.sb("left", [128, HC * W], BF16)
    g_conv = Grp()
    convb = g_conv.sb("convb", [128, HC * W], BF16)
    g_a2t = Grp("right")

    if "attn_a" in STAGES:
        pend = {0: stream_weight("wl0", wlp, 0, DC, wrawA, wbfA0)}
        for m in range(HC):
            wbf = [wbfA0, wbfA1][m % 2]
            wbufs = pend.pop(m)
            if m + 1 < HC:
                pend[m + 1] = stream_weight(f"wl{(m + 1) % 2}", wlp, m + 1, DC, wrawA,
                                            [wbfA0, wbfA1][(m + 1) % 2])
            else:
                pend[0] = stream_weight("wr0", wrp, 0, DC, wrawA, wbfA0)
            ps, psb = [(PS3, P3), (PS12, P12)][m % 2]
            mm_acc(ps, psb, wbf, wbufs, DC,
                   lambda kc, o, cur: xpb[:, kc * WH + 3 + o:kc * WH + 3 + o + cur],
                   W, xpb_bufs)
            k.op("scalar", lambda e, ps=ps, m=m: e.activation(
                    out=left[:, m * W:(m + 1) * W], in_=ps[:, 0:W],
                    func=AF.Gelu_apprx_tanh, bias=bl_t[:, m:m + 1]),
                 reads=[psb, buf("bl_t"), fence_a1], writes=[buf(f"left_{m}")])

        for m in range(HC):
            wbf = [wbfA0, wbfA1][m % 2]
            wbufs = pend.pop(m)
            if m + 1 < HC:
                pend[m + 1] = stream_weight(f"wr{(m + 1) % 2}", wrp, m + 1, DC, wrawA,
                                            [wbfA0, wbfA1][(m + 1) % 2])
            ps, psb = [(PS12, P12), (PS3, P3)][m % 2]
            mm_acc(ps, psb, wbf, wbufs, DC,
                   lambda kc, o, cur: xpb[:, kc * WH + o:kc * WH + o + cur],
                   WH, xpb_bufs)
            rin = rin0
            rb = buf("rin0")
            k.op("scalar", lambda e, ps=ps, rin=rin, m=m: e.activation(
                    out=rin[:, :], in_=ps[:, 0:WH], func=AF.Identity, bias=br_t[:, m:m + 1]),
                 reads=[psb, buf("br_t"), fence_a1], writes=[rb])
            k.op("vector", lambda e, rin=rin: e.tensor_scalar(
                    out=rin[:, 0:3], in0=rin[:, 0:3], scalar1=hv_b[:, 0:1], scalar2=None,
                    op0=ALU.mult),
                 reads=[rb, buf("hv_b")], writes=[rb])
            cacc = cacc0
            cb = buf("cacc0")
            k.op("vector", lambda e, rin=rin, cacc=cacc, m=m: e.tensor_scalar(
                    out=cacc[:, :], in0=rin[:, 0:W], scalar1=cw_t[:, 4 * m:4 * m + 1],
                    scalar2=None, op0=ALU.mult),
                 reads=[rb, buf("cw_t")], writes=[cb])
            for j in (1, 2, 3):
                eng = "vector"
                k.op(eng, lambda e, rin=rin, cacc=cacc, m=m, j=j: e.scalar_tensor_tensor(
                        out=cacc[:, :], in0=rin[:, j:j + W],
                        scalar=cw_t[:, 4 * m + j:4 * m + j + 1],
                        in1=cacc[:, :], op0=ALU.mult, op1=ALU.add),
                     reads=[rb, buf("cw_t"), cb], writes=[cb])
            k.op("vector", lambda e, cacc=cacc, m=m: e.tensor_scalar(
                    out=convb[:, m * W:(m + 1) * W], in0=cacc[:, :],
                    scalar1=cb_t[:, m:m + 1], scalar2=None, op0=ALU.add),
                 reads=[cb, buf("cb_t"), fence_a1], writes=[buf(f"convb_{m}")])

    conv_bufs = [buf(f"convb_{m}") for m in range(HC)]
    left_bufs = [buf(f"left_{m}") for m in range(HC)]
    fence_a2 = fence(xpb_bufs + [buf("u0"), buf("rin0"), buf("cacc0"), buf("e2b0")])
    g_a2t.free()
    g_xp.free()

    g_a3 = Grp("right")
    hbuf = g_a3.sb("hbuf", [128, W])
    cpbuf = g_a3.sb("cpbuf", [128, W])
    wrawI = g_a3.sb("wrawI", [128, HC * 128])
    wbfI = g_a3.sb("wbfI", [128, HC * 128], BF16)
    wbfI2 = g_a3.sb("wbfI2", [128, HC * 128], BF16)
    lcst = g_a3.sb("lcst", [128, W], BF16)
    lcst2 = g_a3.sb("lcst2", [128, W], BF16)

    if "attn_a" in STAGES:
        pendA = {0: stream_weight("wa0", wap, 0, HC, wrawA, wbfA0, "scalar")}
        pendI = {0: stream_weight("wi0", wip, 0, HC, wrawI, wbfI, "gpsimd",
                                  extra=[fence_a2], soff=1)}
        for m in range(HC):
            wbA = [wbfA0, wbfA1][m % 2]
            wbI = [wbfI, wbfI2][m % 2]
            wbufsA = pendA.pop(m)
            wbufsI = pendI.pop(m)
            if m + 1 < HC:
                pendA[m + 1] = stream_weight(f"wa{(m + 1) % 2}", wap, m + 1, HC, wrawA,
                                             [wbfA0, wbfA1][(m + 1) % 2], "scalar")
                pendI[m + 1] = stream_weight(f"wi{(m + 1) % 2}", wip, m + 1, HC, wrawI,
                                             [wbfI, wbfI2][(m + 1) % 2], "gpsimd",
                                             soff=1)
            psR, psRb = [(PS3, P3), (PS12, P12)][m % 2]
            psI, psIb = [(PS12, P12), (PS3, P3)][m % 2]
            for kc in range(HC):
                for o in (0, 512):
                    last = kc == HC - 1 and o == 512
                    first = kc == 0 and o == 0
                    k.op("tensor", lambda e, kc=kc, o=o, wbA=wbA, psR=psR: e.matmul(
                            psR[:, o:o + 512], wbA[:, kc * 128:(kc + 1) * 128],
                            convb[:, kc * W + o:kc * W + o + 512],
                            start=(kc == 0), stop=(kc == HC - 1)),
                         reads=wbufsA + conv_bufs,
                         writes=[psRb] if (last or first) else (), inc=last)
                    k.op("tensor", lambda e, kc=kc, o=o, wbI=wbI, psI=psI: e.matmul(
                            psI[:, o:o + 512], wbI[:, kc * 128:(kc + 1) * 128],
                            convb[:, kc * W + o:kc * W + o + 512],
                            start=(kc == 0), stop=(kc == HC - 1)),
                         reads=wbufsI + conv_bufs + [fence_a2],
                         writes=[psIb] if (last or first) else (), inc=last)
            k.op("scalar", lambda e, psR=psR, m=m: e.activation(
                    out=rin0[:, 0:W], in_=psR[:, 0:W], func=AF.Sigmoid,
                    bias=ba_t[:, m:m + 1]),
                 reads=[psRb, buf("ba_t"), fence_a2], writes=[buf("rin0")])
            k.op("scalar", lambda e, psI=psI, m=m: e.activation(
                    out=cacc0[:, :], in_=psI[:, 0:W], func=AF.Sigmoid,
                    bias=bi_t[:, m:m + 1]),
                 reads=[psIb, buf("bi_t"), fence_a2], writes=[buf("cacc0")])
            k.op("scalar", lambda e, m=m: e.activation(
                    out=e1[:, :], in_=rin0[:, 0:W], func=AF.Exp,
                    scale=slam_t[:, m:m + 1]),
                 reads=[buf("rin0"), buf("slam")], writes=[buf("e1")])
            k.op("scalar", lambda e, m=m: e.activation(
                    out=e2b[:, :], in_=rin0[:, 0:W], func=AF.Exp,
                    scale=slam2_t[:, m:m + 1]),
                 reads=[buf("rin0"), buf("slam2")], writes=[buf("e2b")])
            k.op("vector", lambda e: e.tensor_scalar(out=e2b[:, :], in0=e2b[:, :],
                                                     scalar1=-1.0, scalar2=1.0,
                                                     op0=ALU.mult, op1=ALU.add),
                 reads=[buf("e2b")], writes=[buf("e2b")])
            k.op("vector", lambda e: e.tensor_scalar_max(e2b[:, :], e2b[:, :], 1e-12),
                 reads=[buf("e2b")], writes=[buf("e2b")])
            k.op("scalar", lambda e: e.activation(out=e2b[:, :], in_=e2b[:, :], func=AF.Sqrt),
                 reads=[buf("e2b")], writes=[buf("e2b")])
            k.op("gpsimd", lambda e, m=m: e.tensor_mul(
                    ubuf0[:, :], cacc0[:, :], convb[:, m * W:(m + 1) * W]),
                 reads=[buf("cacc0"), conv_bufs[m]], writes=[buf("ubuf0")])
            k.op("gpsimd", lambda e: e.tensor_mul(ubuf0[:, :], ubuf0[:, :], e2b[:, :]),
                 reads=[buf("ubuf0"), buf("e2b")], writes=[buf("ubuf0")])
            k.op("vector", lambda e: e.tensor_tensor_scan(
                    out=hbuf[:, :], data0=e1[:, :], data1=ubuf0[:, :], initial=0.0,
                    op0=ALU.mult, op1=ALU.add),
                 reads=[buf("e1"), buf("ubuf0")], writes=[buf("hbuf")])
            k.op("vector", lambda e: e.tensor_tensor_scan(
                    out=cpbuf[:, :], data0=e1[:, :], data1=zeros_bw[:, :], initial=1.0,
                    op0=ALU.mult, op1=ALU.add),
                 reads=[buf("e1"), cZER], writes=[buf("cpbuf")])
            k.op("vector", lambda e, m=m: e.reduce_sum(
                    out=atot_t[:, m:m + 1], in_=rin0[:, 0:W], axis=AX.X),
                 reads=[buf("rin0")], writes=[buf("atot")])
            k.op("scalar", lambda e, m=m: e.activation(
                    out=atot_t[:, m:m + 1], in_=atot_t[:, m:m + 1], func=AF.Exp,
                    scale=slam_t[:, m:m + 1]),
                 reads=[buf("atot"), buf("slam")], writes=[buf("atot")])
            k.op("vector", lambda e, m=m: e.tensor_copy(
                    out=hlast_t[:, m:m + 1], in_=hbuf[:, W - 1:W]),
                 reads=[buf("hbuf")], writes=[buf("hlast")])
            k.op("gpsimd", lambda e, m=m: e.tensor_mul(
                    lcst[:, :], left[:, m * W:(m + 1) * W], cpbuf[:, :]),
                 reads=[left_bufs[m], buf("cpbuf"), fence_a2, buf("lcst")],
                 writes=[buf("lcst")])
            k.dma(5, lambda e, m=m: e.dma_start(
                    out=lcp_dram[m * 128:(m + 1) * 128, :], in_=lcst[:, :]),
                  reads=[buf("lcst")], writes=[buf("lcpd")])
            k.op("gpsimd", lambda e, m=m: e.tensor_mul(
                    left[:, m * W:(m + 1) * W], left[:, m * W:(m + 1) * W], hbuf[:, :]),
                 reads=[left_bufs[m], buf("hbuf")], writes=[left_bufs[m]])

        k.dma(6, lambda e: e.dma_start(
                out=relay_in[0:1, 0:HID].rearrange("a (p c) -> (a p) c", p=128),
                in_=atot_t[:, :]),
              reads=[buf("atot")], writes=[buf("relay_in")])
        k.dma(6, lambda e: e.dma_start(
                out=relay_in[0:1, HID:2 * HID].rearrange("a (p c) -> (a p) c", p=128),
                in_=hlast_t[:, :]),
              reads=[buf("hlast")], writes=[buf("relay_in")])
        k.cc(lambda e: e.collective_compute(
                "AllGather", ALU.bypass, replica_groups=GROUPS4,
                ins=[relay_in], outs=[relay_out]),
             reads=[buf("relay_in")], writes=[buf("relay_out")])
        for j in range(4):
            k.dma(6, lambda e, j=j: e.dma_start(
                    out=rel[:, j * HC:(j + 1) * HC],
                    in_=relay_out[j:j + 1, 0:HID].rearrange("a (p c) -> (a p) c", p=128)),
                  reads=[buf("relay_out")], writes=[buf("rel")])
            k.dma(6, lambda e, j=j: e.dma_start(
                    out=rel[:, (4 + j) * HC:(5 + j) * HC],
                    in_=relay_out[j:j + 1, HID:2 * HID].rearrange("a (p c) -> (a p) c", p=128)),
                  reads=[buf("relay_out")], writes=[buf("rel")])
        k.op("vector", lambda e: e.memset(Hst[:, :], 0.0), writes=[buf("Hst")])
        for j in range(4):
            k.op("vector", lambda e, j=j: e.tensor_scalar(
                    out=e0[:, 0:HC], in0=rel[:, j * HC:(j + 1) * HC],
                    scalar1=qsel_b[:, j:j + 1], scalar2=omq_b[:, j:j + 1],
                    op0=ALU.mult, op1=ALU.add),
                 reads=[buf("rel"), buf("qsel_b"), buf("omq_b")], writes=[buf("e0")])
            k.op("vector", lambda e, j=j: e.tensor_scalar(
                    out=e1[:, 0:HC], in0=rel[:, (4 + j) * HC:(5 + j) * HC],
                    scalar1=qsel_b[:, j:j + 1], scalar2=None, op0=ALU.mult),
                 reads=[buf("rel"), buf("qsel_b")], writes=[buf("e1")])
            k.op("vector", lambda e: e.tensor_mul(Hst[:, :], Hst[:, :], e0[:, 0:HC]),
                 reads=[buf("Hst"), buf("e0")], writes=[buf("Hst")])
            k.op("vector", lambda e: e.tensor_add(Hst[:, :], Hst[:, :], e1[:, 0:HC]),
                 reads=[buf("Hst"), buf("e1")], writes=[buf("Hst")])
        for m in range(HC):
            lcsX = [lcst, lcst2][m % 2]
            LCB = buf(f"lcst{m % 2}")
            k.dma([5, 13][m % 2], lambda e, m=m, lcsX=lcsX: e.dma_start(
                    out=lcsX[:, :], in_=lcp_dram[m * 128:(m + 1) * 128, :]),
                  reads=[buf("lcpd"), buf("lcst"), LCB], writes=[LCB])
            k.op("vector", lambda e, m=m, lcsX=lcsX: e.scalar_tensor_tensor(
                    out=left[:, m * W:(m + 1) * W], in0=lcsX[:, :],
                    scalar=Hst[:, m:m + 1], in1=left[:, m * W:(m + 1) * W],
                    op0=ALU.mult, op1=ALU.add),
                 reads=[LCB, buf("Hst"), left_bufs[m]],
                 writes=[left_bufs[m]])

    fence_a3 = fence(conv_bufs + [buf("lcst"), buf("lcst0"), buf("lcst1"),
                                  buf("hbuf"), buf("cpbuf")] +
                     [buf(n) for n in list(B) if n.startswith(("raw_", "bf_"))])
    g_a3.free()
    g_conv.free()

    g_a5 = Grp("right")
    x = g_a5.sb("x", [128, DC * W])
    g_a5r = Grp("right")
    x0c0 = g_a5r.sb("x0c0", [128, W])
    x0c1 = x0c0

    if "attn_a" in STAGES:
        pend = {0: stream_weight("wo0", wop, 0, HC, wrawA, wbfA0)}
        for mo in range(DC):
            wbf = [wbfA0, wbfA1][mo % 2]
            wbufs = pend.pop(mo)
            if mo + 1 < DC:
                pend[mo + 1] = stream_weight(f"wo{(mo + 1) % 2}", wop, mo + 1, HC, wrawA,
                                             [wbfA0, wbfA1][(mo + 1) % 2])
            ps, psb = [(PS3, P3), (PS12, P12)][mo % 2]
            mm_acc(ps, psb, wbf, wbufs, HC,
                   lambda kc, o, cur: left[:, kc * W + o:kc * W + o + cur],
                   W, left_bufs)
            x0c = x0c0
            k.dma(4, lambda e, mo=mo, x0c=x0c: e.dma_start(
                    out=x0c[:, :], in_=x0p[mo * 128:(mo + 1) * 128, 3:WH]),
                  reads=[fence_a3, buf("x0c0")], writes=[buf("x0c0")])
            k.op("vector", lambda e, ps=ps, mo=mo, x0c=x0c: e.scalar_tensor_tensor(
                    out=x[:, mo * W:(mo + 1) * W], in0=ps[:, 0:W],
                    scalar=bo_t[:, mo:mo + 1], in1=x0c[:, :], op0=ALU.add, op1=ALU.add),
                 reads=[psb, buf("bo_t"), buf("x0c0"), fence_a3],
                 writes=[buf(f"x_{mo}")])
    else:
        for mo in range(DC):
            k.dma(4, lambda e, mo=mo: e.dma_start(
                    out=x[:, mo * W:(mo + 1) * W], in_=x0p[mo * 128:(mo + 1) * 128, 3:WH]),
                  reads=[fence_a3], writes=[buf(f"x_{mo}")])

    x_bufs = [buf(f"x_{mo}") for mo in range(DC)]
    fence_a5 = fence(left_bufs +
                     [buf(n) for n in list(B) if n.startswith(("raw_", "bf_"))] +
                     [buf("x0c0")])
    g_a5r.free()
    g_left.free()
    g_attA.free()

    gFper = Grp()
    wrt_t = gFper.sb("wrt_t", [128, DC]); g1_t = gFper.sb("g1_t", [128, DC])
    g2_t = gFper.sb("g2_t", [128, DC])
    blk_t = gFper.sb("blk_t", [128, 4]); qsel4_t = gFper.sb("qsel4_t", [4, 1])
    ones_f = gFper.sb("ones_f", [128, 1])
    cfm = gFper.sb("cfm", [128, 1024], BF16); sfm = gFper.sb("sfm", [128, 1024], BF16)
    fones = gFper.sb("fones", [1, 1024])
    crawf = gFper.sb("crawf", [128, 1024])
    cONESF = buf("cONESF")
    k.op("vector", lambda e: e.memset(ones_f[:, :], 1.0), reads=[fence_a5], writes=[cONESF])
    k.op("vector", lambda e: e.memset(fones[:, :], 1.0), reads=[fence_a5], writes=[buf("fones")])
    for nm, tt_, ap_ in [("wrt_t", wrt_t, wrt_i), ("g1_t", g1_t, g1_i), ("g2_t", g2_t, g2_i),
                         ("blk_t", blk_t, blk_i), ("qsel4_t", qsel4_t, qsel4_i)]:
        k.dma(7, lambda e, t=tt_, a=ap_: e.dma_start(out=t[:, :], in_=a),
              reads=[fence_a5], writes=[buf(nm)])
    k.dma(7, lambda e: e.dma_start(out=crawf[0:64, :], in_=fcos_i),
          reads=[fence_a5], writes=[buf("crawf")])
    k.dma(7, lambda e: e.dma_start(out=crawf[64:128, :], in_=fcos_i),
          reads=[fence_a5], writes=[buf("crawf")])
    k.op("vector", lambda e: e.tensor_copy(out=cfm[:, :], in_=crawf[:, :]),
         reads=[buf("crawf"), fence_a5], writes=[buf("cfm")])
    k.dma(7, lambda e: e.dma_start(out=crawf[0:64, :], in_=fsin_i),
          reads=[buf("cfm")], writes=[buf("crawf")])
    k.dma(7, lambda e: e.dma_start(out=crawf[64:128, :], in_=fsin_i),
          reads=[buf("cfm")], writes=[buf("crawf")])
    k.op("vector", lambda e: e.tensor_copy(out=sfm[:, :], in_=crawf[:, :]),
         reads=[buf("crawf")], writes=[buf("sfm")])

    def fraw_bufs(fr, o0, o1):
        return [buf(f"raw_{id(fr)}_{g}") for g in range(o0 // 128, o1 // 128)]

    def emit_ffn(tag, xt, x_bufs, fence_in, free_x):
        T = tag
        gF1 = Grp()
        xpbf = gF1.sb(T + "xpbf", [128, DC * W], BF16)
        rstdb = gF1.sb(T + "rstdb", [128, W])
        nmrsb = gF1.sb(T + "nmrsb", [128, W])
        statf = gF1.sb(T + "statf", [1, 3 * W])
        sqf = gF1.sb(T + "sqf", [128, W], BF16)
        xc = gF1.sb(T + "xc", [128, W])
        lgt = gF1.sb(T + "lgt", [128, 36])
        cscr = gF1.sb(T + "cscr", [128, 36])
        lohi = gF1.sb(T + "lohi", [1, 12])
        llocal = gF1.sb(T + "llocal", [1, 1024])
        lrow2 = gF1.sb(T + "lrow2", [1, 2 * 1024])
        idxi_row = gF1.sb(T + "idxi_row", [1, 1024], I32)
        trs = gF1.sb(T + "trs", [128, 128])
        scst = gF1.sb(T + "scst", [128, 2048], BF16)
        idxt = gF1.sb(T + "idxt", [128, 8], I32)
        TRS = buf(T + "trs")

        s0 = statf[0:1, 0:W]; s1 = statf[0:1, W:2 * W]; s2 = statf[0:1, 2 * W:3 * W]
        for dc in range(DC):
            k.op("vector", lambda e, dc=dc: e.tensor_copy(
                    out=xpbf[:, dc * W:(dc + 1) * W], in_=xt[:, dc * W:(dc + 1) * W]),
                 reads=[x_bufs[dc], fence_in], writes=[buf(T + f"xb_{dc}")])
            k.op("scalar", lambda e, dc=dc: e.activation(
                    out=sqf[:, :], in_=xpbf[:, dc * W:(dc + 1) * W], func=AF.Square),
                 reads=[buf(T + f"xb_{dc}")], writes=[buf(T + "sq")])
            for o in (0, 512):
                k.op("tensor", lambda e, dc=dc, o=o: e.matmul(
                        PS12[0:1, o:o + 512], ones_b[:, :],
                        xpbf[:, dc * W + o:dc * W + o + 512],
                        start=(dc == 0), stop=(dc == DC - 1)),
                     reads=[buf(T + f"xb_{dc}"), cONES_B], writes=[P12])
                k.op("tensor", lambda e, o=o, dc=dc: e.matmul(
                        PS3[0:1, o:o + 512], ones_b[:, :], sqf[:, o:o + 512],
                        start=(dc == 0), stop=(dc == DC - 1)),
                     reads=[buf(T + "sq"), cONES_B], writes=[P3])
        k.op("vector", lambda e: e.tensor_scalar_mul(s0, PS12[0:1, 0:W], 1.0 / D),
             reads=[P12], writes=[buf(T + "st0")])
        k.op("vector", lambda e: e.tensor_scalar_mul(s1, PS3[0:1, 0:W], 1.0 / D),
             reads=[P3], writes=[buf(T + "st1")])
        k.op("vector", lambda e: e.tensor_mul(s2, s0, s0),
             reads=[buf(T + "st0")], writes=[buf(T + "st2")])
        k.op("vector", lambda e: e.tensor_sub(s1, s1, s2),
             reads=[buf(T + "st1"), buf(T + "st2")], writes=[buf(T + "st1")])
        k.op("scalar", lambda e: e.activation(out=s1, in_=s1, func=AF.Sqrt,
                                              bias=cst[0:1, 0:1]),
             reads=[buf(T + "st1"), buf("cst")], writes=[buf(T + "st1")])
        k.op("vector", lambda e: e.reciprocal(s1, s1),
             reads=[buf(T + "st1")], writes=[buf(T + "st1")])
        k.op("vector", lambda e: e.tensor_mul(s0, s0, s1),
             reads=[buf(T + "st0"), buf(T + "st1")], writes=[buf(T + "st0")])
        k.op("vector", lambda e: e.tensor_scalar_mul(s0, s0, -1.0),
             reads=[buf(T + "st0")], writes=[buf(T + "st0")])
        for o in (0, 512):
            k.op("tensor", lambda e, o=o: e.matmul(
                    PS12[:, o:o + 512], ones_row[:, :], s1[0:1, o:o + 512],
                    start=True, stop=True),
                 reads=[cONES_R, buf(T + "st1")], writes=[P12])
            k.op("tensor", lambda e, o=o: e.matmul(
                    PS3[:, o:o + 512], ones_row[:, :], s0[0:1, o:o + 512],
                    start=True, stop=True),
                 reads=[cONES_R, buf(T + "st0")], writes=[P3])
        k.op("vector", lambda e: e.tensor_copy(out=rstdb[:, :], in_=PS12[:, 0:W]),
             reads=[P12], writes=[buf(T + "rstdb")])
        k.op("vector", lambda e: e.tensor_copy(out=nmrsb[:, :], in_=PS3[:, 0:W]),
             reads=[P3], writes=[buf(T + "nmrsb")])
        for dc in range(DC):
            k.op("vector", lambda e, dc=dc: e.tensor_mul(
                    xc[:, :], xt[:, dc * W:(dc + 1) * W], rstdb[:, :]),
                 reads=[x_bufs[dc], buf(T + "rstdb"), buf(T + "xcb")],
                 writes=[buf(T + "xcb")])
            k.op("vector", lambda e, dc=dc: e.tensor_add(xc[:, :], xc[:, :], nmrsb[:, :]),
                 reads=[buf(T + "xcb"), buf(T + "nmrsb")], writes=[buf(T + "xcb")])
            for o in (0, 512):
                k.op("tensor", lambda e, dc=dc, o=o: e.matmul(
                        PS12[0:1, o:o + 512], wrt_t[:, dc:dc + 1], xc[:, o:o + 512],
                        start=(dc == 0), stop=(dc == DC - 1)),
                     reads=[buf("wrt_t"), buf(T + "xcb")], writes=[P12])
            k.op("gpsimd", lambda e, dc=dc: e.tensor_copy(
                    out=xpbf[:, dc * W:(dc + 1) * W], in_=xc[:, :]),
                 reads=[buf(T + "xcb")], writes=[buf(T + f"xb_{dc}")])
            k.op("gpsimd", lambda e, dc=dc: e.tensor_add(
                    xt[:, dc * W:(dc + 1) * W], xt[:, dc * W:(dc + 1) * W], xc[:, :]),
                 reads=[buf(T + "xcb"), x_bufs[dc]], writes=[x_bufs[dc]])
            k.dma((dc % 4) * 2, lambda e, dc=dc: e.dma_start(
                    out=xst_dram[:, dc * W:(dc + 1) * W], in_=xt[:, dc * W:(dc + 1) * W]),
                  reads=[x_bufs[dc]], writes=[buf("xst")])
        xp_bufs = [buf(T + f"xb_{dc}") for dc in range(DC)]
        k.op("vector", lambda e: e.tensor_copy(out=llocal[:, :], in_=PS12[0:1, 0:1024]),
             reads=[P12], writes=[buf(T + "llocal")])
        k.dma(6, lambda e: e.dma_start(out=lg_in, in_=llocal[:, :]),
              reads=[buf(T + "llocal")], writes=[buf("lg_in")])
        k.cc(lambda e: e.collective_compute(
                "AllGather", ALU.bypass, replica_groups=GROUPS4,
                ins=[lg_in], outs=[lg_out]),
             reads=[buf("lg_in")], writes=[buf("lg_out")])
        k.dma(6, lambda e: e.dma_start(
                out=lgt[:, 0:32], in_=lg_out.rearrange("a (p c) -> (a p) c", p=128)),
              reads=[buf("lg_out")], writes=[buf(T + "lgt")])
        for ch in range(8):
            for hh in range(2):
                k.dma(2, lambda e, ch=ch, hh=hh: e.dma_start(
                        out=cpt_in[ch * 128:(ch + 1) * 128, hh * 1024:(hh + 1) * 1024],
                        in_=zeros_bw[:, :]),
                      reads=[cZER], writes=[buf("cpt_in")])
        k.op("vector", lambda e: e.memset(trs[:, :], 0.0), writes=[TRS])
        k.op("vector", lambda e: e.tensor_reduce(
                out=trs[:, 0:1], in_=lgt[:, 0:32], axis=AX.X, op=ALU.min),
             reads=[buf(T + "lgt")], writes=[TRS])
        k.op("vector", lambda e: e.reduce_max(out=trs[:, 32:33], in_=lgt[:, 0:32], axis=AX.X),
             reads=[buf(T + "lgt")], writes=[TRS])
        k.op("tensor", lambda e: e.transpose(PS3[:, 0:128], trs[:, :], identf[:, :]),
             reads=[TRS, buf("identf")], writes=[P3])
        k.op("vector", lambda e: e.tensor_reduce(
                out=lohi[0:1, 0:1], in_=PS3[0:1, 0:128], axis=AX.X, op=ALU.min),
             reads=[P3], writes=[buf(T + "lo")])
        k.op("vector", lambda e: e.reduce_max(
                out=lohi[0:1, 1:2], in_=PS3[32:33, 0:128], axis=AX.X),
             reads=[P3], writes=[buf(T + "hi")])
        k.op("vector", lambda e: e.tensor_scalar_add(lohi[0:1, 0:1], lohi[0:1, 0:1], -1.0),
             reads=[buf(T + "lo")], writes=[buf(T + "lo")])
        for it in range(48):
            k.op("vector", lambda e: e.tensor_add(lohi[0:1, 2:3], lohi[0:1, 0:1],
                                                  lohi[0:1, 1:2]),
                 reads=[buf(T + "lo"), buf(T + "hi")], writes=[buf(T + "mid")])
            k.op("vector", lambda e: e.tensor_scalar_mul(lohi[0:1, 2:3], lohi[0:1, 2:3], 0.5),
                 reads=[buf(T + "mid")], writes=[buf(T + "mid")])
            k.op("tensor", lambda e: e.matmul(PS3[:, 140:141], ones_row[:, :],
                                              lohi[0:1, 2:3], start=True, stop=True),
                 reads=[cONES_R, buf(T + "mid")], writes=[P3])
            k.op("vector", lambda e: e.tensor_scalar(
                    out=cscr[:, 0:32], in0=lgt[:, 0:32], scalar1=PS3[:, 140:141],
                    scalar2=0.0, op0=ALU.is_gt, op1=ALU.add, accum_out=cscr[:, 34:35]),
                 reads=[buf(T + "lgt"), P3], writes=[buf(T + "csum")])
            k.op("tensor", lambda e: e.matmul(PS3[0:1, 141:142], cscr[:, 34:35],
                                              ones_f[:, :], start=True, stop=True),
                 reads=[buf(T + "csum"), cONESF], writes=[P3])
            k.op("vector", lambda e: e.tensor_single_scalar(
                    out=lohi[0:1, 3:4], in_=PS3[0:1, 141:142], scalar=1023.5,
                    op=ALU.is_gt),
                 reads=[P3], writes=[buf(T + "ge")])
            k.op("vector", lambda e: e.tensor_sub(lohi[0:1, 4:5], lohi[0:1, 2:3],
                                                  lohi[0:1, 0:1]),
                 reads=[buf(T + "mid"), buf(T + "lo")], writes=[buf(T + "dm")])
            k.op("vector", lambda e: e.scalar_tensor_tensor(
                    out=lohi[0:1, 0:1], in0=lohi[0:1, 4:5], scalar=lohi[0:1, 3:4],
                    in1=lohi[0:1, 0:1], op0=ALU.mult, op1=ALU.add),
                 reads=[buf(T + "dm"), buf(T + "ge"), buf(T + "lo")],
                 writes=[buf(T + "lo")])
            k.op("vector", lambda e: e.tensor_scalar(
                    out=lohi[0:1, 5:6], in0=lohi[0:1, 3:4], scalar1=-1.0, scalar2=1.0,
                    op0=ALU.mult, op1=ALU.add),
                 reads=[buf(T + "ge")], writes=[buf(T + "omge")])
            k.op("vector", lambda e: e.tensor_sub(lohi[0:1, 6:7], lohi[0:1, 2:3],
                                                  lohi[0:1, 1:2]),
                 reads=[buf(T + "mid"), buf(T + "hi")], writes=[buf(T + "dmh")])
            k.op("vector", lambda e: e.scalar_tensor_tensor(
                    out=lohi[0:1, 1:2], in0=lohi[0:1, 6:7], scalar=lohi[0:1, 5:6],
                    in1=lohi[0:1, 1:2], op0=ALU.mult, op1=ALU.add),
                 reads=[buf(T + "dmh"), buf(T + "omge"), buf(T + "hi")],
                 writes=[buf(T + "hi")])
        k.op("tensor", lambda e: e.matmul(PS3[:, 140:141], ones_row[:, :],
                                          lohi[0:1, 0:1], start=True, stop=True),
             reads=[cONES_R, buf(T + "lo")], writes=[P3])
        k.op("vector", lambda e: e.tensor_copy(out=cscr[:, 35:36], in_=PS3[:, 140:141]),
             reads=[P3], writes=[buf(T + "lobc")])
        k.op("vector", lambda e: e.tensor_scalar(
                out=cscr[:, 0:32], in0=lgt[:, 0:32], scalar1=cscr[:, 35:36], scalar2=None,
                op0=ALU.is_gt),
             reads=[buf(T + "lgt"), buf(T + "lobc")], writes=[buf(T + "mglob")])
        k.op("tensor", lambda e: e.matmul(PS3[0:4, 256:288], blk_t[:, :], cscr[:, 0:32],
                                          start=True, stop=True),
             reads=[buf("blk_t"), buf(T + "mglob")], writes=[P3])
        k.op("vector", lambda e: e.reduce_sum(out=trs[0:4, 1:2], in_=PS3[0:4, 256:288],
                                              axis=AX.X),
             reads=[P3], writes=[TRS])
        k.op("tensor", lambda e: e.matmul(PS3[0:1, 142:143], trs[0:4, 1:2],
                                          qsel4_t[:, :], start=True, stop=True),
             reads=[TRS, buf("qsel4_t")], writes=[P3])
        k.op("vector", lambda e: e.tensor_scalar_add(lohi[0:1, 7:8], PS3[0:1, 142:143],
                                                     -1.0),
             reads=[P3], writes=[buf(T + "cb1")])
        k.op("vector", lambda e: e.tensor_scalar(
                out=cscr[:, 0:32], in0=cscr[:, 0:32], scalar1=1.0e9, scalar2=-1.0e9,
                op0=ALU.mult, op1=ALU.add),
             reads=[buf(T + "mglob")], writes=[buf(T + "mglob")])
        k.op("vector", lambda e: e.tensor_add(cscr[:, 0:32], cscr[:, 0:32], lgt[:, 0:32]),
             reads=[buf(T + "mglob"), buf(T + "lgt")], writes=[buf(T + "mlg")])
        k.op("vector", lambda e: e.reduce_max(out=trs[:, 64:65], in_=cscr[:, 0:32],
                                              axis=AX.X),
             reads=[buf(T + "mlg")], writes=[TRS])
        k.op("tensor", lambda e: e.transpose(PS3[:, 0:128], trs[:, :], identf[:, :]),
             reads=[TRS, buf("identf")], writes=[P3])
        k.op("vector", lambda e: e.reduce_max(out=lohi[0:1, 8:9], in_=PS3[64:65, 0:128],
                                              axis=AX.X),
             reads=[P3], writes=[buf(T + "Mv")])
        k.op("vector", lambda e: e.tensor_scalar_mul(lohi[0:1, 9:10], lohi[0:1, 8:9], -1.0),
             reads=[buf(T + "Mv")], writes=[buf(T + "Mneg")])
        k.op("tensor", lambda e: e.matmul(PS3[:, 140:141], ones_row[:, :], lohi[0:1, 9:10],
                                          start=True, stop=True),
             reads=[cONES_R, buf(T + "Mneg")], writes=[P3])
        k.op("vector", lambda e: e.tensor_copy(out=cscr[:, 35:36], in_=PS3[:, 140:141]),
             reads=[P3, buf(T + "lobc")], writes=[buf(T + "Mnb")])
        k.op("scalar", lambda e: e.activation(
                out=cscr[:, 0:32], in_=cscr[:, 0:32], func=AF.Exp, bias=cscr[:, 35:36],
                accum_out=cscr[:, 34:35]),
             reads=[buf(T + "mlg"), buf(T + "Mnb"), buf(T + "csum")],
             writes=[buf(T + "esum")]),
        k.op("tensor", lambda e: e.matmul(PS3[0:1, 141:142], cscr[:, 34:35], ones_f[:, :],
                                          start=True, stop=True),
             reads=[buf(T + "esum"), cONESF], writes=[P3])
        k.op("vector", lambda e: e.reciprocal(lohi[0:1, 10:11], PS3[0:1, 141:142]),
             reads=[P3], writes=[buf(T + "Zi")])
        k.op("vector", lambda e: e.tensor_single_scalar(
                out=lrow2[0:1, 0:1024], in_=llocal[:, :], scalar=lohi[0:1, 0:1],
                op=ALU.is_gt),
             reads=[buf(T + "llocal"), buf(T + "lo")], writes=[buf(T + "mmy")])
        k.op("vector", lambda e: e.tensor_tensor_scan(
                out=lrow2[0:1, 1024:2048], data0=fones[:, :], data1=lrow2[0:1, 0:1024],
                initial=0.0, op0=ALU.mult, op1=ALU.add),
             reads=[buf(T + "mmy"), buf("fones")], writes=[buf(T + "cmy")])
        k.op("vector", lambda e: e.tensor_single_scalar(
                out=lrow2[0:1, 1024:2048], in_=lrow2[0:1, 1024:2048], scalar=lohi[0:1, 7:8],
                op=ALU.add),
             reads=[buf(T + "cmy"), buf(T + "cb1")], writes=[buf(T + "cmy")])
        k.op("vector", lambda e: e.tensor_scalar(
                out=lrow2[0:1, 0:1024], in0=lrow2[0:1, 0:1024], scalar1=-4096.0,
                scalar2=4096.0, op0=ALU.mult, op1=ALU.add),
             reads=[buf(T + "mmy")], writes=[buf(T + "mmy")])
        k.op("vector", lambda e: e.tensor_add(
                lrow2[0:1, 1024:2048], lrow2[0:1, 1024:2048], lrow2[0:1, 0:1024]),
             reads=[buf(T + "cmy"), buf(T + "mmy")], writes=[buf(T + "cmy")])
        k.op("vector", lambda e: e.tensor_copy(out=idxi_row[0:1, :],
                                               in_=lrow2[0:1, 1024:2048]),
             reads=[buf(T + "cmy")], writes=[buf(T + "idxi")])
        k.dma(6, lambda e: e.dma_start(out=idx_dram, in_=idxi_row[0:1, :]),
              reads=[buf(T + "idxi")], writes=[buf("idx_dram")])
        k.op("scalar", lambda e: e.activation(
                out=lrow2[0:1, 0:1024], in_=llocal[:, :], func=AF.Exp,
                bias=lohi[0:1, 9:10]),
             reads=[buf(T + "llocal"), buf(T + "Mneg")],
             writes=[buf(T + "rw"), buf(T + "mmy")])
        k.op("vector", lambda e: e.tensor_single_scalar(
                out=lrow2[0:1, 0:1024], in_=lrow2[0:1, 0:1024], scalar=lohi[0:1, 10:11],
                op=ALU.mult),
             reads=[buf(T + "rw"), buf(T + "Zi")], writes=[buf(T + "rw")])
        k.dma(6, lambda e: e.dma_start(out=rw_dram, in_=lrow2[0:1, 0:1024]),
              reads=[buf(T + "rw")], writes=[buf("rw_dram")])
        for ch in range(8):
            k.dma(6, lambda e, ch=ch: e.dma_start(
                    out=idxt[:, ch:ch + 1],
                    in_=idx_dram[0:1, ch * 128:(ch + 1) * 128].rearrange(
                        "a (p c) -> (a p) c", p=128)),
                  reads=[buf("idx_dram")], writes=[buf(T + "idxt")])
            for dc in range(DC):
                col = (dc % 8) * 128
                k.op("tensor", lambda e, ch=ch, dc=dc, col=col: e.transpose(
                        PSB[:, col:col + 128],
                        xpbf[:, dc * W + ch * 128:dc * W + (ch + 1) * 128],
                        identb[:, :]),
                     reads=xp_bufs + [buf("identb")],
                     writes=[PB] if dc in (0, 7, 8, 15) else (), inc=dc in (7, 15))
                if dc in (7, 15):
                    for d2 in range(dc - 7, dc + 1):
                        k.op("vector", lambda e, d2=d2: e.tensor_copy(
                                out=scst[:, d2 * 128:(d2 + 1) * 128],
                                in_=PSB[:, (d2 % 8) * 128:(d2 % 8 + 1) * 128]),
                             reads=[PB], writes=[buf(T + "scst")])
            k.dma(5, lambda e, ch=ch: e.indirect_dma_start(
                    out=cpt_in, out_offset=IndirectOffsetOnAxis(idxt[:, ch:ch + 1], 0),
                    in_=scst[:, :], in_offset=None,
                    bounds_check=1023, oob_is_err=False),
                  reads=[buf(T + "scst"), buf(T + "idxt"), buf("cpt_in")],
                  writes=[buf("cpt_in")])
        k.cc(lambda e: e.collective_compute(
                "AllReduce", ALU.add, replica_groups=GROUPS4,
                ins=[cpt_in], outs=[cpt_out]),
             reads=[buf("cpt_in")], writes=[buf("cpt_out")])

        fence_f1 = fence(xp_bufs + list(x_bufs) + [buf(T + n) for n in
                         ("rstdb", "nmrsb", "st0", "st1", "st2", "sq", "xcb",
                          "lgt", "llocal", "trs", "scst", "mglob",
                          "mlg", "esum", "csum", "idxt", "idxi", "mmy", "cmy", "rw",
                          "lo", "hi", "mid", "ge", "dm", "omge", "dmh", "cb1",
                          "Mv", "Mneg", "Zi", "lobc", "Mnb")])
        gF1.free()
        free_x()

        gFil = Grp()
        filt = gFil.sb(T + "filt", [128, DC * W], BF16)
        gFA = Grp()
        hn = gFA.sb(T + "hn", [128, DC * W], BF16)
        qr = gFA.sb(T + "qr", [128, 4 * W], BF16)
        kr = gFA.sb(T + "kr", [128, 4 * W], BF16)
        vtok = gFA.sb(T + "vtok", [128, 8 * 512], BF16)
        obuf = gFA.sb(T + "obuf", [128, 4 * W], BF16)
        PT = gFA.sb(T + "PT", [128, W], BF16)
        Pb = gFA.sb(T + "Pb", [128, W], BF16)
        PTd = gFA.sb(T + "PTd", [128, W], BF16)
        Pbd = gFA.sb(T + "Pbd", [128, W], BF16)
        msk2 = gFA.sb(T + "msk2", [128, W])
        wobf = gFA.sb(T + "wobf", [128, 512], BF16)
        ftst = gFA.sb(T + "ftst", [128, 2048], BF16)
        ftstc = gFA.sb(T + "ftstc", [128, 2048], BF16)
        fraw = gFA.sb(T + "fraw", [128, DC * 128])
        fbf0 = gFA.sb(T + "fbf0", [128, DC * 128], BF16)
        fbf1 = gFA.sb(T + "fbf1", [128, DC * 128], BF16)
        atst = gFA.sb(T + "atst", [128, W], BF16)
        atstc = gFA.sb(T + "atstc", [128, W], BF16)
        rsc = gFA.sb(T + "rsc", [128, W])
        lrow_f = gFA.sb(T + "lrow_f", [1, W])
        rstb2 = gFA.sb(T + "rstb2", [128, W])
        smx = gFA.sb(T + "smx", [128, 4])
        gWv = Grp()
        wvres = gWv.sb(T + "wvres", [128, 16 * 256], BF16)
        qraw = e1
        rwork = e0[0:64, :]
        rwork2 = e0[64:128, :]

        for ch in range(8):
            ftsX = [ftst, ftstc][ch % 2]
            FTB = buf(T + f"ftstB{ch % 2}")
            k.dma(2, lambda e, ch=ch, ftsX=ftsX: e.dma_start(
                    out=ftsX[:, :], in_=cpt_out[ch * 128:(ch + 1) * 128, :]),
                  reads=[buf("cpt_out"), fence_f1, FTB],
                  writes=[FTB])
            for dc in range(DC):
                col = (dc % 8) * 128
                k.op("tensor", lambda e, dc=dc, col=col, ftsX=ftsX: e.transpose(
                        PSB[:, col:col + 128], ftsX[:, dc * 128:(dc + 1) * 128],
                        identb[:, :]),
                     reads=[FTB, buf("identb")],
                     writes=[PB] if dc in (0, 7, 8, 15) else (), inc=dc in (7, 15))
                if dc in (7, 15):
                    for d2 in range(dc - 7, dc + 1):
                        k.op("vector", lambda e, d2=d2, ch=ch: e.tensor_copy(
                                out=filt[:, d2 * W + ch * 128:d2 * W + (ch + 1) * 128],
                                in_=PSB[:, (d2 % 8) * 128:(d2 % 8 + 1) * 128]),
                             reads=[PB, fence_f1], writes=[buf(T + f"filt_{d2}")])
        filt_bufs = [buf(T + f"filt_{dc}") for dc in range(DC)]

        for dc in range(DC):
            k.op("scalar", lambda e, dc=dc: e.activation(
                    out=Pb[:, :], in_=filt[:, dc * W:(dc + 1) * W], func=AF.Square),
                 reads=[filt_bufs[dc]], writes=[buf(T + "Pb")])
            for o in (0, 512):
                k.op("tensor", lambda e, o=o, dc=dc: e.matmul(
                        PS3[0:1, o:o + 512], ones_b[:, :], Pb[:, o:o + 512],
                        start=(dc == 0), stop=(dc == DC - 1)),
                     reads=[buf(T + "Pb"), cONES_B], writes=[P3])
        k.op("vector", lambda e: e.tensor_scalar_mul(lrow_f[0:1, :], PS3[0:1, 0:W], 1.0 / D),
             reads=[P3], writes=[buf(T + "ms1")])
        k.op("scalar", lambda e: e.activation(out=lrow_f[0:1, :], in_=lrow_f[0:1, :],
                                              func=AF.Sqrt, bias=cst[0:1, 1:2]),
             reads=[buf(T + "ms1"), buf("cst")], writes=[buf(T + "ms1")])
        k.op("vector", lambda e: e.reciprocal(lrow_f[0:1, :], lrow_f[0:1, :]),
             reads=[buf(T + "ms1")], writes=[buf(T + "ms1")])
        for o in (0, 512):
            k.op("tensor", lambda e, o=o: e.matmul(
                    PS12[:, o:o + 512], ones_row[:, :], lrow_f[0:1, o:o + 512],
                    start=True, stop=True),
                 reads=[cONES_R, buf(T + "ms1")], writes=[P12])
        k.op("vector", lambda e: e.tensor_copy(out=rstb2[:, :], in_=PS12[:, 0:W]),
             reads=[P12], writes=[buf(T + "rstb2")])
        for dc in range(DC):
            k.op("vector", lambda e, dc=dc: e.scalar_tensor_tensor(
                    out=hn[:, dc * W:(dc + 1) * W], in0=filt[:, dc * W:(dc + 1) * W],
                    scalar=g1_t[:, dc:dc + 1], in1=rstb2[:, :],
                    op0=ALU.mult, op1=ALU.mult),
                 reads=[filt_bufs[dc], buf("g1_t"), buf(T + "rstb2")],
                 writes=[buf(T + f"hn_{dc}")])
        hn_bufs = [buf(T + f"hn_{dc}") for dc in range(DC)]

        def qk_head(wsrc, dst, dname, pendq, nxt):
            for mh in range(4):
                wbf = [fbf0, fbf1][mh % 2]
                wbufs = pendq.pop(mh)
                if mh + 1 < 4:
                    pendq[mh + 1] = stream_weight(T + f"{dname}{(mh + 1) % 2}", wsrc,
                                                  mh + 1, DC, fraw,
                                                  [fbf0, fbf1][(mh + 1) % 2])
                elif nxt is not None:
                    pendq[0] = stream_weight(T + "k0", nxt, 0, DC, fraw, fbf0)
                ps2, pb2 = [(PS12, P12), (PS3, P3)][mh % 2]
                mm_acc(ps2, pb2, wbf, wbufs, DC,
                       lambda kc, o, cur: hn[:, kc * W + o:kc * W + o + cur], W, hn_bufs)
                k.op("scalar", lambda e, ps2=ps2: e.activation(
                        out=qraw[:, :], in_=ps2[:, 0:W], func=AF.Copy),
                     reads=[pb2, buf("e0"), buf(T + "rsc")], writes=[buf("e1")])
                k.op("vector", lambda e: e.tensor_mul(e0[0:64, :], qraw[0:64, :],
                                                      cfm[0:64, :]),
                     reads=[buf("e1"), buf("cfm"), buf("e0")], writes=[buf("e0")])
                k.op("gpsimd", lambda e: e.tensor_mul(rsc[0:64, :], qraw[64:128, :],
                                                      sfm[64:128, :]),
                     reads=[buf("e1"), buf("sfm"), buf(T + "rsc")], writes=[buf(T + "rsc")])
                k.op("vector", lambda e, mh=mh, dst=dst: e.tensor_sub(
                        dst[0:64, mh * W:(mh + 1) * W], e0[0:64, :], rsc[0:64, :]),
                     reads=[buf("e0"), buf(T + "rsc")],
                     writes=[buf(T + f"{dname}_{mh}")])
                k.op("gpsimd", lambda e: e.tensor_mul(e0[0:64, :], qraw[0:64, :],
                                                      sfm[0:64, :]),
                     reads=[buf("e1"), buf("sfm"), buf("e0")],
                     writes=[buf("e0")])
                k.op("vector", lambda e: e.tensor_mul(rsc[0:64, :], qraw[64:128, :],
                                                      cfm[64:128, :]),
                     reads=[buf("e1"), buf("cfm"), buf(T + "rsc")],
                     writes=[buf(T + "rsc")])
                k.op("vector", lambda e, mh=mh, dst=dst: e.tensor_add(
                        dst[64:128, mh * W:(mh + 1) * W], e0[0:64, :], rsc[0:64, :]),
                     reads=[buf("e0"), buf(T + "rsc")],
                     writes=[buf(T + f"{dname}_{mh}")])

        pendq = {0: stream_weight(T + "qr0", fwq, 0, DC, fraw, fbf0)}
        qk_head(fwq, qr, "qr", pendq, fwk)
        qk_head(fwk, kr, "kr", pendq, None)
        qr_bufs = [buf(T + f"qr_{mh}") for mh in range(4)]
        kr_bufs = [buf(T + f"kr_{mh}") for mh in range(4)]

        for hf in range(2):
            for j in range(4):
                ho = (j % 2) * 1024
                k.dma((2 * hf + j) % 4, lambda e, j=j, hf=hf, ho=ho: e.dma_start(
                        out=fraw[:, ho:ho + 1024], in_=fwv[hf, j]),
                      writes=fraw_bufs(fraw, ho, ho + 1024))
                k.op("scalar", lambda e, j=j, ho=ho: e.activation(
                        out=wvres[:, j * 1024:(j + 1) * 1024], in_=fraw[:, ho:ho + 1024],
                        func=AF.Copy),
                     reads=fraw_bufs(fraw, ho, ho + 1024), writes=[buf(T + "wvr")])
            for tc in range(8):
                ps2, pb2 = [(PS12, P12), (PS3, P3)][tc % 2]
                for kc in range(DC):
                    k.op("tensor", lambda e, tc=tc, kc=kc, ps2=ps2: e.matmul(
                            ps2[:, 1024:1280],
                            hn[:, kc * W + tc * 128:kc * W + (tc + 1) * 128],
                            wvres[:, kc * 256:(kc + 1) * 256],
                            start=(kc == 0), stop=(kc == DC - 1)),
                         reads=hn_bufs + [buf(T + "wvr")],
                         writes=[pb2] if kc in (0, DC - 1) else (), inc=kc == DC - 1)
                k.op("vector", lambda e, tc=tc, hf=hf, ps2=ps2: e.tensor_copy(
                        out=vtok[:, tc * 512 + hf * 256:tc * 512 + (hf + 1) * 256],
                        in_=ps2[:, 1024:1280]),
                     reads=[pb2], writes=[buf(T + f"vt_{tc}")])
        vt_bufs = [buf(T + f"vt_{tc}") for tc in range(8)]
        fence_wv = fence([buf(T + "wvr")])
        gWv.free()

        fmsks = [e0, msk2]
        fheads = [(qt, h) for qt in range(8) for h in range(4)]
        NFH = len(fheads)

        def f_qk(i):
            qt, h = fheads[i]
            if h == 0:
                mk = fmsks[qt % 2]
                mkb = [buf("e0"), buf(T + "msk2")][qt % 2]
                k.dma(2, lambda e, qt=qt, mk=mk: e.dma_start(
                        out=mk[:, :], in_=fmask[qt * 128:(qt + 1) * 128, :]),
                      reads=[mkb], writes=[mkb])
            ps2, pb2 = [(PS12, P12), (PS3, P3)][i % 2]
            for o in (0, 512):
                k.op("tensor", lambda e, qt=qt, h=h, o=o, ps2=ps2: e.matmul(
                        ps2[:, o:o + 512],
                        qr[:, h * W + qt * 128:h * W + (qt + 1) * 128],
                        kr[:, h * W + o:h * W + o + 512],
                        start=True, stop=True),
                     reads=[qr_bufs[h], kr_bufs[h]],
                     writes=[pb2], inc=o == 512)

        def f_chain(i):
            qt, h = fheads[i]
            ps2, pb2 = [(PS12, P12), (PS3, P3)][i % 2]
            st = [e1, rsc][i % 2]
            stb = [buf("e1"), buf(T + "rsc")][i % 2]
            mk = fmsks[qt % 2]
            mkb = [buf("e0"), buf(T + "msk2")][qt % 2]
            Pbx = [Pb, Pbd][i % 2]
            BPB = buf(T + "Pb" + "d" * (i % 2))
            mxc = 2 * (i % 2)
            k.op("vector", lambda e, ps2=ps2, st=st, mk=mk: e.scalar_tensor_tensor(
                    out=st[:, :], in0=ps2[:, 0:W], scalar=0.08838834764831845,
                    in1=mk[:, :], op0=ALU.mult, op1=ALU.add),
                 reads=[pb2, mkb], writes=[stb])
            k.op("vector", lambda e, st=st, mxc=mxc: e.reduce_max(
                    out=smx[:, mxc:mxc + 1], in_=st[:, :], axis=AX.X, negate=True),
                 reads=[stb], writes=[buf(T + f"nmx{i % 2}")])
            k.op("scalar", lambda e, st=st, Pbx=Pbx, mxc=mxc: e.activation(
                    out=Pbx[:, :], in_=st[:, :], func=AF.Exp, bias=smx[:, mxc:mxc + 1],
                    accum_out=smx[:, mxc + 1:mxc + 2]),
                 reads=[stb, buf(T + f"nmx{i % 2}"), BPB], writes=[BPB])
            k.op("vector", lambda e, mxc=mxc: e.reciprocal(
                    smx[:, mxc + 1:mxc + 2], smx[:, mxc + 1:mxc + 2]),
                 reads=[BPB], writes=[buf(T + f"rsum{i % 2}")])
            k.op("scalar", lambda e, Pbx=Pbx, mxc=mxc: e.activation(
                    out=Pbx[:, :], in_=Pbx[:, :], func=AF.Copy,
                    scale=smx[:, mxc + 1:mxc + 2]),
                 reads=[BPB, buf(T + f"rsum{i % 2}")], writes=[BPB])

        def f_tail(i):
            qt, h = fheads[i]
            ps2, pb2 = [(PS12, P12), (PS3, P3)][i % 2]
            Pbx = [Pb, Pbd][i % 2]
            PTx = [PT, PTd][i % 2]
            BPB = buf(T + "Pb" + "d" * (i % 2))
            BPT = buf(T + "PT" + "d" * (i % 2))
            for c in range(8):
                k.op("tensor", lambda e, c=c, Pbx=Pbx: e.transpose(
                        PSB[:, c * 128:(c + 1) * 128], Pbx[:, c * 128:(c + 1) * 128],
                        identb[:, :]),
                     reads=[BPB, buf("identb")],
                     writes=[PB] if c in (0, 7) else (), inc=c == 7)
            k.op("vector", lambda e, PTx=PTx: e.tensor_copy(out=PTx[:, :], in_=PSB[:, 0:W]),
                 reads=[PB], writes=[BPT])
            for c in range(8):
                k.op("tensor", lambda e, c=c, h=h, ps2=ps2, PTx=PTx: e.matmul(
                        ps2[:, 1024:1152],
                        vtok[:, c * 512 + h * 128:c * 512 + (h + 1) * 128],
                        PTx[:, c * 128:(c + 1) * 128],
                        start=(c == 0), stop=(c == 7)),
                     reads=[vt_bufs[c], BPT],
                     writes=[pb2] if c in (0, 7) else (), inc=c == 7)
            k.op("vector", lambda e, h=h, qt=qt, ps2=ps2: e.tensor_copy(
                    out=obuf[:, h * W + qt * 128:h * W + (qt + 1) * 128],
                    in_=ps2[:, 1024:1152]),
                 reads=[pb2], writes=[buf(T + f"ob_{h}")])

        f_qk(0)
        for i in range(NFH):
            if i + 1 < NFH:
                f_qk(i + 1)
            f_chain(i)
            if i >= 1:
                f_tail(i - 1)
        f_tail(NFH - 1)
        ob_bufs = [buf(T + f"ob_{h}") for h in range(4)]

        for mc in range(DC):
            k.dma(mc % 4, lambda e, mc=mc: e.dma_start(out=fraw[:, 0:512], in_=fwo[mc]),
                  reads=[buf(T + "wor")], writes=fraw_bufs(fraw, 0, 512))
            k.op("scalar", lambda e: e.activation(out=wobf[:, :], in_=fraw[:, 0:512],
                                                  func=AF.Copy),
                 reads=fraw_bufs(fraw, 0, 512), writes=[buf(T + "wor")])
            ps2, pb2 = [(PS12, P12), (PS3, P3)][mc % 2]
            for kc in range(4):
                for o in (0, 512):
                    k.op("tensor", lambda e, kc=kc, o=o, ps2=ps2: e.matmul(
                            ps2[:, o:o + 512], wobf[:, kc * 128:(kc + 1) * 128],
                            obuf[:, kc * W + o:kc * W + o + 512],
                            start=(kc == 0), stop=(kc == 3)),
                         reads=[buf(T + "wor")] + ob_bufs,
                         writes=[pb2] if (kc == 0 and o == 0) or (kc == 3 and o == 512)
                         else (), inc=(kc == 3 and o == 512))
            atsX = [atst, atstc][mc % 2]
            ATB = buf(T + f"atst{mc % 2}")
            k.op("vector", lambda e, ps2=ps2, atsX=atsX: e.tensor_copy(
                    out=atsX[:, :], in_=ps2[:, 0:W]),
                 reads=[pb2, ATB], writes=[ATB])
            k.dma(6, lambda e, mc=mc, atsX=atsX: e.dma_start(
                    out=at_in[mc * 128:(mc + 1) * 128, :], in_=atsX[:, :]),
                  reads=[ATB], writes=[buf("at_in")])
        k.cc(lambda e: e.collective_compute(
                "AllReduce", ALU.add, replica_groups=GROUPS4,
                ins=[at_in], outs=[at_out]),
             reads=[buf("at_in")], writes=[buf("at_out")])
        for dc in range(DC):
            atsX = [atst, atstc][dc % 2]
            ATB = buf(T + f"atst{dc % 2}")
            k.dma(2, lambda e, dc=dc, atsX=atsX: e.dma_start(
                    out=atsX[:, :], in_=at_out[dc * 128:(dc + 1) * 128, :]),
                  reads=[buf("at_out"), ATB], writes=[ATB])
            k.op("vector", lambda e, dc=dc, atsX=atsX: e.tensor_add(
                    filt[:, dc * W:(dc + 1) * W], filt[:, dc * W:(dc + 1) * W],
                    atsX[:, :]),
                 reads=[filt_bufs[dc], ATB], writes=[filt_bufs[dc]])

        fence_fa = fence(hn_bufs + qr_bufs + kr_bufs + vt_bufs + ob_bufs +
                         [buf("e0"), buf("e1"), fence_wv] +
                         [buf(T + n) for n in ("PT", "Pb", "PTd", "Pbd", "msk2",
                                               "rsc", "wor", "ftstB0", "ftstB1",
                                               "atst0", "atst1", "nmx0", "nmx1",
                                               "rsum0", "rsum1", "ms1", "rstb2")])
        gFA.free()

        gFB = Grp()
        h2 = gFB.sb(T + "h2", [128, DC * W], BF16)
        gbuf = gFB.sb(T + "g", [128, 16 * W], BF16)
        w2h = gFB.sb(T + "w2h", [128, 16 * 512], BF16)
        fraw2 = gFB.sb(T + "fraw2", [128, DC * 128])
        fbf20 = gFB.sb(T + "fbf20", [128, DC * 128], BF16)
        fbf21 = gFB.sb(T + "fbf21", [128, DC * 128], BF16)
        fbf22 = gFB.sb(T + "fbf22", [128, DC * 128], BF16)
        sq2 = gFB.sb(T + "sq2", [128, W], BF16)
        rstb3 = gFB.sb(T + "rstb3", [128, W])
        lrowb = gFB.sb(T + "lrowb", [1, W])
        tst2 = gFB.sb(T + "tst2", [128, 2048], BF16)
        tst2c = gFB.sb(T + "tst2c", [128, 2048], BF16)

        for dc in range(DC):
            k.op("scalar", lambda e, dc=dc: e.activation(
                    out=sq2[:, :], in_=filt[:, dc * W:(dc + 1) * W], func=AF.Square),
                 reads=[filt_bufs[dc], fence_fa], writes=[buf(T + "sq2")])
            for o in (0, 512):
                k.op("tensor", lambda e, o=o, dc=dc: e.matmul(
                        PS3[0:1, o:o + 512], ones_b[:, :], sq2[:, o:o + 512],
                        start=(dc == 0), stop=(dc == DC - 1)),
                     reads=[buf(T + "sq2"), cONES_B], writes=[P3])
        k.op("vector", lambda e: e.tensor_scalar_mul(lrowb[:, :], PS3[0:1, 0:W], 1.0 / D),
             reads=[P3], writes=[buf(T + "ms2")])
        k.op("scalar", lambda e: e.activation(out=lrowb[:, :], in_=lrowb[:, :],
                                              func=AF.Sqrt, bias=cst[0:1, 1:2]),
             reads=[buf(T + "ms2"), buf("cst")], writes=[buf(T + "ms2")])
        k.op("vector", lambda e: e.reciprocal(lrowb[:, :], lrowb[:, :]),
             reads=[buf(T + "ms2")], writes=[buf(T + "ms2")])
        for o in (0, 512):
            k.op("tensor", lambda e, o=o: e.matmul(
                    PS12[:, o:o + 512], ones_row[:, :], lrowb[0:1, o:o + 512],
                    start=True, stop=True),
                 reads=[cONES_R, buf(T + "ms2")], writes=[P12])
        k.op("vector", lambda e: e.tensor_copy(out=rstb3[:, :], in_=PS12[:, 0:W]),
             reads=[P12], writes=[buf(T + "rstb3")])
        for dc in range(DC):
            k.op("vector", lambda e, dc=dc: e.scalar_tensor_tensor(
                    out=h2[:, dc * W:(dc + 1) * W], in0=filt[:, dc * W:(dc + 1) * W],
                    scalar=g2_t[:, dc:dc + 1], in1=rstb3[:, :],
                    op0=ALU.mult, op1=ALU.mult),
                 reads=[filt_bufs[dc], buf("g2_t"), buf(T + "rstb3"), fence_fa],
                 writes=[buf(T + f"h2_{dc}")])
        h2_bufs = [buf(T + f"h2_{dc}") for dc in range(DC)]

        fbrot = [fbf20, fbf21, fbf22]
        pend1 = {0: stream_weight(T + "w10", fw1, 0, DC, fraw2, fbrot[0], "scalar",
                                  extra=[fence_fa])}
        pend3 = {0: stream_weight(T + "w30", fw3, 0, DC, fraw2, fbrot[1], "scalar",
                                  extra=[fence_fa], soff=1)}
        for m in range(16):
            wb1 = fbrot[(2 * m) % 3]
            wb3 = fbrot[(2 * m + 1) % 3]
            wbufs1 = pend1.pop(m)
            wbufs3 = pend3.pop(m)
            mm_acc(PS12, P12, wb1, wbufs1, DC,
                   lambda kc, o, cur: h2[:, kc * W + o:kc * W + o + cur], W, h2_bufs)
            if m + 1 < 16:
                pend1[m + 1] = stream_weight(T + f"w1{(m + 1) % 2}", fw1, m + 1, DC,
                                             fraw2, fbrot[(2 * m + 2) % 3], "scalar")
                pend3[m + 1] = stream_weight(T + f"w3{(m + 1) % 2}", fw3, m + 1, DC,
                                             fraw2, fbrot[(2 * m + 3) % 3], "scalar",
                                             soff=1)
            mm_acc(PS3, P3, wb3, wbufs3, DC,
                   lambda kc, o, cur: h2[:, kc * W + o:kc * W + o + cur], W, h2_bufs)
            k.op("scalar", lambda e: e.activation(out=e0[:, :], in_=PS12[:, 0:W],
                                                  func=AF.Silu),
                 reads=[P12], writes=[buf("e0")])
            k.op("vector", lambda e, m=m: e.tensor_mul(
                    gbuf[:, m * W:(m + 1) * W], e0[:, :], PS3[:, 0:W]),
                 reads=[buf("e0"), P3], writes=[buf(T + f"g_{m}")])
        g_bufs = [buf(T + f"g_{m}") for m in range(16)]

        for quar in range(4):
            for j in range(8):
                ho = (j % 2) * 1024
                k.dma(j % 4, lambda e, j=j, quar=quar, ho=ho: e.dma_start(
                        out=fraw2[:, ho:ho + 1024], in_=fw2[quar, j]),
                      writes=fraw_bufs(fraw2, ho, ho + 1024))
                k.op("scalar", lambda e, j=j, ho=ho: e.activation(
                        out=w2h[:, j * 1024:(j + 1) * 1024], in_=fraw2[:, ho:ho + 1024],
                        func=AF.Copy),
                     reads=fraw_bufs(fraw2, ho, ho + 1024), writes=[buf(T + "w2c")])
            for tc in range(8):
                ps2, pb2 = [(PS12, P12), (PS3, P3)][tc % 2]
                for kc in range(16):
                    k.op("tensor", lambda e, tc=tc, kc=kc, ps2=ps2: e.matmul(
                            ps2[:, 0:512],
                            gbuf[:, kc * W + tc * 128:kc * W + (tc + 1) * 128],
                            w2h[:, kc * 512:(kc + 1) * 512],
                            start=(kc == 0), stop=(kc == 15)),
                         reads=g_bufs + [buf(T + "w2c")],
                         writes=[pb2] if kc in (0, 15) else (),
                         inc=kc == 15)
                k.op("vector", lambda e, ps2=ps2: e.tensor_copy(
                        out=tst2[:, 0:512], in_=ps2[:, 0:512]),
                     reads=[pb2, buf(T + "tst2")], writes=[buf(T + "tst2")])
                k.dma(6, lambda e, tc=tc, quar=quar: e.dma_start(
                        out=fo_in[tc * 128:(tc + 1) * 128, quar * 512:(quar + 1) * 512],
                        in_=tst2[:, 0:512]),
                      reads=[buf(T + "tst2")], writes=[buf("fo_in")])
        k.cc(lambda e: e.collective_compute(
                "AllReduce", ALU.add, replica_groups=GROUPS4,
                ins=[fo_in], outs=[fo_out]),
             reads=[buf("fo_in")], writes=[buf("fo_out")])

        for ch in range(8):
            tstX = [tst2, tst2c][ch % 2]
            TSB = buf(T + f"tst2{ch % 2}")
            k.dma(2, lambda e, ch=ch, tstX=tstX: e.dma_start(
                    out=tstX[:, :], in_=fo_out[ch * 128:(ch + 1) * 128, :]),
                  reads=[buf("fo_out"), TSB], writes=[TSB])
            for dc in range(DC):
                col = (dc % 8) * 128
                k.op("tensor", lambda e, dc=dc, ch=ch, col=col: e.transpose(
                        PSB[:, col:col + 128],
                        filt[:, dc * W + ch * 128:dc * W + (ch + 1) * 128], identb[:, :]),
                     reads=filt_bufs + [buf("identb")],
                     writes=[PB] if dc in (0, 7, 8, 15) else (), inc=dc in (7, 15))
                if dc in (7, 15):
                    for d2 in range(dc - 7, dc + 1):
                        k.op("vector", lambda e, d2=d2, tstX=tstX: e.tensor_add(
                                tstX[:, d2 * 128:(d2 + 1) * 128],
                                tstX[:, d2 * 128:(d2 + 1) * 128],
                                PSB[:, (d2 % 8) * 128:(d2 % 8 + 1) * 128]),
                             reads=[PB, TSB], writes=[TSB])
            k.dma(6, lambda e, ch=ch, tstX=tstX: e.dma_start(
                    out=xo_dram[ch * 128:(ch + 1) * 128, :], in_=tstX[:, :]),
                  reads=[TSB], writes=[buf("xo_dram")])

        fence_fb = fence(h2_bufs + g_bufs + filt_bufs +
                         [buf(T + n) for n in ("tst20", "tst21", "w2c", "sq2",
                                               "rstb3", "ms2")])
        gFB.free()
        gFil.free()

        gXn = Grp("right")
        xn = gXn.sb(T + "xn", [128, DC * W])
        gFe = Grp()
        gst = gFe.sb(T + "gst", [128, 2048], BF16)
        gsb2 = gFe.sb(T + "gsb2", [128, 2048], BF16)
        gstc = gFe.sb(T + "gstc", [128, 2048], BF16)
        gsb2c = gFe.sb(T + "gsb2c", [128, 2048], BF16)
        rwt = gFe.sb(T + "rwt", [128, 8])
        idxt2 = gFe.sb(T + "idxt2", [128, 8], I32)

        for dc in range(DC):
            k.dma(dc % 4 + 8 * (dc // 4 % 2), lambda e, dc=dc: e.dma_start(
                    out=xn[:, dc * W:(dc + 1) * W], in_=xst_dram[:, dc * W:(dc + 1) * W]),
                  reads=[buf("xst"), fence_fb], writes=[buf(T + f"xn_{dc}")])
        xn_bufs = [buf(T + f"xn_{dc}") for dc in range(DC)]
        for ch in range(8):
            k.dma(6, lambda e, ch=ch: e.dma_start(
                    out=idxt2[:, ch:ch + 1],
                    in_=idx_dram[0:1, ch * 128:(ch + 1) * 128].rearrange(
                        "a (p c) -> (a p) c", p=128)),
                  reads=[buf("idx_dram"), fence_fb], writes=[buf(T + "idxt2")])
            k.dma(6, lambda e, ch=ch: e.dma_start(
                    out=rwt[:, ch:ch + 1],
                    in_=rw_dram[0:1, ch * 128:(ch + 1) * 128].rearrange(
                        "a (p c) -> (a p) c", p=128)),
                  reads=[buf("rw_dram"), fence_fb], writes=[buf(T + "rwt")])
        for ch in range(8):
            gstX = [gst, gstc][ch % 2]
            gsbX = [gsb2, gsb2c][ch % 2]
            GSTB = buf(T + f"gstB{ch % 2}")
            GSBB = buf(T + f"gsb2{ch % 2}")
            k.op("vector", lambda e, gstX=gstX: e.memset(gstX[:, :], 0.0),
                 reads=[fence_fb, GSTB], writes=[GSTB])
            k.dma(5, lambda e, ch=ch, gstX=gstX: e.indirect_dma_start(
                    out=gstX[:, :], out_offset=None,
                    in_=xo_dram, in_offset=IndirectOffsetOnAxis(idxt2[:, ch:ch + 1], 0),
                    bounds_check=1023, oob_is_err=False),
                  reads=[buf("xo_dram"), buf(T + "idxt2"), GSTB],
                  writes=[GSTB])
            k.op("vector", lambda e, ch=ch, gstX=gstX, gsbX=gsbX: e.tensor_scalar(
                    out=gsbX[:, :], in0=gstX[:, :], scalar1=rwt[:, ch:ch + 1],
                    scalar2=None, op0=ALU.mult),
                 reads=[GSTB, buf(T + "rwt"), GSBB], writes=[GSBB])
            for dc in range(DC):
                col = (dc % 8) * 128
                k.op("tensor", lambda e, dc=dc, col=col, gsbX=gsbX: e.transpose(
                        PSB[:, col:col + 128], gsbX[:, dc * 128:(dc + 1) * 128],
                        identb[:, :]),
                     reads=[GSBB, buf("identb")],
                     writes=[PB] if dc in (0, 7, 8, 15) else (), inc=dc in (7, 15))
                if dc in (7, 15):
                    for d2 in range(dc - 7, dc + 1):
                        k.op("vector", lambda e, d2=d2, ch=ch: e.tensor_add(
                                xn[:, d2 * W + ch * 128:d2 * W + (ch + 1) * 128],
                                xn[:, d2 * W + ch * 128:d2 * W + (ch + 1) * 128],
                                PSB[:, (d2 % 8) * 128:(d2 % 8 + 1) * 128]),
                             reads=[PB, buf(T + f"xn_{d2}")], writes=[buf(T + f"xn_{d2}")])
        fence_fe = fence([buf(T + n) for n in ("gstB0", "gstB1", "gsb20", "gsb21",
                                               "rwt", "idxt2")])
        gFe.free()
        return xn, xn_bufs, fence_fe, gXn


    def emit_attn_b(tag, xt, x_bufs, fence_in, free_x):
        T = tag
        ISQ = 0.08838834764831845
        for dc in range(DC):
            k.dma(dc % 2, lambda e, dc=dc: e.dma_start(
                    out=xst_dram[:, dc * W:(dc + 1) * W], in_=xt[:, dc * W:(dc + 1) * W]),
                  reads=[x_bufs[dc], fence_in], writes=[buf("xst")])
        fence_x = fence(list(x_bufs))
        free_x()

        gXp = Grp("right")
        xpb2 = gXp.sb(T + "xpb2", [128, DC * W], BF16)
        gLN = Grp()
        xs0 = gLN.sb(T + "xs0", [128, W])
        xs1 = gLN.sb(T + "xs1", [128, W])
        sqb = gLN.sb(T + "sqb", [128, W], BF16)
        statb = gLN.sb(T + "statb", [1, 3 * W])

        s0 = statb[0:1, 0:W]; s1 = statb[0:1, W:2 * W]; s2 = statb[0:1, 2 * W:3 * W]
        for dc in range(DC):
            xs = [xs0, xs1][dc % 2]
            k.dma((dc % 4) * 2, lambda e, dc=dc, xs=xs: e.dma_start(
                    out=xs[:, :], in_=xst_dram[:, dc * W:(dc + 1) * W]),
                  reads=[buf("xst"), fence_x], writes=[buf(T + f"xs_{dc % 2}")])
            k.op("vector", lambda e, dc=dc, xs=xs: e.tensor_copy(
                    out=xpb2[:, dc * W:(dc + 1) * W], in_=xs[:, :]),
                 reads=[buf(T + f"xs_{dc % 2}"), fence_x], writes=[buf(T + f"xb_{dc}")])
            k.op("scalar", lambda e, dc=dc: e.activation(
                    out=sqb[:, :], in_=xpb2[:, dc * W:(dc + 1) * W], func=AF.Square),
                 reads=[buf(T + f"xb_{dc}")], writes=[buf(T + "sq")])
            for o in (0, 512):
                k.op("tensor", lambda e, dc=dc, o=o: e.matmul(
                        PS12[0:1, o:o + 512], ones_b[:, :],
                        xpb2[:, dc * W + o:dc * W + o + 512],
                        start=(dc == 0), stop=(dc == DC - 1)),
                     reads=[buf(T + f"xb_{dc}"), cONES_B], writes=[P12])
                k.op("tensor", lambda e, o=o, dc=dc: e.matmul(
                        PS3[0:1, o:o + 512], ones_b[:, :], sqb[:, o:o + 512],
                        start=(dc == 0), stop=(dc == DC - 1)),
                     reads=[buf(T + "sq"), cONES_B], writes=[P3])
        k.op("vector", lambda e: e.tensor_scalar_mul(s0, PS12[0:1, 0:W], 1.0 / D),
             reads=[P12], writes=[buf(T + "st0")])
        k.op("vector", lambda e: e.tensor_scalar_mul(s1, PS3[0:1, 0:W], 1.0 / D),
             reads=[P3], writes=[buf(T + "st1")])
        k.op("vector", lambda e: e.tensor_mul(s2, s0, s0),
             reads=[buf(T + "st0")], writes=[buf(T + "st2")])
        k.op("vector", lambda e: e.tensor_sub(s1, s1, s2),
             reads=[buf(T + "st1"), buf(T + "st2")], writes=[buf(T + "st1")])
        k.op("scalar", lambda e: e.activation(out=s1, in_=s1, func=AF.Sqrt,
                                              bias=cst[0:1, 0:1]),
             reads=[buf(T + "st1"), buf("cst")], writes=[buf(T + "st1")])
        k.op("vector", lambda e: e.reciprocal(s1, s1),
             reads=[buf(T + "st1")], writes=[buf(T + "st1")])
        k.op("vector", lambda e: e.tensor_mul(s0, s0, s1),
             reads=[buf(T + "st0"), buf(T + "st1")], writes=[buf(T + "st0")])
        k.op("vector", lambda e: e.tensor_scalar_mul(s0, s0, -1.0),
             reads=[buf(T + "st0")], writes=[buf(T + "st0")])
        for o in (0, 512):
            k.op("tensor", lambda e, o=o: e.matmul(
                    PS12[:, o:o + 512], ones_row[:, :], s1[0:1, o:o + 512],
                    start=True, stop=True),
                 reads=[cONES_R, buf(T + "st1")], writes=[P12])
            k.op("tensor", lambda e, o=o: e.matmul(
                    PS3[:, o:o + 512], ones_row[:, :], s0[0:1, o:o + 512],
                    start=True, stop=True),
                 reads=[cONES_R, buf(T + "st0")], writes=[P3])
        k.op("vector", lambda e: e.tensor_copy(out=e0[:, :], in_=PS12[:, 0:W]),
             reads=[P12, buf("e0")], writes=[buf("e0")])
        k.op("vector", lambda e: e.tensor_copy(out=e1[:, :], in_=PS3[:, 0:W]),
             reads=[P3, buf("e1")], writes=[buf("e1")])
        for dc in range(DC):
            xs = [xs0, xs1][dc % 2]
            k.dma((dc % 4) * 2, lambda e, dc=dc, xs=xs: e.dma_start(
                    out=xs[:, :], in_=xst_dram[:, dc * W:(dc + 1) * W]),
                  reads=[buf("xst"), buf(T + f"xs_{dc % 2}")],
                  writes=[buf(T + f"xs_{dc % 2}")])
            eng = "vector" if dc % 2 == 0 else "gpsimd"
            k.op(eng, lambda e, dc=dc, xs=xs: e.tensor_mul(
                    xpb2[:, dc * W:(dc + 1) * W], xs[:, :], e0[:, :]),
                 reads=[buf(T + f"xs_{dc % 2}"), buf("e0"), buf(T + f"xb_{dc}")],
                 writes=[buf(T + f"xb_{dc}")])
            k.op(eng, lambda e, dc=dc: e.tensor_add(
                    xpb2[:, dc * W:(dc + 1) * W], xpb2[:, dc * W:(dc + 1) * W], e1[:, :]),
                 reads=[buf(T + f"xb_{dc}"), buf("e1")], writes=[buf(T + f"xb_{dc}")])
        xb_bufs = [buf(T + f"xb_{dc}") for dc in range(DC)]
        fence_ln = fence([buf(T + "xs_0"), buf(T + "xs_1"), buf(T + "sq"),
                          buf(T + "st0"), buf(T + "st1"), buf(T + "st2"),
                          buf("e0"), buf("e1")])
        gLN.free()

        gBo = Grp()
        o2 = gBo.sb(T + "o2", [128, DC * W], BF16)
        braw = gBo.sb(T + "braw", [128, DC * 128])
        bbf0 = gBo.sb(T + "bbf0", [128, DC * 128], BF16)
        bbf1 = gBo.sb(T + "bbf1", [128, DC * 128], BF16)
        tstb = gBo.sb(T + "tstb", [128, W], BF16)
        gBq = Grp()
        qfm = gBq.sb(T + "qfm", [128, DC * W], BF16)
        gB1 = Grp()
        kfm = gB1.sb(T + "kfm", [128, 4 * W], BF16)
        wvres2 = gB1.sb(T + "wvres2", [128, 16 * 512], BF16)

        pendb = {0: stream_weight(T + "k0", bwk, 0, DC, braw, bbf0,
                                  extra=[fence_ln])}
        for mc in range(4):
            wbf = [bbf0, bbf1][mc % 2]
            wbufs = pendb.pop(mc)
            if mc + 1 < 4:
                pendb[mc + 1] = stream_weight(T + f"k{(mc + 1) % 2}", bwk, mc + 1, DC,
                                              braw, [bbf0, bbf1][(mc + 1) % 2])
            else:
                pendb[0] = stream_weight(T + "q0", bwq, 0, DC, braw, bbf0)
            ps2, pb2 = [(PS12, P12), (PS3, P3)][mc % 2]
            mm_acc(ps2, pb2, wbf, wbufs, DC,
                   lambda kc, o, cur: xpb2[:, kc * W + o:kc * W + o + cur], W, xb_bufs)
            k.op("vector", lambda e, mc=mc, ps2=ps2: e.tensor_copy(
                    out=kfm[:, mc * W:(mc + 1) * W], in_=ps2[:, 0:W]),
                 reads=[pb2], writes=[buf(T + "kf")])
        for tc in range(8):
            for c in range(4):
                k.op("tensor", lambda e, tc=tc, c=c: e.transpose(
                        PSB[:, c * 128:(c + 1) * 128],
                        kfm[:, c * W + tc * 128:c * W + (tc + 1) * 128], identb[:, :]),
                     reads=[buf(T + "kf"), buf("identb")],
                     writes=[PB] if c in (0, 3) else (), inc=c == 3)
            k.op("vector", lambda e, tc=tc: e.tensor_copy(out=tstb[:, 0:512],
                                                          in_=PSB[:, 0:512]),
                 reads=[PB, buf(T + "tstbB")], writes=[buf(T + "tstbB")])
            k.dma(6, lambda e, tc=tc: e.dma_start(
                    out=kvag_in[tc * 128:(tc + 1) * 128, 0:512], in_=tstb[:, 0:512]),
                  reads=[buf(T + "tstbB")], writes=[buf("kvag_in")])
        for j in range(8):
            ho = (j % 2) * 1024
            k.dma(j % 4, lambda e, j=j, ho=ho: e.dma_start(
                    out=braw[:, ho:ho + 1024], in_=bwv[j]),
                  writes=fraw_bufs(braw, ho, ho + 1024))
            k.op("scalar", lambda e, j=j, ho=ho: e.activation(
                    out=wvres2[:, j * 1024:(j + 1) * 1024], in_=braw[:, ho:ho + 1024],
                    func=AF.Copy),
                 reads=fraw_bufs(braw, ho, ho + 1024), writes=[buf(T + "wv2")])
        for tc in range(8):
            ps2, pb2 = [(PS12, P12), (PS3, P3)][tc % 2]
            for kc in range(DC):
                k.op("tensor", lambda e, tc=tc, kc=kc, ps2=ps2: e.matmul(
                        ps2[:, 1024:1536],
                        xpb2[:, kc * W + tc * 128:kc * W + (tc + 1) * 128],
                        wvres2[:, kc * 512:(kc + 1) * 512],
                        start=(kc == 0), stop=(kc == DC - 1)),
                     reads=xb_bufs + [buf(T + "wv2")],
                     writes=[pb2] if kc in (0, DC - 1) else (), inc=kc == DC - 1)
            k.op("vector", lambda e, tc=tc, ps2=ps2: e.tensor_copy(
                    out=tstb[:, 0:512], in_=ps2[:, 1024:1536]),
                 reads=[pb2, buf(T + "tstbB")], writes=[buf(T + "tstbB")])
            k.dma(6, lambda e, tc=tc: e.dma_start(
                    out=kvag_in[tc * 128:(tc + 1) * 128, 512:1024], in_=tstb[:, 0:512]),
                  reads=[buf(T + "tstbB")], writes=[buf("kvag_in")])
        k.cc(lambda e: e.collective_compute(
                "AllGather", ALU.bypass, replica_groups=GROUPS4,
                ins=[kvag_in], outs=[kvag_out]),
             reads=[buf("kvag_in")], writes=[buf("kvag_out")])
        for mc in range(DC):
            wbf = [bbf0, bbf1][mc % 2]
            wbufs = pendb.pop(mc)
            if mc + 1 < DC:
                pendb[mc + 1] = stream_weight(T + f"q{(mc + 1) % 2}", bwq, mc + 1, DC,
                                              braw, [bbf0, bbf1][(mc + 1) % 2])
            ps2, pb2 = [(PS12, P12), (PS3, P3)][mc % 2]
            mm_acc(ps2, pb2, wbf, wbufs, DC,
                   lambda kc, o, cur: xpb2[:, kc * W + o:kc * W + o + cur], W, xb_bufs)
            k.op("vector", lambda e, mc=mc, ps2=ps2: e.tensor_copy(
                    out=qfm[:, mc * W:(mc + 1) * W], in_=ps2[:, 0:W]),
                 reads=[pb2], writes=[buf(T + f"q_{mc}")])
        q_bufs = [buf(T + f"q_{mc}") for mc in range(DC)]
        fence_b1 = fence(xb_bufs + [buf(T + n) for n in ("kf", "wv2", "tstbB")])
        gB1.free()
        gXp.free()

        gBA = Grp()
        kfmw = gBA.sb(T + "kfmw", [128, 4 * 2048], BF16)
        vwin = gBA.sb(T + "vwin", [128, 16 * 512], BF16)
        kvix = gBA.sb(T + "kvix", [128, 16], I32)
        gtst = gBA.sb(T + "gtst", [128, 1024], BF16)
        gtstc = gBA.sb(T + "gtstc", [128, 1024], BF16)
        PT2a = gBA.sb(T + "PT2a", [128, 1152], BF16)
        PT2b = gBA.sb(T + "PT2b", [128, 1152], BF16)
        Pb2a = gBA.sb(T + "Pb2a", [128, 1152], BF16)
        Pb2b = gBA.sb(T + "Pb2b", [128, 1152], BF16)
        bmq = gBA.sb(T + "bmq", [128, 1152], BF16)
        bmq2 = gBA.sb(T + "bmq2", [128, 1152], BF16)
        smx2 = gBA.sb(T + "smx2", [128, 4])

        k.dma(6, lambda e: e.dma_start(out=kvix[:, :], in_=kvidx_i),
              reads=[fence_b1], writes=[buf(T + "kvix")])
        for wi, wc in enumerate(list(range(8, 16)) + list(range(8))):
            gtsX = [gtst, gtstc][wi % 2]
            GTB = buf(T + f"gtstB{wi % 2}")
            if wc < 8:
                k.op("vector", lambda e, gtsX=gtsX: e.memset(gtsX[:, :], 0.0),
                     reads=[fence_b1, GTB], writes=[GTB])
                k.dma(5, lambda e, wc=wc, gtsX=gtsX: e.indirect_dma_start(
                        out=gtsX[:, :], out_offset=None,
                        in_=kvag_out, in_offset=IndirectOffsetOnAxis(kvix[:, wc:wc + 1], 0),
                        bounds_check=4095, oob_is_err=False),
                      reads=[buf("kvag_out"), buf(T + "kvix"), GTB],
                      writes=[GTB])
            else:
                k.dma(5, lambda e, wc=wc, gtsX=gtsX: e.dma_start(
                        out=gtsX[:, :],
                        in_=kvag_in[(wc - 8) * 128:(wc - 7) * 128, :]),
                      reads=[buf("kvag_in"), fence_b1, GTB],
                      writes=[GTB])
            k.op("vector", lambda e, wc=wc, gtsX=gtsX: e.tensor_copy(
                    out=vwin[:, wc * 512:(wc + 1) * 512], in_=gtsX[:, 512:1024]),
                 reads=[GTB], writes=[buf(T + "vwin")])
            for c in range(4):
                k.op("tensor", lambda e, c=c, gtsX=gtsX: e.transpose(
                        PSB[:, c * 128:(c + 1) * 128], gtsX[:, c * 128:(c + 1) * 128],
                        identb[:, :]),
                     reads=[GTB, buf("identb")],
                     writes=[PB] if c in (0, 3) else (), inc=c == 3)
            for c in range(4):
                k.op("vector", lambda e, c=c, wc=wc: e.tensor_copy(
                        out=kfmw[:, c * 2048 + wc * 128:c * 2048 + (wc + 1) * 128],
                        in_=PSB[:, c * 128:(c + 1) * 128]),
                     reads=[PB], writes=[buf(T + "kfmw")])
        bmqs = [bmq, bmq2]
        heads = [(qt, h) for qt in range(8) for h in range(16)]
        NH = len(heads)

        def b_qk(i):
            qt, h = heads[i]
            if h == 0:
                mb = bmqs[qt % 2]
                k.dma(2, lambda e, qt=qt, mb=mb: e.dma_start(
                        out=mb[:, :], in_=bmask_i[qt * 128:(qt + 1) * 128, :]),
                      reads=[buf(T + f"bmqB{qt % 2}")],
                      writes=[buf(T + f"bmqB{qt % 2}")])
            kvh = h // 4
            ps2, pb2 = [(PS12, P12), (PS3, P3)][i % 2]
            for (o, cur) in ((0, 512), (512, 512), (1024, 128)):
                k.op("tensor", lambda e, qt=qt, h=h, o=o, cur=cur, ps2=ps2,
                     kvh=kvh: e.matmul(
                        ps2[:, o:o + cur],
                        qfm[:, h * W + qt * 128:h * W + (qt + 1) * 128],
                        kfmw[:, kvh * 2048 + qt * 128 + o:kvh * 2048 + qt * 128 + o + cur],
                        start=True, stop=True),
                     reads=[q_bufs[h], buf(T + "kfmw")],
                     writes=[pb2], inc=o == 1024)

        def b_chain(i):
            qt, h = heads[i]
            ps2, pb2 = [(PS12, P12), (PS3, P3)][i % 2]
            Pb2 = [Pb2a, Pb2b][i % 2]
            BPB = buf(T + "Pb2" + "ab"[i % 2])
            mxc = 2 * (i % 2)
            mb = bmqs[qt % 2]
            k.op("vector", lambda e, ps2=ps2, mxc=mxc: e.reduce_max(
                    out=smx2[:, mxc:mxc + 1], in_=ps2[:, 0:1152], axis=AX.X,
                    negate=True),
                 reads=[pb2], writes=[buf(T + f"nmx{i % 2}")])
            k.op("vector", lambda e, mxc=mxc: e.tensor_scalar_mul(
                    smx2[:, mxc:mxc + 1], smx2[:, mxc:mxc + 1], ISQ),
                 reads=[buf(T + f"nmx{i % 2}")], writes=[buf(T + f"nmx{i % 2}")])
            k.op("scalar", lambda e, ps2=ps2, Pb2=Pb2, mxc=mxc: e.activation(
                    out=Pb2[:, :], in_=ps2[:, 0:1152], func=AF.Exp, scale=ISQ,
                    bias=smx2[:, mxc:mxc + 1]),
                 reads=[pb2, buf(T + f"nmx{i % 2}"), BPB], writes=[BPB])
            k.op("vector", lambda e, Pb2=Pb2, mxc=mxc, mb=mb: e.scalar_tensor_tensor(
                    out=Pb2[:, :], in0=Pb2[:, :], scalar=1.0, in1=mb[:, :],
                    op0=ALU.mult, op1=ALU.mult,
                    accum_out=smx2[:, mxc + 1:mxc + 2]),
                 reads=[BPB, buf(T + f"bmqB{qt % 2}")], writes=[BPB])
            k.op("vector", lambda e, mxc=mxc: e.reciprocal(
                    smx2[:, mxc + 1:mxc + 2], smx2[:, mxc + 1:mxc + 2]),
                 reads=[BPB], writes=[buf(T + f"rs{i % 2}")])
            k.op("scalar", lambda e, Pb2=Pb2, mxc=mxc: e.activation(
                    out=Pb2[:, :], in_=Pb2[:, :], func=AF.Copy,
                    scale=smx2[:, mxc + 1:mxc + 2]),
                 reads=[BPB, buf(T + f"rs{i % 2}")], writes=[BPB])

        def b_tail(i):
            qt, h = heads[i]
            kvh = h // 4
            ps2, pb2 = [(PS12, P12), (PS3, P3)][i % 2]
            ooff = 1536 if i % 2 == 0 else 1280
            PT2 = [PT2a, PT2b][i % 2]
            Pb2 = [Pb2a, Pb2b][i % 2]
            BPT = buf(T + "PT2" + "ab"[i % 2])
            BPB = buf(T + "Pb2" + "ab"[i % 2])
            for c in range(8):
                k.op("tensor", lambda e, c=c, Pb2=Pb2: e.transpose(
                        PSB[:, c * 128:(c + 1) * 128], Pb2[:, c * 128:(c + 1) * 128],
                        identb[:, :]),
                     reads=[BPB, buf("identb")],
                     writes=[PB] if c in (0, 7) else (), inc=c == 7)
            k.op("vector", lambda e, PT2=PT2: e.tensor_copy(out=PT2[:, 0:1024],
                                                            in_=PSB[:, 0:1024]),
                 reads=[PB], writes=[BPT])
            k.op("tensor", lambda e, Pb2=Pb2: e.transpose(
                    PSB[:, 0:128], Pb2[:, 1024:1152], identb[:, :]),
                 reads=[BPB, buf("identb"), PB], writes=[PB])
            k.op("vector", lambda e, PT2=PT2: e.tensor_copy(out=PT2[:, 1024:1152],
                                                            in_=PSB[:, 0:128]),
                 reads=[PB], writes=[BPT])
            for c in range(9):
                k.op("tensor", lambda e, c=c, qt=qt, kvh=kvh, ps2=ps2, PT2=PT2,
                     ooff=ooff: e.matmul(
                        ps2[:, ooff:ooff + 128],
                        vwin[:, (qt + c) * 512 + kvh * 128:(qt + c) * 512 + (kvh + 1) * 128],
                        PT2[:, c * 128:(c + 1) * 128],
                        start=(c == 0), stop=(c == 8)),
                     reads=[buf(T + "vwin"), BPT],
                     writes=[pb2] if c in (0, 8) else (), inc=c == 8)
            k.op("vector", lambda e, h=h, qt=qt, ps2=ps2, ooff=ooff: e.tensor_copy(
                    out=o2[:, h * W + qt * 128:h * W + (qt + 1) * 128],
                    in_=ps2[:, ooff:ooff + 128]),
                 reads=[pb2], writes=[buf(T + f"o2_{h}")])

        b_qk(0)
        for i in range(NH):
            if i + 1 < NH:
                b_qk(i + 1)
            b_chain(i)
            if i >= 1:
                b_tail(i - 1)
        b_tail(NH - 1)
        o2_bufs = [buf(T + f"o2_{h}") for h in range(16)]
        fence_ba = fence(q_bufs + [buf(T + n) for n in
                         ("kfmw", "vwin", "kvix", "gtstB0", "gtstB1", "PT2a", "PT2b",
                          "Pb2a", "Pb2b", "bmqB0", "bmqB1", "nmx0", "nmx1",
                          "rs0", "rs1")])
        gBA.free()
        gBq.free()

        gXb = Grp("right")
        xn = gXb.sb(T + "xn", [128, DC * W])
        for dc in range(DC):
            k.dma(dc % 4, lambda e, dc=dc: e.dma_start(
                    out=xn[:, dc * W:(dc + 1) * W], in_=xst_dram[:, dc * W:(dc + 1) * W]),
                  reads=[buf("xst"), fence_ba], writes=[buf(T + f"xn_{dc}")])
        xn_bufs = [buf(T + f"xn_{dc}") for dc in range(DC)]
        pendo = {0: stream_weight(T + "wo0", bwo, 0, DC, braw, bbf0,
                                  extra=[fence_ba])}
        for mc in range(DC):
            wbf = [bbf0, bbf1][mc % 2]
            wbufs = pendo.pop(mc)
            if mc + 1 < DC:
                pendo[mc + 1] = stream_weight(T + f"wo{(mc + 1) % 2}", bwo, mc + 1, DC,
                                              braw, [bbf0, bbf1][(mc + 1) % 2])
            ps2, pb2 = [(PS12, P12), (PS3, P3)][mc % 2]
            mm_acc(ps2, pb2, wbf, wbufs, DC,
                   lambda kc, o, cur: o2[:, kc * W + o:kc * W + o + cur], W, o2_bufs)
            k.op("vector", lambda e, mc=mc, ps2=ps2: e.tensor_add(
                    xn[:, mc * W:(mc + 1) * W], xn[:, mc * W:(mc + 1) * W], ps2[:, 0:W]),
                 reads=[pb2, buf(T + f"xn_{mc}")], writes=[buf(T + f"xn_{mc}")])
        fence_bo = fence(o2_bufs + [buf(n) for n in list(B) if n.startswith(("raw_", "bf_"))])
        gBo.free()
        return xn, xn_bufs, fence_bo, gXb

    XT, XB, XF, XG = x, x_bufs, fence_a5, g_a5
    if "ffn1" in STAGES:
        _g = XG
        XT, XB, XF, XG = emit_ffn("f1_", XT, XB, XF, lambda g=_g: g.free())
    if "attn_b" in STAGES:
        _g = XG
        XT, XB, XF, XG = emit_attn_b("b_", XT, XB, XF, lambda g=_g: g.free())
    if "ffn2" in STAGES:
        _g = XG
        XT, XB, XF, XG = emit_ffn("f2_", XT, XB, XF, lambda g=_g: g.free())

    g_out = Grp()
    outb = g_out.sb("outb", [128, D])
    for tc in range(8):
        for dc in range(DC):
            ps = PS3 if dc < 8 else PS12
            psb = P3 if dc < 8 else P12
            col = (dc % 8) * 128
            last = dc in (7, 15)
            first = dc in (0, 8)
            k.op("tensor", lambda e, tc=tc, dc=dc, ps=ps, col=col: e.transpose(
                    ps[:, col:col + 128],
                    XT[:, dc * W + tc * 128:dc * W + (tc + 1) * 128], identf[:, :]),
                 reads=XB + [buf("identf")],
                 writes=[psb] if (last or first) else (), inc=last)
        k.op("vector", lambda e: e.tensor_copy(out=outb[:, 0:1024], in_=PS3[:, 0:1024]),
             reads=[P3], writes=[buf("outb")])
        k.op("vector", lambda e: e.tensor_copy(out=outb[:, 1024:2048], in_=PS12[:, 0:1024]),
             reads=[P12], writes=[buf("outb")])
        k.dma(6, lambda e, tc=tc: e.dma_start(
                out=out_t[tc * 128:(tc + 1) * 128, :], in_=outb[:, :]),
              reads=[buf("outb")], writes=[buf("out")])

    k.finish("sync")
    k.finish("gpsimd")

    assert peak[0] < 214 * 1024, f"SBUF peak {peak[0]} bytes/partition"

    g_out.free(); XG.free(); gFper.free(); g_persist.free()
    with nc.Block() as block:
        k.build(block)
    return nc


def pack_w(Wm, KC, MC):
    Kd, Md = Wm.shape
    assert Kd == KC * 128 and Md == MC * 128
    return np.ascontiguousarray(
        Wm.reshape(KC, 128, MC, 128).transpose(2, 1, 0, 3).reshape(MC, 128, KC * 128))


def vec_t(v, MC):
    return np.ascontiguousarray(v.reshape(MC, 128).T)


_prog_cache = {}


def build_inmaps(ii):
    x_in = ii["L0_attn_a_x"].astype(np.float32)

    shared = {
        "wl": pack_w(ii["L0_attn_a_Wl"], DC, HC),
        "wr": pack_w(ii["L0_attn_a_Wr"], DC, HC),
        "wa": pack_w(ii["L0_attn_a_Wa"], HC, HC),
        "wi": pack_w(ii["L0_attn_a_Wi"], HC, HC),
        "wo": pack_w(ii["L0_attn_a_Wo"], HC, DC),
        "bl": vec_t(ii["L0_attn_a_bl"], HC),
        "br": vec_t(ii["L0_attn_a_br"], HC),
        "ba": vec_t(ii["L0_attn_a_ba"], HC),
        "bi": vec_t(ii["L0_attn_a_bi"], HC),
        "bo": vec_t(ii["L0_attn_a_bo"], DC),
        "cb": vec_t(ii["L0_attn_a_conv_b"], HC),
        "lam": vec_t(ii["L0_attn_a_lam"], HC),
        "idf": np.eye(128, dtype=np.float32),
        "idb": np.eye(128, dtype=np.float32).astype(ml_dtypes.bfloat16),
    }
    cw = ii["L0_attn_a_conv_w"][:, 0, :]
    shared["cw"] = np.ascontiguousarray(
        cw.reshape(4, HC, 128).transpose(2, 1, 0).reshape(128, HC * 4))
    shared["wrt"] = vec_t(ii["L0_ffn_w_router"], DC)
    shared["g1"] = vec_t(ii["L0_ffn_g1"], DC)
    shared["g2"] = vec_t(ii["L0_ffn_g2"], DC)
    blk = np.zeros((128, 4), np.float32)
    for p in range(128):
        blk[p, p // 32] = 1.0
    shared["blkid"] = blk
    shared["fmask"] = np.ascontiguousarray(ii["L0_ffn_mask"].astype(np.float32))
    fr = ii["L0_ffn_freqs_cis"].astype(np.float32)
    shared["fcos"] = np.ascontiguousarray(fr[:, :, 0].T)
    shared["fsin"] = np.ascontiguousarray(fr[:, :, 1].T)
    shared["bwq"] = pack_w(ii["L0_attn_b_Wq"], DC, DC)
    shared["bwk"] = pack_w(ii["L0_attn_b_Wk"], DC, 4)
    shared["bwv"] = np.ascontiguousarray(
        ii["L0_attn_b_Wv"].astype(np.float32)
        .reshape(8, 2, 128, 512).transpose(0, 2, 1, 3).reshape(8, 128, 1024))
    shared["bwo"] = pack_w(ii["L0_attn_b_Wo"], DC, DC)
    bmasks = []
    kvidxs = []
    r_ = np.arange(128)
    j_ = np.arange(1152)
    for q in range(4):
        mm_ = np.full((1024, 1152), -1.0e9, np.float32)
        for i in range(8):
            base = 1024 * q + 128 * i
            ktg = (1024 * (q - 1) + 128 * i) + j_[None, :]
            valid = (j_[None, :] >= r_[:, None]) & (j_[None, :] <= r_[:, None] + 1024) \
                    & (ktg >= 0)
            blkm = np.where(valid, 1.0, 0.0).astype(np.float32)
            mm_[i * 128:(i + 1) * 128] = blkm
        bmasks.append(mm_.astype(ml_dtypes.bfloat16))
        kv = np.empty((128, 16), np.int32)
        for wc in range(16):
            gi = 1024 * (q - 1) + 128 * wc + r_
            kv[:, wc] = np.where(gi >= 0, gi, 8192).astype(np.int32)
        kvidxs.append(kv)

    fslices = []
    for q in range(4):
        fs = {
            "fwq": pack_w(np.ascontiguousarray(ii["L0_ffn_wq"][:, 512 * q:512 * (q + 1)]),
                          DC, 4),
            "fwk": pack_w(np.ascontiguousarray(ii["L0_ffn_wk"][:, 512 * q:512 * (q + 1)]),
                          DC, 4),
            "fwv": np.ascontiguousarray(
                ii["L0_ffn_wv"][:, 512 * q:512 * (q + 1)].astype(np.float32)
                .reshape(4, 4, 128, 2, 256).transpose(3, 0, 2, 1, 4)
                .reshape(2, 4, 128, 1024)),
            "fwo": pack_w(np.ascontiguousarray(ii["L0_ffn_wo"][512 * q:512 * (q + 1), :]),
                          4, DC),
            "fw1": pack_w(np.ascontiguousarray(ii["L0_ffn_w1"][:, 2048 * q:2048 * (q + 1)]),
                          DC, 16),
            "fw3": pack_w(np.ascontiguousarray(ii["L0_ffn_w3"][:, 2048 * q:2048 * (q + 1)]),
                          DC, 16),
            "fw2": np.ascontiguousarray(
                ii["L0_ffn_w2"][2048 * q:2048 * (q + 1), :].astype(np.float32)
                .reshape(8, 2, 128, 4, 512).transpose(3, 0, 2, 1, 4)
                .reshape(4, 8, 128, 1024)),
        }
        fslices.append(fs)

    xT = [np.ascontiguousarray(x_in[b].T) for b in range(2)]

    in_maps = []
    for c in range(NCORES):
        b, q = c // 4, c % 4
        m = dict(shared)
        x0 = np.zeros((D, WH), np.float32)
        lo = 1024 * q - 3
        if lo < 0:
            x0[:, 3:] = xT[b][:, 0:1024 * (q + 1)]
        else:
            x0[:, :] = xT[b][:, lo:1024 * (q + 1)]
        m["x0"] = x0
        m["hv"] = np.array([[0.0 if q == 0 else 1.0]], np.float32)
        qs = np.zeros((1, 4), np.float32)
        qs[0, :q] = 1.0
        m["qsel"] = qs
        m["qsel4"] = np.ascontiguousarray(qs.reshape(4, 1))
        m.update(fslices[q])
        m["bmask"] = bmasks[q]
        m["kvidx"] = kvidxs[q]
        in_maps.append(m)
    return in_maps


def kernel(**inputs):
    ii = {kk: np.asarray(v) for kk, v in inputs.items()}
    in_maps = build_inmaps(ii)

    if "prog" not in _prog_cache:
        _prog_cache["prog"] = build_program()
    nc = _prog_cache["prog"]
    res = run_bass_kernel_spmd(nc, in_maps, core_ids=list(range(NCORES)))

    y = np.empty((2, 4096, D), np.float32)
    for c in range(NCORES):
        b, q = c // 4, c % 4
        y[b, 1024 * q:1024 * (q + 1), :] = res.results[c]["out"]
    return y
